# version 04
# baseline (speedup 1.0000x reference)
import numpy as np
from contextlib import ExitStack

import concourse.bass as bass
import concourse.mybir as mybir
from concourse.bass_utils import run_bass_kernel_spmd

F32 = mybir.dt.float32
BF16 = mybir.dt.bfloat16
I32 = mybir.dt.int32
AF = mybir.ActivationFunctionType
ALU = mybir.AluOpType
AX = mybir.AxisListType.X

NCORE = 8
T = 8192
TPC = T // NCORE
NT = TPC // 128
D = 2048
KT = D // 128
H = 8
HK = 128
HV = 256
KEY = 1024
CH = 64
NCH = TPC // CH
HIDA = 5504
MA = HIDA // 128
HIDQ = 8192
LOW = 16
C1 = float(HK) ** -0.5
IGN = 1.0 / 16.0
ENGS = ("tensor", "vector", "scalar", "sync", "gpsimd")

SL_R, SL_MR, SL_RR2, SL_MRR2, SL_SQ, SL_RCPQ, SL_XWG, SL_XWU = range(8)
SL_TSWS, SL_RCPT, SL_MXT, SL_R2B, SL_T0, SL_T1, SL_T2, SL_T3 = range(8, 16)


class Prog:
    def __init__(self, nc, sems):
        self.nc = nc
        self.sem = sems
        self.q = {e: [] for e in ENGS}
        self.cnt = {s: 0 for s in sems}
        self.lastw = {e: {} for e in ENGS}

    def emit(self, eng, fn, inc=None, waits=(), inc_n=1):
        wl = []
        for w in waits:
            if w is None:
                continue
            s, v = w
            if v <= 0 or self.lastw[eng].get(s, 0) >= v:
                continue
            self.lastw[eng][s] = v
            wl.append((s, v))
        if inc is not None:
            self.cnt[inc] += inc_n

        def run(h, _fn=fn, _wl=tuple(wl), _inc=inc, _n=inc_n):
            for s, v in _wl:
                h.wait_ge(self.sem[s], v)
            ins = _fn(h)
            if _inc is not None:
                ins.then_inc(self.sem[_inc], _n)

        self.q[eng].append(run)
        return (inc, self.cnt[inc]) if inc is not None else None

    def dma(self, engq, out, in_, waits=(), slow=False, sem=None, gate=None):
        s = sem or {"sync": "ds", "gpsimd": "dg"}[engq]
        if gate is None:
            gate = (engq == "gpsimd")
        w = list(waits)
        if gate:
            w.append((s, self.cnt[s]))
        return self.emit(engq, lambda h, o=out, i=in_, sl=slow: h.dma_start(
            out=o, in_=i, allow_slow_non_contiguous=sl),
            inc=s, waits=w, inc_n=16)

    def barrier(self):
        evs = [(s, self.cnt[s]) for s in self.cnt if self.cnt[s] > 0]
        for e in ENGS:
            for s, v in evs:
                if self.lastw[e].get(s, 0) < v:
                    self.lastw[e][s] = v
                    self.q[e].append(lambda h, _s=s, _v=v: h.wait_ge(self.sem[_s], _v))

    def finalize(self, block):
        for e in ENGS:
            def mk(ops_):
                def body(h):
                    for f in ops_:
                        f(h)
                return body
            getattr(block, e)(mk(list(self.q[e])))


def build_nc(debug_taps=()):
    nc = bass.Bass("TRN2", target_bir_lowering=False)

    dx = nc.dram_tensor("x", [TPC, D], F32, kind="ExternalInput").ap()
    dwq = nc.dram_tensor("wq", [D, KEY], F32, kind="ExternalInput").ap()
    dwk = nc.dram_tensor("wk", [D, KEY], F32, kind="ExternalInput").ap()
    dwv = nc.dram_tensor("wv", [D, D], F32, kind="ExternalInput").ap()
    dgw1 = nc.dram_tensor("gk_w1", [D, LOW], F32, kind="ExternalInput").ap()
    dgw2 = nc.dram_tensor("gk_w2", [LOW, KEY], F32, kind="ExternalInput").ap()
    dgkb = nc.dram_tensor("gk_b", [KEY], F32, kind="ExternalInput").ap()
    dwg = nc.dram_tensor("wg", [D, D], F32, kind="ExternalInput").ap()
    dwo = nc.dram_tensor("wo", [D, D], F32, kind="ExternalInput").ap()
    dwga = nc.dram_tensor("w_gate_a", [D, HIDA], F32, kind="ExternalInput").ap()
    dwua = nc.dram_tensor("w_up_a", [D, HIDA], F32, kind="ExternalInput").ap()
    dwda = nc.dram_tensor("w_down_a", [HIDA, D], F32, kind="ExternalInput").ap()
    dwgq = nc.dram_tensor("wgT", [D, HIDQ], F32, kind="ExternalInput").ap()
    dwuq = nc.dram_tensor("wuT", [D, HIDQ], F32, kind="ExternalInput").ap()
    dwdq = nc.dram_tensor("wdT", [HIDQ, D], F32, kind="ExternalInput").ap()
    dwsl = nc.dram_tensor("w_scale_slice", [12, 128, 4096], F32, kind="ExternalInput").ap()
    dn1 = nc.dram_tensor("n1w_row", [1, D], F32, kind="ExternalInput").ap()
    dffw = nc.dram_tensor("ffnw_row", [1, D], F32, kind="ExternalInput").ap()
    dgnw = nc.dram_tensor("gnw_row", [1, D], F32, kind="ExternalInput").ap()
    dident = nc.dram_tensor("ident_in", [128, 128], F32, kind="ExternalInput").ap()
    dmaskt = nc.dram_tensor("maskT_in", [64, 64], F32, kind="ExternalInput").ap()
    dgsel = nc.dram_tensor("gsel_row", [1, 16], F32, kind="ExternalInput").ap()
    dseg = nc.dram_tensor("seg_row", [1, TPC], F32, kind="ExternalInput").ap()

    dout = nc.dram_tensor("out", [TPC, D], F32, kind="ExternalOutput").ap()

    d_t = nc.dram_tensor("t_buf", [TPC, HIDQ], F32)
    d_y = nc.dram_tensor("y_buf", [TPC, D], F32)
    d_sc = nc.dram_tensor("sc_bounce", [1, 8], F32)
    ag_sin = nc.dram_tensor("ag_sin", [1, 4], F32)
    ag_sout = nc.dram_tensor("ag_sout", [1, 4], F32)
    ag_in = nc.dram_tensor("ag_in", [128, 2056], F32)
    ag_out = nc.dram_tensor("ag_out", [128 * NCORE, 2056], F32)

    taps = {}
    tap_shapes = {"xn": [TPC, D], "qffn": [TPC, D], "h": [TPC, D],
                  "sp": [KEY, TPC], "qe": [KEY, TPC], "ke": [KEY, TPC],
                  "o_loc": [TPC, D], "S": [HK, H * HV], "x1": [TPC, D],
                  "ofin": [TPC, D], "tq": [TPC, HIDQ], "ocor": [TPC, D],
                  "cbof": [128, KT * TPC], "cbdump": [128, D],
                  "rsqdump": [128, 64], "sqdump": [128, D],
                  "statdump": [128, 144], "thrdump": [128, 8],
                  "xqdump": [128, KT * TPC]}
    for name in debug_taps:
        taps[name] = nc.dram_tensor("tap_" + name, tap_shapes[name], F32,
                                    kind="ExternalOutput").ap()

    ctx = ExitStack()
    E = ctx.enter_context
    sems = {s: E(nc.semaphore(s)) for s in ("T", "V", "A", "G", "ds", "dg", "dt", "dw", "cc", "q0", "q1", "q2", "q3")}
    p = Prog(nc, sems)

    ident = E(nc.sbuf_tensor("ident", [128, 128], BF16))
    maskt = E(nc.sbuf_tensor("maskt", [64, 64], BF16))
    STAT = E(nc.sbuf_tensor("STAT", [128, 144], F32))
    THR = E(nc.sbuf_tensor("THR", [128, 8], F32))
    CK = E(nc.sbuf_tensor("CK", [128, 8], F32))
    DEC = E(nc.sbuf_tensor("DEC", [128, H * NCH], F32))
    PSH = E(nc.sbuf_tensor("PSH", [128, H * NCH], F32))
    PIN = E(nc.sbuf_tensor("PIN", [128, H * NCH], F32))
    GKB = E(nc.sbuf_tensor("GKB", [128, H], F32))
    NGKB = E(nc.sbuf_tensor("NGKB", [128, H], F32))
    GSEL = E(nc.sbuf_tensor("GSEL", [128, 16], F32))
    SEGM = E(nc.sbuf_tensor("SEGM", [128, TPC], BF16))
    ONES16 = E(nc.sbuf_tensor("ONES16", [128, 16], F32))
    PE8 = E(nc.sbuf_tensor("PE8", [128, 8], F32))
    PJ8 = E(nc.sbuf_tensor("PJ8", [128, 8], F32))
    PJM = E(nc.sbuf_tensor("PJM", [128, 8], F32))
    RSQ = E(nc.sbuf_tensor("RSQ", [128, 64], F32))
    SC1 = E(nc.sbuf_tensor("SC1", [1, 8], F32))
    SAM = E(nc.sbuf_tensor("SAM", [128, 64], BF16))
    SKDT = E(nc.sbuf_tensor("SKDT", [128, 64], BF16))
    SKD = E(nc.sbuf_tensor("SKD", [128, 128], BF16))
    Ub = E(nc.sbuf_tensor("Ub", [128, 512], BF16))
    Vb = E(nc.sbuf_tensor("Vb", [128, 512], BF16))

    M0 = 24544
    A = nc.alloc_sbuf_tensor_at
    Bb = A("Bb", [128, KT * TPC], BF16, offset=M0)
    Cb = A("Cb", [128, KT * TPC], BF16, offset=M0 + 32768)
    A8 = A("A8", [128, NT * D], F32, offset=M0 + 65536)
    Db = A("Db", [128, NT * D], BF16, offset=M0 + 65536)
    Z1W = A("Z1W", [16, 2048], BF16, offset=M0 + 65536 + 4096)
    XQB = A("XQB", [128, D], BF16, offset=M0 + 65536 + 8192)
    OSC = A("OSC", [128, 256], BF16, offset=M0 + 65536 + 12288)
    Eb = A("Eb", [128, H * TPC], BF16, offset=M0 + 98304)
    GTS = A("GTS", [128, NT * 512], F32, offset=M0 + 98304)
    EK = A("EK", [128, 1024], F32, offset=M0 + 114688)
    S_bf = A("S_bf", [128, H * HV], BF16, offset=M0 + 118784)
    S = A("S", [128, H * HV], F32, offset=M0 + 122880)
    RAW = A("RAW", [128, 2048], F32, offset=M0 + 131072)
    Fb = A("Fb", [128, H * TPC], BF16, offset=M0 + 131072)
    Gb = A("Gb", [128, H * TPC], BF16, offset=M0 + 147456)
    WS = A("WS", [128, KT * 512], BF16, offset=M0 + 163840)
    X2 = A("X2", [128, D], F32, offset=M0 + 180224)
    I32S = A("I32S", [128, D], I32, offset=M0 + 188416)
    CB = A("CB", [128, D], F32, offset=M0 + 188416)
    OUTS = A("OUTS", [128, D], F32, offset=M0 + 188416)
    Hs = A("Hs", [128, 2048], F32, offset=M0 + 196608)

    def st(i, sl):
        return STAT[:, i * 16 + sl: i * 16 + sl + 1]

    def BbT(kt, i):
        return Bb[:, kt * TPC + i * 128: kt * TPC + (i + 1) * 128]

    def CbT(kt, i):
        return Cb[:, kt * TPC + i * 128: kt * TPC + (i + 1) * 128]

    out_evs = []

    def dump_tiles(tap, srcfn, n=NT, evs=()):
        for i in range(n):
            eve = srcfn(i)
            out_evs.append(p.dma("gpsimd", taps[tap][i * 128:(i + 1) * 128, :],
                                 Hs[:, :], waits=(eve,) + tuple(evs)))
            p.barrier()

    p.dma("gpsimd", ident[:, :], dident)
    p.dma("gpsimd", maskt[:, :], dmaskt)
    ev_gsel = p.dma("gpsimd", GSEL[:, :], dgsel.broadcast_to([128, 16]))
    ev_seg = p.dma("gpsimd", SEGM[:, :], dseg.broadcast_to([128, TPC]))
    ev_gkb = p.dma("gpsimd", GKB[:, :], dgkb.rearrange("(h q) -> q h", h=H),
                   slow=True)
    ev_ck = p.emit("vector", lambda h: h.memset(CK[:, 0:3], 0.0), inc="V")
    ev_ck = p.emit("vector", lambda h: h.memset(CK[:, 0:1], 1.0), inc="V")
    ev_ckm = p.emit("vector", lambda h: h.memset(CK[:, 1:2], -1.0), inc="V")
    ev_o16 = p.emit("vector", lambda h: h.memset(ONES16[:, :], 1.0), inc="V")
    ev_ngkb = p.emit("vector", lambda h: h.tensor_scalar_mul(
        out=NGKB[:, :], in0=GKB[:, :], scalar1=-1.0), inc="V", waits=(ev_gkb,))

    ev_pacc0 = p.emit("vector", lambda h: h.memset(STAT[:, 128:133], 0.0), inc="V")
    last_red = ev_pacc0
    last_raw_user = [None, None]
    for sidx in range(12):
        wcol = sidx // 4
        for qc in range(4):
            slot = (sidx * 4 + qc) % 2
            evd = p.dma("sync", RAW[:, slot * 1024: slot * 1024 + 1024],
                        dwsl[sidx, :, qc * 1024:(qc + 1) * 1024],
                        waits=(last_raw_user[slot],), sem=f"q{slot}")
            evr = p.emit("vector", lambda h, sl=slot: h.reduce_sum(
                out=STAT[:, 132:133], in_=RAW[:, sl * 1024: sl * 1024 + 1024],
                axis=AX, apply_absolute_value=True), inc="V",
                waits=(evd, last_red))
            last_raw_user[slot] = evr
            last_red = p.emit("vector", lambda h, w=wcol: h.tensor_add(
                out=STAT[:, 128 + w:129 + w], in0=STAT[:, 128 + w:129 + w],
                in1=STAT[:, 132:133]), inc="V", waits=(evr,))
    with nc.psum_tensor("PS1", [1, 4], F32) as PS1:
        ev_mm1 = p.emit("tensor", lambda h: h.matmul(
            PS1[:, :], CK[:, 0:1], STAT[:, 128:132], start=True, stop=True),
            inc="T", waits=(last_red, ev_ck))
        ev_sc1 = p.emit("vector", lambda h: h.tensor_copy(
            out=SC1[0:1, 0:4], in_=PS1[:, :]), inc="V", waits=(ev_mm1,))
    ev_scd = p.dma("gpsimd", ag_sin.ap(), SC1[0:1, 0:4], waits=(ev_sc1,))
    ev_ar = p.emit("gpsimd", lambda h: h.collective_compute(
        "AllReduce", ALU.add, replica_groups=[list(range(NCORE))],
        ins=[ag_sin.ap().opt()], outs=[ag_sout.ap().opt()]),
        inc="cc", waits=(ev_scd,))
    ev_scl = p.dma("gpsimd", SC1[0:1, 0:4], ag_sout.ap(), waits=(ev_ar,))
    ev_ws = p.emit("vector", lambda h: h.tensor_scalar(
        out=SC1[0:1, 4:7], in0=SC1[0:1, 0:3], scalar1=1.0 / (D * HIDQ),
        scalar2=1e-5, op0=ALU.mult, op1=ALU.add), inc="V", waits=(ev_scl,))
    ev_inv = p.emit("vector", lambda h: h.reciprocal(
        out=SC1[0:1, 0:3], in_=SC1[0:1, 4:7]), inc="V", waits=(ev_ws,))
    ev_scb = p.dma("gpsimd", d_sc.ap(), SC1[0:1, 0:8], waits=(ev_inv,))
    ev_thr = p.dma("gpsimd", THR[:, :], d_sc.ap().broadcast_to([128, 8]),
                   waits=(ev_scb,))

    ev_xq_done = []
    ev_xread = {}
    with nc.psum_tensor("PTR1", [128, 512], BF16) as PTR1v:
        for i in range(NT):
            xa = X2[:, :]
            evx = p.dma("gpsimd", xa, dx[i * 128:(i + 1) * 128, :],
                        waits=(ev_xread.get(i - 1),))
            ev_s1 = p.emit("vector", lambda h, xa=xa, i=i: h.reduce_sum(
                out=st(i, 8), in_=xa, axis=AX), inc="V", waits=(evx,))
            ev_sq = p.emit("scalar", lambda h, xa=xa, i=i: h.activation(
                out=A8[:, 0:D], in_=xa, func=AF.Square,
                accum_out=st(i, 10)), inc="A", waits=(evx,))
            e1 = p.emit("vector", lambda h, i=i: h.tensor_scalar_mul(
                out=st(i, 9), in0=st(i, 8), scalar1=1.0 / D), inc="V",
                waits=(ev_s1,))
            e2 = p.emit("vector", lambda h, i=i: h.tensor_scalar_mul(
                out=st(i, 10), in0=st(i, 10), scalar1=1.0 / D), inc="V",
                waits=(ev_sq,))
            e3 = p.emit("vector", lambda h, i=i: h.tensor_mul(
                out=st(i, 11), in0=st(i, 9), in1=st(i, 9)), inc="V", waits=(e1,))
            e4 = p.emit("vector", lambda h, i=i: h.tensor_sub(
                out=st(i, 11), in0=st(i, 10), in1=st(i, 11)), inc="V",
                waits=(e2, e3))
            e5 = p.emit("vector", lambda h, i=i: h.tensor_scalar_add(
                out=st(i, 11), in0=st(i, 11), scalar1=1e-5), inc="V", waits=(e4,))
            e6 = p.emit("vector", lambda h, i=i: h.reciprocal(
                out=st(i, 12), in_=st(i, 11)), inc="V", waits=(e5,))
            ev_r = p.emit("scalar", lambda h, i=i: h.activation(
                out=st(i, SL_R), in_=st(i, 12), func=AF.Sqrt), inc="A",
                waits=(e6,))
            ev_mr = p.emit("vector", lambda h, i=i: h.tensor_mul(
                out=st(i, SL_MR), in0=st(i, 9), in1=st(i, SL_R)), inc="V",
                waits=(ev_r,))
            e7 = p.emit("vector", lambda h, i=i: h.tensor_mul(
                out=st(i, 13), in0=st(i, SL_R), in1=st(i, SL_R)), inc="V",
                waits=(ev_r,))
            e8 = p.emit("vector", lambda h, i=i: h.tensor_scalar(
                out=st(i, 13), in0=st(i, 13), scalar1=-1e-5,
                scalar2=1.0 + 1e-6, op0=ALU.mult, op1=ALU.add), inc="V",
                waits=(e7,))
            e9 = p.emit("vector", lambda h, i=i: h.reciprocal(
                out=st(i, 14), in_=st(i, 13)), inc="V", waits=(e8,))
            ev_r2 = p.emit("scalar", lambda h, i=i: h.activation(
                out=st(i, 15), in_=st(i, 14), func=AF.Sqrt), inc="A", waits=(e9,))
            p.emit("vector", lambda h, i=i: h.tensor_mul(
                out=st(i, SL_RR2), in0=st(i, SL_R), in1=st(i, 15)), inc="V",
                waits=(ev_r2,))
            p.emit("vector", lambda h, i=i: h.tensor_mul(
                out=st(i, SL_MRR2), in0=st(i, SL_MR), in1=st(i, 15)), inc="V",
                waits=(ev_r2, ev_mr))
            ev_xn = p.emit("vector", lambda h, xa=xa, i=i: h.tensor_scalar(
                out=Hs[:, 0:D], in0=xa, scalar1=st(i, SL_R),
                scalar2=st(i, SL_MR), op0=ALU.mult, op1=ALU.subtract), inc="V",
                waits=(ev_mr,))
            ev_xread[i] = ev_xn
            ev_mx = p.emit("vector", lambda h, i=i: h.reduce_max(
                out=st(i, 12), in_=Hs[:, 0:D], axis=AX,
                apply_absolute_value=True), inc="V", waits=(ev_xn,))
            ev_sqv = p.emit("vector", lambda h, i=i: h.tensor_scalar(
                out=st(i, SL_SQ), in0=st(i, 12), scalar1=1.0 / 127.0,
                scalar2=1e-5, op0=ALU.mult, op1=ALU.add), inc="V", waits=(ev_mx,))
            ev_rq = p.emit("vector", lambda h, i=i: h.reciprocal(
                out=st(i, SL_RCPQ), in_=st(i, SL_SQ)), inc="V", waits=(ev_sqv,))
            ev_i32 = p.emit("vector", lambda h, i=i: h.tensor_scalar_mul(
                out=I32S[:, :], in0=Hs[:, 0:D],
                scalar1=st(i, SL_RCPQ)), inc="V", waits=(ev_rq,))
            ev_xqb = p.emit("vector", lambda h: h.tensor_copy(
                out=XQB[:, :], in_=I32S[:, :]), inc="V",
                waits=(ev_i32, ev_xq_done[-1] if ev_xq_done else None))
            for g in range(4):
                evtr = None
                for j in range(4):
                    kt = g * 4 + j
                    evtr = p.emit("tensor", lambda h, kt=kt, j=j: h.transpose(
                        PTR1v[:, j * 128:(j + 1) * 128],
                        XQB[:, kt * 128:(kt + 1) * 128], ident[:, :]),
                        inc="T", waits=(ev_xqb, ev_sc1,
                                        ev_xq_done[-1] if ev_xq_done else None))
                dst = Bb[:].rearrange("q (kt t) -> q kt t", kt=KT)[
                    :, g * 4:(g + 1) * 4, i * 128:(i + 1) * 128]
                eve = p.emit("scalar", lambda h, dst=dst: h.copy(
                    out=dst, in_=PTR1v[:].rearrange("q (a b) -> q a b", a=4)),
                    inc="A", waits=(evtr,))
                ev_xq_done.append(eve)
    ev_xqT = ev_xq_done[-1]

    if "xqdump" in taps:
        out_evs.append(p.dma("gpsimd", taps["xqdump"], Bb[:, :],
                             waits=(ev_xqT,)))
        p.barrier()
    if "xn" in taps:
        def mk_xn(i):
            xa = X2[:, :]
            evx = p.dma("gpsimd", xa, dx[i * 128:(i + 1) * 128, :],
                        waits=(ev_xqT,))
            return p.emit("vector", lambda h, xa=xa, i=i: h.tensor_scalar(
                out=Hs[:, :], in0=xa, scalar1=st(i, SL_R), scalar2=st(i, SL_MR),
                op0=ALU.mult, op1=ALU.subtract), inc="V", waits=(evx,))
        dump_tiles("xn", mk_xn)

    p.barrier()

    for i in range(NT):
        p.emit("vector", lambda h, i=i: h.tensor_mul(
            out=st(i, SL_XWG), in0=st(i, SL_SQ), in1=THR[:, 4:5]), inc="V",
            waits=(ev_thr,))
        p.emit("vector", lambda h, i=i: h.tensor_mul(
            out=st(i, SL_XWU), in0=st(i, SL_SQ), in1=THR[:, 5:6]), inc="V",
            waits=(ev_thr,))
        p.emit("vector", lambda h, i=i: h.memset(st(i, SL_MXT), 0.0), inc="V")
    ev_xw = ("V", p.cnt["V"])

    with ExitStack() as pctx:
        PW2v = [pctx.enter_context(nc.psum_tensor(f"PW2{i}", [128, 512], F32))
              for i in range(8)]
        ev_mm = [None] * 8
        ev_free = [None] * 8
        ev_gate = [None] * 8
        ev_gb_users = None
        ev_tdma_prev = None
        raw_guard = {}
        for w in range(16):
            for wt in range(2):
                dsrc = dwgq if wt == 0 else dwuq
                thr_inv = THR[:, 0:1] if wt == 0 else THR[:, 1:2]
                ev_q = []
                for kt in range(KT):
                    slot = kt % 4
                    evd = p.dma("sync", RAW[:, slot * 512:(slot + 1) * 512],
                                dsrc[kt * 128:(kt + 1) * 128,
                                     w * 512:(w + 1) * 512],
                                waits=(raw_guard.get(slot),), sem=f"q{slot}")
                    evu = p.emit("vector", lambda h, sl=slot, ti=thr_inv:
                                 h.tensor_scalar(
                        out=Ub[:, :],
                        in0=RAW[:, sl * 512:(sl + 1) * 512], scalar1=ti,
                        scalar2=0.5, op0=ALU.mult, op1=ALU.is_ge), inc="V",
                        waits=(evd, ev_thr))
                    evv = p.emit("vector", lambda h, sl=slot, ti=thr_inv:
                                 h.tensor_scalar(
                        out=Vb[:, :],
                        in0=RAW[:, sl * 512:(sl + 1) * 512], scalar1=ti,
                        scalar2=-0.5, op0=ALU.mult, op1=ALU.is_le), inc="V",
                        waits=(evd,))
                    raw_guard[slot] = evv
                    evw = p.emit("vector", lambda h, kt=kt: h.tensor_sub(
                        out=Gb[:, kt * 512:(kt + 1) * 512],
                        in0=Ub[:, :], in1=Vb[:, :]), inc="V",
                        waits=(evu, evv, ev_gb_users if kt == 0 else None))
                    ev_q.append((evw, evv))
                for i in range(NT):
                    evm = None
                    for kt in range(KT):
                        evm = p.emit("tensor", lambda h, i=i, kt=kt: h.matmul(
                            PW2v[i][:, :], BbT(kt, i),
                            Gb[:, kt * 512:(kt + 1) * 512],
                            start=(kt == 0), stop=(kt == KT - 1)),
                            inc="T" if kt == KT - 1 else None,
                            waits=(ev_q[kt][0], ev_xqT,
                                   ev_free[i] if kt == 0 else None))
                    ev_mm[i] = evm
                ev_gb_users = ev_mm[7]
                for i in range(NT):
                    if wt == 0:
                        ev_gate[i] = p.emit("scalar", lambda h, i=i: h.activation(
                            out=GTS[:, i * 512:(i + 1) * 512], in_=PW2v[i][:, :],
                            func=AF.Silu, scale=st(i, SL_XWG), bias=CK[:, 2:3]),
                            inc="A", waits=(ev_mm[i], ev_xw, ev_ck))
                        ev_free[i] = ev_gate[i]
                    else:
                        evt = p.emit("vector", lambda h, i=i: h.scalar_tensor_tensor(
                            out=Hs[:, 512:1024], in0=PW2v[i][:, :],
                            scalar=st(i, SL_XWU),
                            in1=GTS[:, i * 512:(i + 1) * 512],
                            op0=ALU.mult, op1=ALU.mult), inc="V",
                            waits=(ev_mm[i], ev_gate[i], ev_xw,
                                   ev_tdma_prev))
                        evr = p.emit("vector", lambda h, i=i: h.reduce_max(
                            out=st(i, SL_T1), in_=Hs[:, 512:1024], axis=AX,
                            apply_absolute_value=True), inc="V", waits=(evt,))
                        p.emit("vector", lambda h, i=i: h.tensor_max(
                            out=st(i, SL_MXT), in0=st(i, SL_MXT),
                            in1=st(i, SL_T1)), inc="V", waits=(evr,))
                        evdma = p.dma("gpsimd",
                                      d_t.ap()[i * 128:(i + 1) * 128,
                                               w * 512:(w + 1) * 512],
                                      Hs[:, 512:1024], waits=(evt,), sem="dt")
                        ev_tdma_prev = evdma
                        ev_free[i] = evt
    if "tq" in taps:
        out_evs.append(p.dma("gpsimd", taps["tq"], d_t.ap(),
                             waits=(("dg", p.cnt["dg"]),)))
        p.barrier()
    p.barrier()

    for i in range(NT):
        e1 = p.emit("vector", lambda h, i=i: h.tensor_scalar(
            out=st(i, SL_T0), in0=st(i, SL_MXT), scalar1=1.0 / 127.0,
            scalar2=1e-5, op0=ALU.mult, op1=ALU.add), inc="V")
        p.emit("vector", lambda h, i=i: h.reciprocal(
            out=st(i, SL_RCPT), in_=st(i, SL_T0)), inc="V", waits=(e1,))
        p.emit("vector", lambda h, i=i: h.tensor_mul(
            out=st(i, SL_TSWS), in0=st(i, SL_T0), in1=THR[:, 6:7]), inc="V",
            waits=(e1, ev_thr))
    ev_t3 = ("V", p.cnt["V"])
    if "thrdump" in taps:
        out_evs.append(p.dma("gpsimd", taps["thrdump"], THR[:, :],
                             waits=(ev_thr,)))
        p.barrier()
    if "statdump" in taps:
        out_evs.append(p.dma("gpsimd", taps["statdump"], STAT[:, :],
                             waits=(ev_t3,)))
        p.barrier()

    with ExitStack() as pctx:
        PTR3v = pctx.enter_context(nc.psum_tensor("PTR3", [128, 512], BF16))
        PW3v = [pctx.enter_context(nc.psum_tensor(f"PW3{i}", [128, 512], F32))
              for i in range(4)]
        ev_add = {}
        ev_tq_prev = None
        for q in range(4):
            ev_tq = []
            for i in range(NT):
                evd = p.dma("gpsimd", Hs[:, 0:D],
                            d_t.ap()[i * 128:(i + 1) * 128,
                                     q * 2048:(q + 1) * 2048],
                            waits=(ev_tq[-1] if ev_tq else ev_tq_prev,))
                evi = p.emit("vector", lambda h, i=i: h.tensor_scalar_mul(
                    out=I32S[:, :], in0=Hs[:, 0:D],
                    scalar1=st(i, SL_RCPT)), inc="V",
                    waits=(evd, ev_t3, ev_tq[-1] if ev_tq else ev_tq_prev))
                evb = p.emit("vector", lambda h: h.tensor_copy(
                    out=WS[:, 0:2048], in_=I32S[:, :]), inc="V", waits=(evi,))
                for g in range(4):
                    evtr = None
                    for j in range(4):
                        kt = g * 4 + j
                        evtr = p.emit("tensor", lambda h, kt=kt, j=j: h.transpose(
                            PTR3v[:, j * 128:(j + 1) * 128],
                            WS[:, kt * 128:(kt + 1) * 128], ident[:, :]),
                            inc="T", waits=(evb,
                                            ev_tq[-1] if ev_tq else ev_tq_prev))
                    dst = Cb[:].rearrange("q (kt t) -> q kt t", kt=KT)[
                        :, g * 4:(g + 1) * 4, i * 128:(i + 1) * 128]
                    eve = p.emit("scalar", lambda h, dst=dst: h.copy(
                        out=dst, in_=PTR3v[:].rearrange("q (a b) -> q a b", a=4)),
                        inc="A", waits=(evtr,))
                    ev_tq.append(eve)
            ev_tqT = ev_tq[-1]
            ev_tq_prev = ev_tqT
            ev_gb3 = None
            for sl in range(4):
                ev_q = []
                raw_guard3 = getattr(p, "_rg3", {})
                for kt in range(KT):
                    slot = kt % 4
                    row0 = q * 2048 + kt * 128
                    evd = p.dma("sync", RAW[:, slot * 512:(slot + 1) * 512],
                                dwdq[row0:row0 + 128, sl * 512:(sl + 1) * 512],
                                waits=(raw_guard3.get(slot),), sem=f"q{slot}")
                    evu = p.emit("vector", lambda h, sl_=slot: h.tensor_scalar(
                        out=Ub[:, :],
                        in0=RAW[:, sl_ * 512:(sl_ + 1) * 512],
                        scalar1=THR[:, 2:3], scalar2=0.5,
                        op0=ALU.mult, op1=ALU.is_ge), inc="V", waits=(evd,))
                    evv = p.emit("vector", lambda h, sl_=slot: h.tensor_scalar(
                        out=Vb[:, :],
                        in0=RAW[:, sl_ * 512:(sl_ + 1) * 512],
                        scalar1=THR[:, 2:3], scalar2=-0.5,
                        op0=ALU.mult, op1=ALU.is_le), inc="V", waits=(evd,))
                    raw_guard3[slot] = evv
                    p._rg3 = raw_guard3
                    evw = p.emit("vector", lambda h, kt=kt: h.tensor_sub(
                        out=Gb[:, kt * 512:(kt + 1) * 512],
                        in0=Ub[:, :], in1=Vb[:, :]), inc="V",
                        waits=(evu, evv, ev_gb3 if kt == 0 else None))
                    ev_q.append((evw, evv))
                for i in range(NT):
                    bank = i % 4
                    evm = None
                    for kt in range(KT):
                        evm = p.emit("tensor", lambda h, bank=bank, kt=kt, i=i:
                                     h.matmul(
                            PW3v[bank][:, :], CbT(kt, i),
                            Gb[:, kt * 512:(kt + 1) * 512],
                            start=(kt == 0), stop=(kt == KT - 1)),
                            inc="T" if kt == KT - 1 else None,
                            waits=(ev_q[kt][0], ev_tqT,
                                   ev_add.get(bank) if kt == 0 else None))
                    dst = A8[:, i * D + sl * 512: i * D + (sl + 1) * 512]
                    if q == 0:
                        eva = p.emit("vector", lambda h, dst=dst, bank=bank:
                                     h.tensor_copy(out=dst, in_=PW3v[bank][:, :]),
                                     inc="V", waits=(evm,))
                    else:
                        eva = p.emit("vector", lambda h, dst=dst, bank=bank:
                                     h.tensor_add(out=dst, in0=dst,
                                                  in1=PW3v[bank][:, :]),
                                     inc="V", waits=(evm,))
                    ev_add[bank] = eva
                    if i == NT - 1:
                        ev_gb3 = evm
    ev_y = ("V", p.cnt["V"])
    for i in range(NT):
        p.dma("gpsimd", d_y.ap()[i * 128:(i + 1) * 128, :],
              A8[:, i * D:(i + 1) * D], waits=(ev_y,))
    if "qffn" in taps:
        def mk_qf(i):
            return p.emit("vector", lambda h, i=i: h.tensor_scalar_mul(
                out=Hs[:, :], in0=A8[:, i * D:(i + 1) * D],
                scalar1=st(i, SL_TSWS)), inc="V", waits=(ev_y,))
        dump_tiles("qffn", mk_qf)
    p.barrier()

    ev_cb = p.dma("gpsimd", CB[:, :], dn1.broadcast_to([128, D]))
    ev_gw1 = p.dma("gpsimd",
                   OSC[:, 0:KT * LOW].rearrange("q (kt l) -> q kt l", kt=KT),
                   dgw1.rearrange("(kt q) l -> q kt l", q=128), sem="dw", gate=True)
    ev_gw2 = p.dma("gpsimd", Z1W[0:16, 1024:1024 + KEY], dgw2, sem="dw", gate=True)

    ev_hT = []
    ev_xr = {}
    with nc.psum_tensor("PTR4", [128, 512], BF16) as PTR4v:
        for i in range(NT):
            xa = X2[:, :]
            evx = p.dma("gpsimd", xa, dx[i * 128:(i + 1) * 128, :],
                        waits=(ev_xr.get(i - 1),))
            evt1 = p.emit("vector", lambda h, xa=xa, i=i: h.tensor_scalar(
                out=Hs[:, 0:D], in0=xa, scalar1=st(i, SL_RR2),
                scalar2=st(i, SL_MRR2), op0=ALU.mult, op1=ALU.subtract), inc="V",
                waits=(evx,))
            ev_xr[i] = evt1
            evt2 = p.emit("vector", lambda h: h.tensor_mul(
                out=XQB[:, :], in0=Hs[:, 0:D], in1=CB[:, :]), inc="V",
                waits=(evt1, ev_cb, ev_hT[-1] if ev_hT else None))
            for g in range(4):
                evtr = None
                for j in range(4):
                    kt = g * 4 + j
                    evtr = p.emit("tensor", lambda h, kt=kt, j=j: h.transpose(
                        PTR4v[:, j * 128:(j + 1) * 128],
                        XQB[:, kt * 128:(kt + 1) * 128], ident[:, :]),
                        inc="T", waits=(evt2, ev_hT[-1] if ev_hT else None))
                dst = Bb[:].rearrange("q (kt t) -> q kt t", kt=KT)[
                    :, g * 4:(g + 1) * 4, i * 128:(i + 1) * 128]
                eve = p.emit("scalar", lambda h, dst=dst: h.copy(
                    out=dst, in_=PTR4v[:].rearrange("q (a b) -> q a b", a=4)),
                    inc="A", waits=(evtr,))
                ev_hT.append(eve)
    ev_hTd = ev_hT[-1]
    p.barrier()

    ev_S0 = p.emit("vector", lambda h: h.memset(S[:, :], 0.0), inc="V")
    ev_Sbf0 = p.emit("vector", lambda h: h.memset(S_bf[:, :], 0.0), inc="V")
    ev_qe = {}
    ev_ke = {}
    ev_psh = {}
    with ExitStack() as pctx:
        PZ1 = pctx.enter_context(nc.psum_tensor("PZ1", [16, 1024], F32))
        PZ2 = pctx.enter_context(nc.psum_tensor("PZ2", [128, 1024], F32))
        PQ = pctx.enter_context(nc.psum_tensor("PQ", [128, 1024], F32))
        PK = pctx.enter_context(nc.psum_tensor("PK", [128, 1024], F32))
        ev_z1 = None
        for half in range(2):
            for kt in range(KT):
                ev_z1 = p.emit("tensor", lambda h, kt=kt, half=half: h.matmul(
                    PZ1[:, half * 512:(half + 1) * 512],
                    OSC[:, kt * LOW:(kt + 1) * LOW],
                    Bb[:, kt * TPC + half * 512: kt * TPC + (half + 1) * 512],
                    start=(kt == 0), stop=(kt == KT - 1)),
                    inc="T" if kt == KT - 1 else None,
                    waits=(ev_gw1, ev_hTd))
        ev_z1sb = p.emit("vector", lambda h: h.tensor_copy(
            out=Z1W[0:16, 0:1024], in_=PZ1[:, :]), inc="V", waits=(ev_z1,))

        ev_wl = {}
        for h_ in range(H):
            sl = h_ % 2
            wq_dst = WS[:, sl * 2048:(sl + 1) * 2048]
            wk_dst = WS[:, 4096 + sl * 2048: 4096 + (sl + 1) * 2048]
            ev_wl[("q", h_)] = p.dma(
                "gpsimd", wq_dst.rearrange("q (kt m) -> q kt m", kt=KT),
                dwq.rearrange("(kt q) m -> q kt m", q=128)[
                    :, :, h_ * 128:(h_ + 1) * 128],
                waits=(ev_qe.get(h_ - 2),), sem="dw")
            ev_wl[("k", h_)] = p.dma(
                "gpsimd", wk_dst.rearrange("q (kt m) -> q kt m", kt=KT),
                dwk.rearrange("(kt q) m -> q kt m", q=128)[
                    :, :, h_ * 128:(h_ + 1) * 128],
                waits=(ev_ke.get(h_ - 2),), sem="dw")

            ev_z2 = None
            for half in range(2):
                ev_z2 = p.emit("tensor", lambda h, h_=h_, half=half: h.matmul(
                    PZ2[:, half * 512:(half + 1) * 512],
                    Z1W[0:16, 1024 + h_ * 128:1024 + (h_ + 1) * 128],
                    Z1W[0:16, half * 512:(half + 1) * 512],
                    start=True, stop=True),
                    inc="T" if half == 1 else None,
                    waits=(ev_gw2, ev_z1sb,
                           ev_psh.get(h_ - 1)))
            e1 = p.emit("scalar", lambda h, h_=h_: h.activation(
                out=Hs[:, 0:1024], in_=PZ2[:, :], func=AF.Abs,
                scale=CK[:, 0:1], bias=GKB[:, h_:h_ + 1]), inc="A",
                waits=(ev_z2, ev_gkb, ev_ck))
            e2 = p.emit("scalar", lambda h: h.activation(
                out=Hs[:, 1024:2048], in_=Hs[:, 0:1024], func=AF.Exp,
                scale=-1.0), inc="A", waits=(e1,))
            e3 = p.emit("scalar", lambda h: h.activation(
                out=Hs[:, 0:1024], in_=Hs[:, 1024:2048], func=AF.Ln,
                scale=CK[:, 0:1], bias=CK[:, 0:1]), inc="A", waits=(e2,))
            e4 = p.emit("scalar", lambda h, h_=h_: h.activation(
                out=Hs[:, 1024:2048], in_=PZ2[:, :], func=AF.Relu,
                scale=CK[:, 1:2], bias=NGKB[:, h_:h_ + 1]), inc="A",
                waits=(e3, ev_ngkb, ev_ckm))
            e5 = p.emit("vector", lambda h: h.tensor_add(
                out=Hs[:, 0:1024], in0=Hs[:, 0:1024], in1=Hs[:, 1024:2048]),
                inc="V", waits=(e4,))
            evsp_dump = None
            if "sp" in taps:
                evsp_dump = p.dma("gpsimd", taps["sp"][h_ * 128:(h_ + 1) * 128, :],
                                  Hs[:, 0:1024], waits=(e5,))
                out_evs.append(evsp_dump)
            e6 = p.emit("vector", lambda h: h.tensor_tensor_scan(
                out=Hs[:, 1024:2048], data0=SEGM[:, :], data1=Hs[:, 0:1024],
                initial=0.0, op0=ALU.mult, op1=ALU.add), inc="V",
                waits=(e5, ev_seg))
            e7 = p.emit("scalar", lambda h: h.activation(
                out=Hs[:, 0:1024], in_=Hs[:, 1024:2048], func=AF.Exp,
                scale=-IGN), inc="A", waits=(e6, evsp_dump))
            e8 = p.emit("scalar", lambda h: h.activation(
                out=EK[:, :], in_=Hs[:, 1024:2048], func=AF.Exp,
                scale=IGN), inc="A", waits=(e6,))
            e9 = p.emit("vector", lambda h, h_=h_: h.tensor_copy(
                out=DEC[:, h_ * NCH:(h_ + 1) * NCH],
                in_=Hs[:, 0:1024].rearrange("q (c j) -> q c j", c=NCH)
                [:, :, 63:64].squeeze(2)), inc="V", waits=(e7,))
            e10 = p.emit("vector", lambda h, h_=h_: h.tensor_tensor_scan(
                out=PIN[:, h_ * NCH:(h_ + 1) * NCH],
                data0=DEC[:, h_ * NCH:(h_ + 1) * NCH], data1=ONES16[:, :],
                initial=1.0, op0=ALU.mult, op1=ALU.mult), inc="V",
                waits=(e9, ev_o16))
            p.emit("vector", lambda h, h_=h_: h.memset(
                PSH[:, h_ * NCH:h_ * NCH + 1], 1.0), inc="V")
            ev_psh[h_] = p.emit("vector", lambda h, h_=h_: h.tensor_copy(
                out=PSH[:, h_ * NCH + 1:(h_ + 1) * NCH],
                in_=PIN[:, h_ * NCH:(h_ + 1) * NCH - 1]), inc="V", waits=(e10,))
            ev_q = None
            ev_k = None
            for half in range(2):
                for kt in range(KT):
                    ev_q = p.emit("tensor", lambda h, sl=sl, kt=kt, half=half:
                                  h.matmul(
                        PQ[:, half * 512:(half + 1) * 512],
                        WS[:, sl * 2048 + kt * 128: sl * 2048 + (kt + 1) * 128],
                        Bb[:, kt * TPC + half * 512: kt * TPC + (half + 1) * 512],
                        start=(kt == 0), stop=(kt == KT - 1)),
                        inc="T" if kt == KT - 1 else None,
                        waits=(ev_wl[("q", h_)], ev_hTd,
                               ev_qe.get(h_ - 1) if kt == 0 else None))
            for half in range(2):
                for kt in range(KT):
                    ev_k = p.emit("tensor", lambda h, sl=sl, kt=kt, half=half:
                                  h.matmul(
                        PK[:, half * 512:(half + 1) * 512],
                        WS[:, 4096 + sl * 2048 + kt * 128:
                           4096 + sl * 2048 + (kt + 1) * 128],
                        Bb[:, kt * TPC + half * 512: kt * TPC + (half + 1) * 512],
                        start=(kt == 0), stop=(kt == KT - 1)),
                        inc="T" if kt == KT - 1 else None,
                        waits=(ev_wl[("k", h_)],
                               ev_ke.get(h_ - 1) if kt == 0 else None))
            ev_qe[h_] = p.emit("vector", lambda h, h_=h_: h.scalar_tensor_tensor(
                out=Eb[:, h_ * TPC:(h_ + 1) * TPC], in0=PQ[:, :], scalar=C1,
                in1=Hs[:, 0:1024], op0=ALU.mult, op1=ALU.mult), inc="V",
                waits=(ev_q, e7, e9))
            ev_ke[h_] = p.emit("vector", lambda h, h_=h_: h.tensor_mul(
                out=Fb[:, h_ * TPC:(h_ + 1) * TPC], in0=PK[:, :],
                in1=EK[:, :]), inc="V", waits=(ev_k, e8))
            for c in range(NCH):
                p.emit("vector", lambda h, h_=h_, c=c: h.tensor_scalar_mul(
                    out=Gb[:, h_ * TPC + c * 64: h_ * TPC + (c + 1) * 64],
                    in0=Eb[:, h_ * TPC + c * 64: h_ * TPC + (c + 1) * 64],
                    scalar1=PSH[:, h_ * NCH + c: h_ * NCH + c + 1]), inc="V",
                    waits=(ev_qe[h_], ev_psh[h_]))
    ev_qt = ("V", p.cnt["V"])
    if "qe" in taps:
        for h_ in range(H):
            eve = p.emit("vector", lambda h, h_=h_: h.tensor_copy(
                out=Hs[:, 0:1024], in_=Eb[:, h_ * TPC:(h_ + 1) * TPC]),
                inc="V", waits=(ev_qe[h_],))
            out_evs.append(p.dma("gpsimd", taps["qe"][h_ * 128:(h_ + 1) * 128, :],
                                 Hs[:, 0:1024], waits=(eve,)))
            p.barrier()
    if "ke" in taps:
        for h_ in range(H):
            eve = p.emit("vector", lambda h, h_=h_: h.tensor_copy(
                out=Hs[:, 0:1024], in_=Fb[:, h_ * TPC:(h_ + 1) * TPC]),
                inc="V", waits=(ev_ke[h_],))
            out_evs.append(p.dma("gpsimd", taps["ke"][h_ * 128:(h_ + 1) * 128, :],
                                 Hs[:, 0:1024], waits=(eve,)))
            p.barrier()
    p.barrier()

    with ExitStack() as pctx:
        PV = [pctx.enter_context(nc.psum_tensor(f"PV{i}", [128, 512], F32))
              for i in range(2)]
        ev_vfree = [None, None]
        ev_wv = None
        for sl in range(4):
            ev_wv_prev = ev_wv
            ev_wv = None
            for kt in range(KT):
                ev_wv = p.dma("gpsimd", WS[:, kt * 512:(kt + 1) * 512],
                              dwv[kt * 128:(kt + 1) * 128,
                                  sl * 512:(sl + 1) * 512],
                              waits=(ev_vfree[0], ev_vfree[1]) if kt == 0 else (),
                              sem="dw", gate=(kt == 0))
            for i in range(NT):
                b = i % 2
                evm = None
                for kt in range(KT):
                    evm = p.emit("tensor", lambda h, b=b, kt=kt, i=i: h.matmul(
                        PV[b][:, :], BbT(kt, i),
                        WS[:, kt * 512:(kt + 1) * 512],
                        start=(kt == 0), stop=(kt == KT - 1)),
                        inc="T" if kt == KT - 1 else None,
                        waits=(ev_wv, ev_vfree[b] if kt == 0 else None))
                eve = p.emit("scalar", lambda h, b=b, i=i, sl=sl: h.copy(
                    out=Cb[:, i * D + sl * 512: i * D + (sl + 1) * 512],
                    in_=PV[b][:, :]), inc="A", waits=(evm,))
                ev_vfree[b] = eve
    ev_v = ("A", p.cnt["A"])
    p.barrier()

    with ExitStack() as pctx:
        pO = pctx.enter_context(nc.psum_tensor("pO", [64, 2048], F32))
        pA = pctx.enter_context(nc.psum_tensor("pA", [64, 64], F32))
        pT = pctx.enter_context(nc.psum_tensor("pT", [64, 128], BF16))
        pS = pctx.enter_context(nc.psum_tensor("pS", [128, 256], F32))
        ev_S_upd = {h_: ev_S0 for h_ in range(H)}
        ev_Sbf = {h_: ev_Sbf0 for h_ in range(H)}
        ev_oe = {}
        ev_Am = None
        ev_kd = None
        ev_pS_user = None
        ev_o2_prev = None
        ev_s1_prev = None
        for c in range(NCH):
            r = (c % 2) * 64
            i = c // 2
            for h_ in range(H):
                qs = Eb[:, h_ * TPC + c * 64: h_ * TPC + (c + 1) * 64]
                ks = Fb[:, h_ * TPC + c * 64: h_ * TPC + (c + 1) * 64]
                vsl = Cb[r:r + 64, i * D + h_ * HV: i * D + (h_ + 1) * HV]
                evA = p.emit("tensor", lambda h, qs=qs, ks=ks: h.matmul(
                    pA[:, :], ks, qs, start=True, stop=True), inc="T",
                    waits=(ev_qe[h_], ev_ke[h_], ev_Am))
                ev_Am = p.emit("vector", lambda h, r=r: h.tensor_mul(
                    out=SAM[r:r + 64, :], in0=pA[:, :], in1=maskt[:, :]),
                    inc="V", waits=(evA, ev_o2_prev))
                evk1 = p.emit("vector", lambda h, ks=ks, h_=h_, c=c: h.tensor_scalar_mul(
                    out=SKDT[:, :], in0=ks,
                    scalar1=DEC[:, h_ * NCH + c: h_ * NCH + c + 1]), inc="V",
                    waits=(ev_ke[h_], ev_kd))
                evk2 = p.emit("tensor", lambda h: h.transpose(
                    pT[:, :], SKDT[:, :], ident[:, :]), inc="T",
                    waits=(evk1, ev_kd))
                evk3 = p.emit("scalar", lambda h, r=r: h.copy(
                    out=SKD[r:r + 64, :], in_=pT[:, :]), inc="A",
                    waits=(evk2, ev_s1_prev))
                ev_kd = evk3
                evo1 = p.emit("tensor", lambda h, qs=qs, h_=h_: h.matmul(
                    pO[:, h_ * HV:(h_ + 1) * HV], qs,
                    S_bf[:, h_ * HV:(h_ + 1) * HV], start=True, stop=False),
                    inc="T",
                    waits=(ev_Sbf[h_], ev_oe.get((c - 1, h_)),
                           ev_oe.get((c - 1, h_ ^ 1)),
                           ev_oe.get((c, h_ ^ 1))))
                evo2 = p.emit("tensor", lambda h, r=r, h_=h_, vsl=vsl: h.matmul(
                    pO[:, h_ * HV:(h_ + 1) * HV], SAM[r:r + 64, :], vsl,
                    start=False, stop=True), inc="T", waits=(ev_Am, ev_v))
                ev_o2_prev = evo2
                evoe = p.emit("scalar", lambda h, r=r, i=i, h_=h_: h.copy(
                    out=Db[r:r + 64, i * D + h_ * HV: i * D + (h_ + 1) * HV],
                    in_=pO[:, h_ * HV:(h_ + 1) * HV]), inc="A", waits=(evo2,))
                ev_oe[(c, h_)] = evoe
                evs1 = p.emit("tensor", lambda h, r=r, vsl=vsl: h.matmul(
                    pS[:, :], SKD[r:r + 64, :], vsl, start=True, stop=True),
                    inc="T", waits=(evk3, ev_pS_user))
                ev_s1_prev = evs1
                ev_S_upd[h_] = p.emit("vector", lambda h, h_=h_, c=c:
                                      h.scalar_tensor_tensor(
                    out=S[:, h_ * HV:(h_ + 1) * HV],
                    in0=S[:, h_ * HV:(h_ + 1) * HV],
                    scalar=DEC[:, h_ * NCH + c: h_ * NCH + c + 1],
                    in1=pS[:, :], op0=ALU.mult, op1=ALU.add), inc="V",
                    waits=(evs1, evo1))
                ev_pS_user = ev_S_upd[h_]
                ev_Sbf[h_] = p.emit("vector", lambda h, h_=h_: h.tensor_copy(
                    out=S_bf[:, h_ * HV:(h_ + 1) * HV],
                    in_=S[:, h_ * HV:(h_ + 1) * HV]), inc="V",
                    waits=(ev_S_upd[h_],))
    ev_scan = ("V", p.cnt["V"])
    if "o_loc" in taps:
        def mk_ol(i):
            return p.emit("vector", lambda h, i=i: h.tensor_copy(
                out=Hs[:, :], in_=Db[:, i * D:(i + 1) * D]), inc="V",
                waits=(ev_scan,))
        dump_tiles("o_loc", mk_ol)
    if "S" in taps:
        eve = p.emit("vector", lambda h: h.tensor_copy(
            out=Hs[:, :], in_=S[:, :]), inc="V", waits=(ev_scan,))
        out_evs.append(p.dma("gpsimd", taps["S"], Hs[:, :], waits=(eve,)))
        p.barrier()
    p.barrier()

    ev_pe = p.emit("vector", lambda h: h.tensor_copy(
        out=PE8[:, :],
        in_=PIN[:].rearrange("q (h c) -> q h c", h=H)[:, :, NCH - 1:NCH]
        .squeeze(2)), inc="V")
    evd1 = p.dma("gpsimd", ag_in.ap()[:, 0:2048], S[:, :], waits=(ev_scan,))
    evd2 = p.dma("gpsimd", ag_in.ap()[:, 2048:2056], PE8[:, :], waits=(ev_pe,))
    ev_ag = p.emit("gpsimd", lambda h: h.collective_compute(
        "AllGather", ALU.bypass, replica_groups=[list(range(NCORE))],
        ins=[ag_in.ap().opt()], outs=[ag_out.ap().opt()]),
        inc="cc", waits=(evd1, evd2))
    ev_Sz = p.emit("vector", lambda h: h.memset(S[:, :], 0.0), inc="V",
                   waits=(evd1,))
    ev_pref = ev_Sz
    sj_user = [None]
    for j in range(NCORE):
        evl1 = p.dma("gpsimd", Hs[:, 0:2048],
                     ag_out.ap()[j * 128:(j + 1) * 128, 0:2048],
                     waits=(ev_ag, sj_user[0]))
        evl2 = p.dma("gpsimd", PJ8[:, :],
                     ag_out.ap()[j * 128:(j + 1) * 128, 2048:2056],
                     waits=(ev_ag, ev_pref))
        evm = p.emit("vector", lambda h, j=j: h.tensor_scalar(
            out=PJM[:, :], in0=PJ8[:, :], scalar1=GSEL[:, j:j + 1],
            scalar2=GSEL[:, 8 + j:9 + j], op0=ALU.mult, op1=ALU.add), inc="V",
            waits=(evl2, ev_gsel))
        evsm = p.emit("vector", lambda h, j=j: h.tensor_scalar_mul(
            out=Hs[:, 0:2048], in0=Hs[:, 0:2048],
            scalar1=GSEL[:, j:j + 1]), inc="V", waits=(evl1,))
        for h_ in range(H):
            ev_pref = p.emit("vector", lambda h, h_=h_: h.scalar_tensor_tensor(
                out=S[:, h_ * HV:(h_ + 1) * HV], in0=S[:, h_ * HV:(h_ + 1) * HV],
                scalar=PJM[:, h_:h_ + 1],
                in1=Hs[:, h_ * HV:(h_ + 1) * HV],
                op0=ALU.mult, op1=ALU.add), inc="V", waits=(evm, evsm, ev_pref))
        sj_user[0] = ev_pref
    ev_sin = ev_pref
    ev_sin = p.emit("vector", lambda h: h.tensor_copy(
        out=S_bf[:, :], in_=S[:, :]), inc="V", waits=(ev_pref,))
    p.barrier()

    with ExitStack() as pctx:
        PC = [pctx.enter_context(nc.psum_tensor(f"PC{i}", [128, 256], F32))
              for i in range(2)]
        ev_cfree = [None, None]
        for h_ in range(H):
            for i in range(NT):
                b = (h_ * NT + i) % 2
                evc = p.emit("tensor", lambda h, h_=h_, i=i, b=b: h.matmul(
                    PC[b][:, :], Gb[:, h_ * TPC + i * 128: h_ * TPC + (i + 1) * 128],
                    S_bf[:, h_ * HV:(h_ + 1) * HV], start=True, stop=True),
                    inc="T", waits=(ev_sin, ev_qt, ev_cfree[b]))
                eva = p.emit("vector", lambda h, h_=h_, i=i, b=b: h.tensor_add(
                    out=Db[:, i * D + h_ * HV: i * D + (h_ + 1) * HV],
                    in0=Db[:, i * D + h_ * HV: i * D + (h_ + 1) * HV],
                    in1=PC[b][:, :]), inc="V", waits=(evc, ev_scan))
                ev_cfree[b] = eva
    ev_corr = ("V", p.cnt["V"])
    if "ocor" in taps:
        def mk_oc(i):
            return p.emit("vector", lambda h, i=i: h.tensor_copy(
                out=Hs[:, :], in_=Db[:, i * D:(i + 1) * D]), inc="V",
                waits=(ev_corr,))
        dump_tiles("ocor", mk_oc)
    for i in range(NT):
        evsq = p.emit("vector", lambda h, i=i: h.tensor_mul(
            out=X2[:, 0:D], in0=Db[:, i * D:(i + 1) * D],
            in1=Db[:, i * D:(i + 1) * D]), inc="V", waits=(ev_corr,))
        ev_red = None
        for h_ in range(H):
            ev_red = p.emit("vector", lambda h, i=i, h_=h_: h.reduce_sum(
                out=RSQ[:, i * 8 + h_: i * 8 + h_ + 1],
                in_=X2[:, h_ * HV:(h_ + 1) * HV], axis=AX), inc="V",
                waits=(evsq,))
        if "sqdump" in taps and i == 7:
            out_evs.append(p.dma("gpsimd", taps["sqdump"], X2[:, :],
                                 waits=(("V", p.cnt["V"]),)))
            p.barrier()
        e1 = p.emit("vector", lambda h, i=i: h.tensor_scalar(
            out=RSQ[:, i * 8:(i + 1) * 8], in0=RSQ[:, i * 8:(i + 1) * 8],
            scalar1=1.0 / HV, scalar2=1e-6, op0=ALU.mult, op1=ALU.add),
            inc="V", waits=(ev_red,))
        e2 = p.emit("vector", lambda h, i=i: h.reciprocal(
            out=RSQ[:, i * 8:(i + 1) * 8], in_=RSQ[:, i * 8:(i + 1) * 8]),
            inc="V", waits=(e1,))
    ev_e2all = ("V", p.cnt["V"])
    p.emit("scalar", lambda h: h.activation(
        out=RSQ[:, :], in_=RSQ[:, :], func=AF.Sqrt), inc="A",
        waits=(ev_e2all,))
    ev_rsq = ("A", p.cnt["A"])
    ev_rsqV = ("V", p.cnt["V"])
    ev_cb2 = p.dma("gpsimd", CB[:, :], dgnw.broadcast_to([128, D]),
                   waits=(ev_xr.get(NT - 1),))
    if "cbdump" in taps:
        out_evs.append(p.dma("gpsimd", taps["cbdump"], CB[:, :],
                             waits=(ev_cb2,)))
        p.barrier()
    if "rsqdump" in taps:
        out_evs.append(p.dma("gpsimd", taps["rsqdump"], RSQ[:, :],
                             waits=(ev_rsq,)))
        p.barrier()
    with ExitStack() as pctx:
        PG = [pctx.enter_context(nc.psum_tensor(f"PG{i}", [128, 512], F32))
              for i in range(2)]
        PTR7v = pctx.enter_context(nc.psum_tensor("PTR7", [128, 512], BF16))
        ev_gfree = [None, None]
        ev_tr_ev = None
        ev_wgl = None
        ev_em2_prev = None
        ev_tr4_prev = None
        for sl in range(4):
            for kt in range(KT):
                ev_wgl = p.dma("gpsimd", WS[:, kt * 512:(kt + 1) * 512],
                               dwg[kt * 128:(kt + 1) * 128,
                                   sl * 512:(sl + 1) * 512],
                               waits=(ev_gfree[0], ev_gfree[1]) if kt == 0 else (),
                               sem="dw", gate=(kt == 0))
            for i in range(NT):
                b = i % 2
                evm = None
                for kt in range(KT):
                    evm = p.emit("tensor", lambda h, b=b, kt=kt, i=i: h.matmul(
                        PG[b][:, :], BbT(kt, i), WS[:, kt * 512:(kt + 1) * 512],
                        start=(kt == 0), stop=(kt == KT - 1)),
                        inc="T" if kt == KT - 1 else None,
                        waits=(ev_wgl, ev_corr,
                               ev_gfree[b] if kt == 0 else None))
                evsil = p.emit("scalar", lambda h, b=b: h.activation(
                    out=Gb[:, 0:512], in_=PG[b][:, :], func=AF.Silu), inc="A",
                    waits=(evm, ev_rsq, ev_em2_prev, ev_corr))
                ev_gfree[b] = evsil
                h0, h1 = sl * 2, sl * 2 + 1
                en1 = p.emit("vector", lambda h, i=i, h0=h0: h.tensor_scalar_mul(
                    out=Gb[:, 512:768],
                    in0=Db[:, i * D + h0 * HV: i * D + (h0 + 1) * HV],
                    scalar1=RSQ[:, i * 8 + h0: i * 8 + h0 + 1]), inc="V",
                    waits=(ev_rsq, ev_corr, ev_em2_prev))
                en2 = p.emit("vector", lambda h, i=i, h1=h1: h.tensor_scalar_mul(
                    out=Gb[:, 768:1024],
                    in0=Db[:, i * D + h1 * HV: i * D + (h1 + 1) * HV],
                    scalar1=RSQ[:, i * 8 + h1: i * 8 + h1 + 1]), inc="V",
                    waits=(ev_rsq,))
                em1 = p.emit("vector", lambda h, sl=sl: h.tensor_mul(
                    out=Gb[:, 512:1024], in0=Gb[:, 512:1024],
                    in1=CB[:, sl * 512:(sl + 1) * 512]), inc="V",
                    waits=(en1, en2, ev_cb2))
                em2 = p.emit("vector", lambda h: h.tensor_mul(
                    out=Ub[:, :], in0=Gb[:, 512:1024],
                    in1=Gb[:, 0:512]), inc="V",
                    waits=(em1, evsil, ev_tr4_prev))
                ev_em2_prev = em2
                evtr = None
                for j in range(4):
                    evtr = p.emit("tensor", lambda h, j=j: h.transpose(
                        PTR7v[:, j * 128:(j + 1) * 128],
                        Ub[:, j * 128:(j + 1) * 128], ident[:, :]),
                        inc="T", waits=(em2, ev_tr_ev))
                ev_tr4_prev = evtr
                dst = Cb[:].rearrange("q (kt t) -> q kt t", kt=KT)[
                    :, sl * 4:(sl + 1) * 4, i * 128:(i + 1) * 128]
                ev_tr_ev = p.emit("scalar", lambda h, dst=dst: h.copy(
                    out=dst, in_=PTR7v[:].rearrange("q (a b) -> q a b", a=4)),
                    inc="A", waits=(evtr,))
    ev_ofT = ("A", p.cnt["A"])
    if "cbof" in taps:
        out_evs.append(p.dma("gpsimd", taps["cbof"], Cb[:, :],
                             waits=(ev_ofT,)))
        p.barrier()
    if "ofin" in taps:
        def mk_of(i):
            return p.emit("vector", lambda h, i=i: h.tensor_copy(
                out=Hs[:, :], in_=Db[:, i * D:(i + 1) * D]), inc="V",
                waits=(ev_ofT,))
        dump_tiles("ofin", mk_of)
    p.barrier()

    ev_acc = {}
    for i in range(NT):
        xa = X2[:, :]
        evx = p.dma("gpsimd", xa, dx[i * 128:(i + 1) * 128, :],
                    waits=(ev_acc.get(i - 1), ev_rsqV))
        ev_acc[i] = p.emit("vector", lambda h, xa=xa, i=i: h.tensor_scalar(
            out=A8[:, i * D:(i + 1) * D], in0=xa, scalar1=st(i, SL_R),
            scalar2=st(i, SL_MR), op0=ALU.mult, op1=ALU.subtract), inc="V",
            waits=(evx,))
    with ExitStack() as pctx:
        PO2 = [pctx.enter_context(nc.psum_tensor(f"PO2{i}", [128, 512], F32))
               for i in range(2)]
        ev_ofree = [None, None]
        for sl in range(4):
            ev_wol = None
            for kt in range(KT):
                ev_wol = p.dma("gpsimd", WS[:, kt * 512:(kt + 1) * 512],
                               dwo[kt * 128:(kt + 1) * 128,
                                   sl * 512:(sl + 1) * 512],
                               waits=(ev_ofree[0], ev_ofree[1]) if kt == 0 else (),
                               sem="dw", gate=(kt == 0))
            for i in range(NT):
                b = i % 2
                evm = None
                for kt in range(KT):
                    evm = p.emit("tensor", lambda h, b=b, kt=kt, i=i: h.matmul(
                        PO2[b][:, :], CbT(kt, i),
                        WS[:, kt * 512:(kt + 1) * 512],
                        start=(kt == 0), stop=(kt == KT - 1)),
                        inc="T" if kt == KT - 1 else None,
                        waits=(ev_wol, ev_ofT,
                               ev_ofree[b] if kt == 0 else None))
                eva = p.emit("vector", lambda h, b=b, i=i, sl=sl: h.tensor_add(
                    out=A8[:, i * D + sl * 512: i * D + (sl + 1) * 512],
                    in0=A8[:, i * D + sl * 512: i * D + (sl + 1) * 512],
                    in1=PO2[b][:, :]), inc="V", waits=(evm, ev_acc[i]))
                ev_ofree[b] = eva
    ev_x1 = ("V", p.cnt["V"])
    if "x1" in taps:
        def mk_x1(i):
            return p.emit("vector", lambda h, i=i: h.tensor_copy(
                out=Hs[:, :], in_=A8[:, i * D:(i + 1) * D]), inc="V",
                waits=(ev_x1,))
        dump_tiles("x1", mk_x1)
    p.barrier()

    ev_cb3 = p.dma("gpsimd", CB[:, :], dffw.broadcast_to([128, D]))
    for i in range(NT):
        evsq = p.emit("scalar", lambda h, i=i: h.activation(
            out=Hs[:, 0:D], in_=A8[:, i * D:(i + 1) * D], func=AF.Square,
            accum_out=st(i, 12)), inc="A", waits=(ev_x1,))
        e1 = p.emit("vector", lambda h, i=i: h.tensor_scalar(
            out=st(i, 13), in0=st(i, 12), scalar1=1.0 / D, scalar2=1e-6,
            op0=ALU.mult, op1=ALU.add), inc="V", waits=(evsq,))
        e2 = p.emit("vector", lambda h, i=i: h.reciprocal(
            out=st(i, 14), in_=st(i, 13)), inc="V", waits=(e1,))
        p.emit("scalar", lambda h, i=i: h.activation(
            out=st(i, SL_R2B), in_=st(i, 14), func=AF.Sqrt), inc="A",
            waits=(e2,))
    ev_r2b = ("A", p.cnt["A"])
    ev_h2 = []
    with nc.psum_tensor("PTR8", [128, 512], BF16) as PTR8v:
        for i in range(NT):
            evh = p.emit("vector", lambda h, i=i: h.scalar_tensor_tensor(
                out=Fb[:, 0:2048], in0=A8[:, i * D:(i + 1) * D],
                scalar=st(i, SL_R2B), in1=CB[:, :], op0=ALU.mult,
                op1=ALU.mult), inc="V", waits=(ev_r2b, ev_cb3, ev_x1,
                                               ev_h2[-1] if ev_h2 else None))
            for g in range(4):
                evtr = None
                for j in range(4):
                    kt = g * 4 + j
                    evtr = p.emit("tensor", lambda h, kt=kt, j=j: h.transpose(
                        PTR8v[:, j * 128:(j + 1) * 128],
                        Fb[:, kt * 128:(kt + 1) * 128], ident[:, :]),
                        inc="T", waits=(evh, ev_h2[-1] if ev_h2 else None))
                dst = Cb[:].rearrange("q (kt t) -> q kt t", kt=KT)[
                    :, g * 4:(g + 1) * 4, i * 128:(i + 1) * 128]
                eve = p.emit("scalar", lambda h, dst=dst: h.copy(
                    out=dst, in_=PTR8v[:].rearrange("q (a b) -> q a b", a=4)),
                    inc="A", waits=(evtr,))
                ev_h2.append(eve)
    ev_h2T = ev_h2[-1]
    p.barrier()

    chunks = [(0, 16), (16, 32), (32, MA)]
    with ExitStack() as pctx:
        PG2 = pctx.enter_context(nc.psum_tensor("PG8", [128, 1024], F32))
        PU2 = pctx.enter_context(nc.psum_tensor("PU8", [128, 1024], F32))
        PD = [pctx.enter_context(nc.psum_tensor(f"PD8{i}", [128, 512], F32))
              for i in range(2)]
        ev_ffb = None
        ev_dfree = [None, None]
        for (m0, m1) in chunks:
            nm = m1 - m0
            ev_ff = []
            for mg in range((nm + 1) // 2):
                gw = min(2, nm - mg * 2)
                c0 = (m0 + mg * 2) * 128
                ev_wg2 = None
                ev_wu2 = None
                for kt in range(KT):
                    ev_wg2 = p.dma("gpsimd", WS[:, kt * 256: kt * 256 + gw * 128],
                                   dwga[kt * 128:(kt + 1) * 128, c0:c0 + gw * 128],
                                   waits=(ev_ff[-1] if ev_ff else ev_ffb,)
                                   if kt == 0 else (),
                                   sem="dw", gate=(kt == 0))
                    ev_wu2 = p.dma("gpsimd",
                                   WS[:, 4096 + kt * 256: 4096 + kt * 256 + gw * 128],
                                   dwua[kt * 128:(kt + 1) * 128, c0:c0 + gw * 128],
                                   waits=(ev_ff[-1] if ev_ff else ev_ffb,)
                                   if kt == 0 else (),
                                   sem="dw", gate=False)
                for ml in range(gw):
                    m = m0 + mg * 2 + ml
                    evg = None
                    evu = None
                    for half in range(2):
                        for kt in range(KT):
                            evg = p.emit("tensor", lambda h, kt=kt, ml=ml, half=half:
                                         h.matmul(
                                PG2[:, half * 512:(half + 1) * 512],
                                WS[:, kt * 256 + ml * 128: kt * 256 + (ml + 1) * 128],
                                Cb[:, kt * TPC + half * 512:
                                   kt * TPC + (half + 1) * 512],
                                start=(kt == 0), stop=(kt == KT - 1)),
                                inc="T" if kt == KT - 1 else None,
                                waits=(ev_wg2, ev_h2T,
                                       ev_ff[-1] if (ev_ff and kt == 0) else None))
                    for half in range(2):
                        for kt in range(KT):
                            evu = p.emit("tensor", lambda h, kt=kt, ml=ml, half=half:
                                         h.matmul(
                                PU2[:, half * 512:(half + 1) * 512],
                                WS[:, 4096 + kt * 256 + ml * 128:
                                   4096 + kt * 256 + (ml + 1) * 128],
                                Cb[:, kt * TPC + half * 512:
                                   kt * TPC + (half + 1) * 512],
                                start=(kt == 0), stop=(kt == KT - 1)),
                                inc="T" if kt == KT - 1 else None,
                                waits=(ev_wu2,
                                       ev_ff[-1] if (ev_ff and kt == 0) else None))
                    evs1 = p.emit("scalar", lambda h: h.activation(
                        out=Gb[:, 0:512], in_=PG2[:, 0:512], func=AF.Silu),
                        inc="A", waits=(evg,))
                    evs2 = p.emit("scalar", lambda h: h.activation(
                        out=Gb[:, 512:1024], in_=PG2[:, 512:1024],
                        func=AF.Silu), inc="A", waits=(evg,))
                    mloc = m - m0
                    ef1 = p.emit("vector", lambda h, mloc=mloc: h.tensor_mul(
                        out=Bb[:, mloc * TPC: mloc * TPC + 512],
                        in0=Gb[:, 0:512], in1=PU2[:, 0:512]), inc="V",
                        waits=(evs1, evu))
                    ef2 = p.emit("vector", lambda h, mloc=mloc: h.tensor_mul(
                        out=Bb[:, mloc * TPC + 512: mloc * TPC + 1024],
                        in0=Gb[:, 512:1024], in1=PU2[:, 512:1024]), inc="V",
                        waits=(evs2, evu))
                    ev_ff.append(ef2)
                    ev_ff.append(ef1)
            ev_ffd = ev_ff[-1]
            ev_ffv = ("V", p.cnt["V"])
            for sl in range(4):
                ev_wd2 = None
                for jm in range(nm):
                    ev_wd2 = p.dma("gpsimd", WS[:, jm * 512:(jm + 1) * 512],
                                   dwda[(m0 + jm) * 128:(m0 + jm + 1) * 128,
                                        sl * 512:(sl + 1) * 512],
                                   waits=(ev_ffv, ev_dfree[0], ev_dfree[1])
                                   if jm == 0 else (),
                                   sem="dw", gate=(jm == 0))
                for i in range(NT):
                    b = i % 2
                    evm = None
                    for jm in range(nm):
                        evm = p.emit("tensor", lambda h, b=b, jm=jm, i=i, nm=nm: h.matmul(
                            PD[b][:, :],
                            Bb[:, jm * TPC + i * 128: jm * TPC + (i + 1) * 128],
                            WS[:, jm * 512:(jm + 1) * 512],
                            start=(jm == 0), stop=(jm == nm - 1)),
                            inc="T" if jm == nm - 1 else None,
                            waits=(ev_wd2, ev_ffv,
                                   ev_dfree[b] if jm == 0 else None))
                    eva = p.emit("vector", lambda h, b=b, i=i, sl=sl: h.tensor_add(
                        out=A8[:, i * D + sl * 512: i * D + (sl + 1) * 512],
                        in0=A8[:, i * D + sl * 512: i * D + (sl + 1) * 512],
                        in1=PD[b][:, :]), inc="V", waits=(evm,))
                    ev_dfree[b] = eva
            ev_ffb = ("V", p.cnt["V"])
    p.barrier()

    ev_fin = {}
    for i in range(NT):
        evy = p.dma("gpsimd", X2[:, :], d_y.ap()[i * 128:(i + 1) * 128, :],
                    waits=(ev_fin.get(i - 1),))
        e1 = p.emit("vector", lambda h, i=i: h.scalar_tensor_tensor(
            out=OUTS[:, :], in0=X2[:, :], scalar=st(i, SL_TSWS),
            in1=A8[:, i * D:(i + 1) * D], op0=ALU.mult, op1=ALU.add), inc="V",
            waits=(evy,))
        evx = p.dma("gpsimd", X2[:, :], dx[i * 128:(i + 1) * 128, :],
                    waits=(e1,))
        e2 = p.emit("vector", lambda h: h.tensor_add(
            out=OUTS[:, :], in0=OUTS[:, :], in1=X2[:, :]), inc="V",
            waits=(evx,))
        ev_fin[i] = e2
        out_evs.append(p.dma("gpsimd", dout[i * 128:(i + 1) * 128, :],
                             OUTS[:, :], waits=(e2,)))

    p.barrier()
    with nc.Block() as block:
        p.finalize(block)
    ctx.close()
    return nc


_NC_CACHE = {}


def _get_nc(debug_taps=()):
    key = tuple(sorted(debug_taps))
    if key not in _NC_CACHE:
        _NC_CACHE[key] = build_nc(debug_taps=key)
    return _NC_CACHE[key]


def stage_inputs(inputs, debug_taps=()):
    g = lambda k: np.ascontiguousarray(np.asarray(inputs[k], dtype=np.float32))
    x = g("L0_attn_x").reshape(T, D)
    wgT = np.ascontiguousarray(g("L0_ffn_w_gate").T)
    wuT = np.ascontiguousarray(g("L0_ffn_w_up").T)
    wdT = np.ascontiguousarray(g("L0_ffn_w_down").T)
    shared = dict(
        wq=g("L0_attn_wq"), wk=g("L0_attn_wk"), wv=g("L0_attn_wv"),
        gk_w1=g("L0_attn_gk_w1"), gk_w2=g("L0_attn_gk_w2"),
        gk_b=g("L0_attn_gk_b"), wg=g("L0_attn_wg"), wo=g("L0_attn_wo"),
        w_gate_a=g("L0_attn_w_gate"), w_up_a=g("L0_attn_w_up"),
        w_down_a=g("L0_attn_w_down"),
        wgT=wgT, wuT=wuT, wdT=wdT,
        n1w_row=g("L0_attn_norm1_w").reshape(1, D),
        ffnw_row=g("L0_attn_ffn_norm_w").reshape(1, D),
        gnw_row=np.ascontiguousarray(
            np.tile(g("L0_attn_gnorm_w"), H)).reshape(1, D),
        ident_in=np.eye(128, dtype=np.float32),
        maskT_in=np.triu(np.ones((64, 64), dtype=np.float32)),
        seg_row=np.tile(
            np.concatenate([[0.0], np.ones(63)]).astype(np.float32),
            NCH).reshape(1, TPC),
    )
    in_maps = []
    for c in range(NCORE):
        im = dict(shared)
        im["x"] = np.ascontiguousarray(x[c * TPC:(c + 1) * TPC])
        gs = np.zeros((1, 16), np.float32)
        gs[0, :c] = 1.0
        gs[0, 8:16] = 1.0 - gs[0, 0:8]
        im["gsel_row"] = gs
        sl = np.empty((12, 128, 4096), np.float32)
        sl[0:4] = wgT[c * 256:(c + 1) * 256].reshape(4, 128, 4096)
        sl[4:8] = wuT[c * 256:(c + 1) * 256].reshape(4, 128, 4096)
        sl[8:12] = wdT[c * 1024:(c + 1) * 1024].reshape(4, 128, 4096)
        im["w_scale_slice"] = sl
        in_maps.append(im)
    return in_maps


def kernel(**inputs):
    nc = _get_nc()
    in_maps = stage_inputs(inputs)
    res = run_bass_kernel_spmd(nc, in_maps, core_ids=list(range(NCORE)))
    out = np.concatenate([r["out"] for r in res.results], axis=0)
    return out.reshape(1, T, D).astype(np.float32)


# version 06
# speedup vs baseline: 1.0784x; 1.0784x over previous
import numpy as np
from contextlib import ExitStack

import concourse.bass as bass
import concourse.mybir as mybir
from concourse.bass_utils import run_bass_kernel_spmd

F32 = mybir.dt.float32
BF16 = mybir.dt.bfloat16
I32 = mybir.dt.int32
AF = mybir.ActivationFunctionType
ALU = mybir.AluOpType
AX = mybir.AxisListType.X

NCORE = 8
T = 8192
TPC = T // NCORE
NT = TPC // 128
D = 2048
KT = D // 128
H = 8
HK = 128
HV = 256
KEY = 1024
CH = 64
NCH = TPC // CH
HIDA = 5504
MA = HIDA // 128
HIDQ = 8192
LOW = 16
C1 = float(HK) ** -0.5
IGN = 1.0 / 16.0
ENGS = ("tensor", "vector", "scalar", "sync", "gpsimd")

SL_R, SL_MR, SL_RR2, SL_MRR2, SL_SQ, SL_RCPQ, SL_XWG, SL_XWU = range(8)
SL_TSWS, SL_RCPT, SL_MXT, SL_R2B, SL_T0, SL_T1, SL_T2, SL_T3 = range(8, 16)


class Prog:
    def __init__(self, nc, sems):
        self.nc = nc
        self.sem = sems
        self.q = {e: [] for e in ENGS}
        self.cnt = {s: 0 for s in sems}
        self.lastw = {e: {} for e in ENGS}

    def emit(self, eng, fn, inc=None, waits=(), inc_n=1):
        wl = []
        for w in waits:
            if w is None:
                continue
            s, v = w
            if v <= 0 or self.lastw[eng].get(s, 0) >= v:
                continue
            self.lastw[eng][s] = v
            wl.append((s, v))
        if inc is not None:
            self.cnt[inc] += inc_n

        def run(h, _fn=fn, _wl=tuple(wl), _inc=inc, _n=inc_n):
            for s, v in _wl:
                h.wait_ge(self.sem[s], v)
            ins = _fn(h)
            if _inc is not None:
                ins.then_inc(self.sem[_inc], _n)

        self.q[eng].append(run)
        return (inc, self.cnt[inc]) if inc is not None else None

    def dma(self, engq, out, in_, waits=(), slow=False, sem=None, gate=None):
        s = sem or {"sync": "ds", "gpsimd": "dg"}[engq]
        if gate is None:
            gate = (engq == "gpsimd")
        w = list(waits)
        if gate:
            w.append((s, self.cnt[s]))
        return self.emit(engq, lambda h, o=out, i=in_, sl=slow: h.dma_start(
            out=o, in_=i, allow_slow_non_contiguous=sl),
            inc=s, waits=w, inc_n=16)

    def barrier(self):
        evs = [(s, self.cnt[s]) for s in self.cnt if self.cnt[s] > 0]
        for e in ENGS:
            for s, v in evs:
                if self.lastw[e].get(s, 0) < v:
                    self.lastw[e][s] = v
                    self.q[e].append(lambda h, _s=s, _v=v: h.wait_ge(self.sem[_s], _v))

    def finalize(self, block):
        for e in ENGS:
            def mk(ops_):
                def body(h):
                    for f in ops_:
                        f(h)
                return body
            getattr(block, e)(mk(list(self.q[e])))


def build_nc(debug_taps=()):
    nc = bass.Bass("TRN2", target_bir_lowering=False)

    dx = nc.dram_tensor("x", [TPC, D], F32, kind="ExternalInput").ap()
    dwq = nc.dram_tensor("wq", [D, KEY], F32, kind="ExternalInput").ap()
    dwk = nc.dram_tensor("wk", [D, KEY], F32, kind="ExternalInput").ap()
    dwv = nc.dram_tensor("wv", [D, D], F32, kind="ExternalInput").ap()
    dgw1 = nc.dram_tensor("gk_w1", [D, LOW], F32, kind="ExternalInput").ap()
    dgw2 = nc.dram_tensor("gk_w2", [LOW, KEY], F32, kind="ExternalInput").ap()
    dgkb = nc.dram_tensor("gk_b", [KEY], F32, kind="ExternalInput").ap()
    dwg = nc.dram_tensor("wg", [D, D], F32, kind="ExternalInput").ap()
    dwo = nc.dram_tensor("wo", [D, D], F32, kind="ExternalInput").ap()
    dwga = nc.dram_tensor("w_gate_a", [D, HIDA], F32, kind="ExternalInput").ap()
    dwua = nc.dram_tensor("w_up_a", [D, HIDA], F32, kind="ExternalInput").ap()
    dwda = nc.dram_tensor("w_down_a", [HIDA, D], F32, kind="ExternalInput").ap()
    dwgq = nc.dram_tensor("wgT", [D, HIDQ], F32, kind="ExternalInput").ap()
    dwuq = nc.dram_tensor("wuT", [D, HIDQ], F32, kind="ExternalInput").ap()
    dwdq = nc.dram_tensor("wdT", [HIDQ, D], F32, kind="ExternalInput").ap()
    dwsl = nc.dram_tensor("w_scale_slice", [12, 128, 4096], F32, kind="ExternalInput").ap()
    dn1 = nc.dram_tensor("n1w_row", [1, D], F32, kind="ExternalInput").ap()
    dffw = nc.dram_tensor("ffnw_row", [1, D], F32, kind="ExternalInput").ap()
    dgnw = nc.dram_tensor("gnw_row", [1, D], F32, kind="ExternalInput").ap()
    dident = nc.dram_tensor("ident_in", [128, 128], F32, kind="ExternalInput").ap()
    dmaskt = nc.dram_tensor("maskT_in", [64, 64], F32, kind="ExternalInput").ap()
    dgsel = nc.dram_tensor("gsel_row", [1, 16], F32, kind="ExternalInput").ap()
    dseg = nc.dram_tensor("seg_row", [1, TPC], F32, kind="ExternalInput").ap()

    dout = nc.dram_tensor("out", [TPC, D], F32, kind="ExternalOutput").ap()

    d_t = nc.dram_tensor("t_buf", [TPC, HIDQ], F32)
    d_y = nc.dram_tensor("y_buf", [TPC, D], F32)
    d_sc = nc.dram_tensor("sc_bounce", [1, 8], F32)
    ag_sin = nc.dram_tensor("ag_sin", [1, 4], F32)
    ag_sout = nc.dram_tensor("ag_sout", [1, 4], F32)
    ag_in = nc.dram_tensor("ag_in", [128, 2056], F32)
    ag_out = nc.dram_tensor("ag_out", [128 * NCORE, 2056], F32)

    taps = {}
    tap_shapes = {"xn": [TPC, D], "qffn": [TPC, D], "h": [TPC, D],
                  "sp": [KEY, TPC], "qe": [KEY, TPC], "ke": [KEY, TPC],
                  "o_loc": [TPC, D], "S": [HK, H * HV], "x1": [TPC, D],
                  "ofin": [TPC, D], "tq": [TPC, HIDQ], "ocor": [TPC, D],
                  "cbof": [128, KT * TPC], "cbdump": [128, D],
                  "rsqdump": [128, 64], "sqdump": [128, D],
                  "statdump": [128, 144], "thrdump": [128, 8],
                  "xqdump": [128, KT * TPC]}
    for name in debug_taps:
        taps[name] = nc.dram_tensor("tap_" + name, tap_shapes[name], F32,
                                    kind="ExternalOutput").ap()

    ctx = ExitStack()
    E = ctx.enter_context
    sems = {s: E(nc.semaphore(s)) for s in ("T", "V", "A", "G", "ds", "dg", "dt", "dt2", "dw", "dw2", "cc", "q0", "q1", "q2", "q3")}
    p = Prog(nc, sems)

    ident = E(nc.sbuf_tensor("ident", [128, 128], BF16))
    maskt = E(nc.sbuf_tensor("maskt", [64, 64], BF16))
    STAT = E(nc.sbuf_tensor("STAT", [128, 144], F32))
    THR = E(nc.sbuf_tensor("THR", [128, 8], F32))
    CK = E(nc.sbuf_tensor("CK", [128, 8], F32))
    DEC = E(nc.sbuf_tensor("DEC", [128, H * NCH], F32))
    PSH = E(nc.sbuf_tensor("PSH", [128, H * NCH], F32))
    PIN = E(nc.sbuf_tensor("PIN", [128, H * NCH], F32))
    GKB = E(nc.sbuf_tensor("GKB", [128, H], F32))
    NGKB = E(nc.sbuf_tensor("NGKB", [128, H], F32))
    GSEL = E(nc.sbuf_tensor("GSEL", [128, 16], F32))
    SEGM = E(nc.sbuf_tensor("SEGM", [128, TPC], BF16))
    ONES16 = E(nc.sbuf_tensor("ONES16", [128, 16], F32))
    PE8 = E(nc.sbuf_tensor("PE8", [128, 8], F32))
    PJ8 = E(nc.sbuf_tensor("PJ8", [128, 8], F32))
    PJM = E(nc.sbuf_tensor("PJM", [128, 8], F32))
    RSQ = E(nc.sbuf_tensor("RSQ", [128, 64], F32))
    SC1 = E(nc.sbuf_tensor("SC1", [1, 8], F32))
    SAM = E(nc.sbuf_tensor("SAM", [128, 64], BF16))
    SKDT = E(nc.sbuf_tensor("SKDT", [128, 64], BF16))
    SKD = E(nc.sbuf_tensor("SKD", [128, 128], BF16))
    Ub = E(nc.sbuf_tensor("Ub", [128, 512], BF16))
    Vb = E(nc.sbuf_tensor("Vb", [128, 512], BF16))

    M0 = 24544
    A = nc.alloc_sbuf_tensor_at
    Bb = A("Bb", [128, KT * TPC], BF16, offset=M0)
    Cb = A("Cb", [128, KT * TPC], BF16, offset=M0 + 32768)
    A8 = A("A8", [128, NT * D], F32, offset=M0 + 65536)
    Db = A("Db", [128, NT * D], BF16, offset=M0 + 65536)
    Z1W = A("Z1W", [16, 2048], BF16, offset=M0 + 65536 + 4096)
    XQB = A("XQB", [128, D], BF16, offset=M0 + 65536 + 8192)
    OSC = A("OSC", [128, 256], BF16, offset=M0 + 65536 + 12288)
    Eb = A("Eb", [128, H * TPC], BF16, offset=M0 + 98304)
    GTS = A("GTS", [128, NT * 512], F32, offset=M0 + 98304)
    EK = A("EK", [128, 1024], F32, offset=M0 + 114688)
    S_bf = A("S_bf", [128, H * HV], BF16, offset=M0 + 118784)
    S = A("S", [128, H * HV], F32, offset=M0 + 122880)
    RAW = A("RAW", [128, 2048], F32, offset=M0 + 131072)
    Fb = A("Fb", [128, H * TPC], BF16, offset=M0 + 131072)
    Gb = A("Gb", [128, H * TPC], BF16, offset=M0 + 147456)
    WS = A("WS", [128, KT * 512], BF16, offset=M0 + 163840)
    X2 = A("X2", [128, D], F32, offset=M0 + 180224)
    I32S = A("I32S", [128, D], I32, offset=M0 + 188416)
    CB = A("CB", [128, D], F32, offset=M0 + 188416)
    OUTS = A("OUTS", [128, D], F32, offset=M0 + 188416)
    Hs = A("Hs", [128, 2048], F32, offset=M0 + 196608)

    def st(i, sl):
        return STAT[:, i * 16 + sl: i * 16 + sl + 1]

    def BbT(kt, i):
        return Bb[:, kt * TPC + i * 128: kt * TPC + (i + 1) * 128]

    def CbT(kt, i):
        return Cb[:, kt * TPC + i * 128: kt * TPC + (i + 1) * 128]

    out_evs = []

    def dump_tiles(tap, srcfn, n=NT, evs=()):
        for i in range(n):
            eve = srcfn(i)
            out_evs.append(p.dma("gpsimd", taps[tap][i * 128:(i + 1) * 128, :],
                                 Hs[:, :], waits=(eve,) + tuple(evs)))
            p.barrier()

    p.dma("gpsimd", ident[:, :], dident)
    p.dma("gpsimd", maskt[:, :], dmaskt)
    ev_gsel = p.dma("gpsimd", GSEL[:, :], dgsel.broadcast_to([128, 16]))
    ev_seg = p.dma("gpsimd", SEGM[:, :], dseg.broadcast_to([128, TPC]))
    ev_gkb = p.dma("gpsimd", GKB[:, :], dgkb.rearrange("(h q) -> q h", h=H),
                   slow=True)
    ev_ck = p.emit("vector", lambda h: h.memset(CK[:, 0:3], 0.0), inc="V")
    ev_ck = p.emit("vector", lambda h: h.memset(CK[:, 0:1], 1.0), inc="V")
    ev_ckm = p.emit("vector", lambda h: h.memset(CK[:, 1:2], -1.0), inc="V")
    ev_o16 = p.emit("vector", lambda h: h.memset(ONES16[:, :], 1.0), inc="V")
    ev_ngkb = p.emit("vector", lambda h: h.tensor_scalar_mul(
        out=NGKB[:, :], in0=GKB[:, :], scalar1=-1.0), inc="V", waits=(ev_gkb,))

    ev_pacc0 = p.emit("vector", lambda h: h.memset(STAT[:, 128:133], 0.0), inc="V")
    last_red = ev_pacc0
    last_raw_user = [None, None]
    for sidx in range(12):
        wcol = sidx // 4
        for qc in range(4):
            slot = (sidx * 4 + qc) % 2
            evd = p.dma("sync", RAW[:, slot * 1024: slot * 1024 + 1024],
                        dwsl[sidx, :, qc * 1024:(qc + 1) * 1024],
                        waits=(last_raw_user[slot],), sem=f"q{slot}")
            evr = p.emit("vector", lambda h, sl=slot: h.reduce_sum(
                out=STAT[:, 132:133], in_=RAW[:, sl * 1024: sl * 1024 + 1024],
                axis=AX, apply_absolute_value=True), inc="V",
                waits=(evd, last_red))
            last_raw_user[slot] = evr
            last_red = p.emit("vector", lambda h, w=wcol: h.tensor_add(
                out=STAT[:, 128 + w:129 + w], in0=STAT[:, 128 + w:129 + w],
                in1=STAT[:, 132:133]), inc="V", waits=(evr,))
    with nc.psum_tensor("PS1", [1, 4], F32) as PS1:
        ev_mm1 = p.emit("tensor", lambda h: h.matmul(
            PS1[:, :], CK[:, 0:1], STAT[:, 128:132], start=True, stop=True),
            inc="T", waits=(last_red, ev_ck))
        ev_sc1 = p.emit("vector", lambda h: h.tensor_copy(
            out=SC1[0:1, 0:4], in_=PS1[:, :]), inc="V", waits=(ev_mm1,))
    ev_scd = p.dma("gpsimd", ag_sin.ap(), SC1[0:1, 0:4], waits=(ev_sc1,))
    ev_ar = p.emit("gpsimd", lambda h: h.collective_compute(
        "AllReduce", ALU.add, replica_groups=[list(range(NCORE))],
        ins=[ag_sin.ap().opt()], outs=[ag_sout.ap().opt()]),
        inc="cc", waits=(ev_scd,))
    ev_scl = p.dma("gpsimd", SC1[0:1, 0:4], ag_sout.ap(), waits=(ev_ar,))
    ev_ws = p.emit("vector", lambda h: h.tensor_scalar(
        out=SC1[0:1, 4:7], in0=SC1[0:1, 0:3], scalar1=1.0 / (D * HIDQ),
        scalar2=1e-5, op0=ALU.mult, op1=ALU.add), inc="V", waits=(ev_scl,))
    ev_inv = p.emit("vector", lambda h: h.reciprocal(
        out=SC1[0:1, 0:3], in_=SC1[0:1, 4:7]), inc="V", waits=(ev_ws,))
    ev_scb = p.dma("gpsimd", d_sc.ap(), SC1[0:1, 0:8], waits=(ev_inv,))
    ev_thr = p.dma("gpsimd", THR[:, :], d_sc.ap().broadcast_to([128, 8]),
                   waits=(ev_scb,))

    ev_xq_done = []
    ev_xread = {}
    with nc.psum_tensor("PTR1", [128, 512], BF16) as PTR1v:
        for i in range(NT):
            xa = X2[:, :]
            evx = p.dma("gpsimd", xa, dx[i * 128:(i + 1) * 128, :],
                        waits=(ev_xread.get(i - 1),))
            ev_s1 = p.emit("vector", lambda h, xa=xa, i=i: h.reduce_sum(
                out=st(i, 8), in_=xa, axis=AX), inc="V", waits=(evx,))
            ev_sq = p.emit("scalar", lambda h, xa=xa, i=i: h.activation(
                out=A8[:, 0:D], in_=xa, func=AF.Square,
                accum_out=st(i, 10)), inc="A", waits=(evx,))
            e1 = p.emit("vector", lambda h, i=i: h.tensor_scalar_mul(
                out=st(i, 9), in0=st(i, 8), scalar1=1.0 / D), inc="V",
                waits=(ev_s1,))
            e2 = p.emit("vector", lambda h, i=i: h.tensor_scalar_mul(
                out=st(i, 10), in0=st(i, 10), scalar1=1.0 / D), inc="V",
                waits=(ev_sq,))
            e3 = p.emit("vector", lambda h, i=i: h.tensor_mul(
                out=st(i, 11), in0=st(i, 9), in1=st(i, 9)), inc="V", waits=(e1,))
            e4 = p.emit("vector", lambda h, i=i: h.tensor_sub(
                out=st(i, 11), in0=st(i, 10), in1=st(i, 11)), inc="V",
                waits=(e2, e3))
            e5 = p.emit("vector", lambda h, i=i: h.tensor_scalar_add(
                out=st(i, 11), in0=st(i, 11), scalar1=1e-5), inc="V", waits=(e4,))
            e6 = p.emit("vector", lambda h, i=i: h.reciprocal(
                out=st(i, 12), in_=st(i, 11)), inc="V", waits=(e5,))
            ev_r = p.emit("scalar", lambda h, i=i: h.activation(
                out=st(i, SL_R), in_=st(i, 12), func=AF.Sqrt), inc="A",
                waits=(e6,))
            ev_mr = p.emit("vector", lambda h, i=i: h.tensor_mul(
                out=st(i, SL_MR), in0=st(i, 9), in1=st(i, SL_R)), inc="V",
                waits=(ev_r,))
            e7 = p.emit("vector", lambda h, i=i: h.tensor_mul(
                out=st(i, 13), in0=st(i, SL_R), in1=st(i, SL_R)), inc="V",
                waits=(ev_r,))
            e8 = p.emit("vector", lambda h, i=i: h.tensor_scalar(
                out=st(i, 13), in0=st(i, 13), scalar1=-1e-5,
                scalar2=1.0 + 1e-6, op0=ALU.mult, op1=ALU.add), inc="V",
                waits=(e7,))
            e9 = p.emit("vector", lambda h, i=i: h.reciprocal(
                out=st(i, 14), in_=st(i, 13)), inc="V", waits=(e8,))
            ev_r2 = p.emit("scalar", lambda h, i=i: h.activation(
                out=st(i, 15), in_=st(i, 14), func=AF.Sqrt), inc="A", waits=(e9,))
            p.emit("vector", lambda h, i=i: h.tensor_mul(
                out=st(i, SL_RR2), in0=st(i, SL_R), in1=st(i, 15)), inc="V",
                waits=(ev_r2,))
            p.emit("vector", lambda h, i=i: h.tensor_mul(
                out=st(i, SL_MRR2), in0=st(i, SL_MR), in1=st(i, 15)), inc="V",
                waits=(ev_r2, ev_mr))
            ev_xn = p.emit("vector", lambda h, xa=xa, i=i: h.tensor_scalar(
                out=Hs[:, 0:D], in0=xa, scalar1=st(i, SL_R),
                scalar2=st(i, SL_MR), op0=ALU.mult, op1=ALU.subtract), inc="V",
                waits=(ev_mr,))
            ev_xread[i] = ev_xn
            ev_mx = p.emit("vector", lambda h, i=i: h.reduce_max(
                out=st(i, 12), in_=Hs[:, 0:D], axis=AX,
                apply_absolute_value=True), inc="V", waits=(ev_xn,))
            ev_sqv = p.emit("vector", lambda h, i=i: h.tensor_scalar(
                out=st(i, SL_SQ), in0=st(i, 12), scalar1=1.0 / 127.0,
                scalar2=1e-5, op0=ALU.mult, op1=ALU.add), inc="V", waits=(ev_mx,))
            ev_rq = p.emit("vector", lambda h, i=i: h.reciprocal(
                out=st(i, SL_RCPQ), in_=st(i, SL_SQ)), inc="V", waits=(ev_sqv,))
            ev_i32 = p.emit("vector", lambda h, i=i: h.tensor_scalar_mul(
                out=I32S[:, :], in0=Hs[:, 0:D],
                scalar1=st(i, SL_RCPQ)), inc="V", waits=(ev_rq,))
            ev_xqb = p.emit("vector", lambda h: h.tensor_copy(
                out=XQB[:, :], in_=I32S[:, :]), inc="V",
                waits=(ev_i32, ev_xq_done[-1] if ev_xq_done else None))
            for g in range(4):
                evtr = None
                for j in range(4):
                    kt = g * 4 + j
                    evtr = p.emit("tensor", lambda h, kt=kt, j=j: h.transpose(
                        PTR1v[:, j * 128:(j + 1) * 128],
                        XQB[:, kt * 128:(kt + 1) * 128], ident[:, :]),
                        inc="T", waits=(ev_xqb, ev_sc1,
                                        ev_xq_done[-1] if ev_xq_done else None))
                dst = Bb[:].rearrange("q (kt t) -> q kt t", kt=KT)[
                    :, g * 4:(g + 1) * 4, i * 128:(i + 1) * 128]
                eve = p.emit("scalar", lambda h, dst=dst: h.copy(
                    out=dst, in_=PTR1v[:].rearrange("q (a b) -> q a b", a=4)),
                    inc="A", waits=(evtr,))
                ev_xq_done.append(eve)
    ev_xqT = ev_xq_done[-1]

    if "xqdump" in taps:
        out_evs.append(p.dma("gpsimd", taps["xqdump"], Bb[:, :],
                             waits=(ev_xqT,)))
        p.barrier()
    if "xn" in taps:
        def mk_xn(i):
            xa = X2[:, :]
            evx = p.dma("gpsimd", xa, dx[i * 128:(i + 1) * 128, :],
                        waits=(ev_xqT,))
            return p.emit("vector", lambda h, xa=xa, i=i: h.tensor_scalar(
                out=Hs[:, :], in0=xa, scalar1=st(i, SL_R), scalar2=st(i, SL_MR),
                op0=ALU.mult, op1=ALU.subtract), inc="V", waits=(evx,))
        dump_tiles("xn", mk_xn)

    p.barrier()

    for i in range(NT):
        p.emit("vector", lambda h, i=i: h.tensor_mul(
            out=st(i, SL_XWG), in0=st(i, SL_SQ), in1=THR[:, 4:5]), inc="V",
            waits=(ev_thr,))
        p.emit("vector", lambda h, i=i: h.tensor_mul(
            out=st(i, SL_XWU), in0=st(i, SL_SQ), in1=THR[:, 5:6]), inc="V",
            waits=(ev_thr,))
        p.emit("vector", lambda h, i=i: h.memset(st(i, SL_MXT), 0.0), inc="V")
    ev_xw = ("V", p.cnt["V"])

    with ExitStack() as pctx:
        PW2v = [pctx.enter_context(nc.psum_tensor(f"PW2{i}", [128, 512], F32))
              for i in range(8)]
        ev_mm = [None] * 8
        ev_free = [None] * 8
        ev_gate = [None] * 8
        ev_gb_users = None
        ev_tdma_prev = [None, None]
        raw_guard = {}
        for w in range(16):
            for wt in range(2):
                dsrc = dwgq if wt == 0 else dwuq
                thr_inv = THR[:, 0:1] if wt == 0 else THR[:, 1:2]
                ev_q = []
                for kt in range(KT):
                    slot = kt % 4
                    evd = p.dma("sync", RAW[:, slot * 512:(slot + 1) * 512],
                                dsrc[kt * 128:(kt + 1) * 128,
                                     w * 512:(w + 1) * 512],
                                waits=(raw_guard.get(slot),), sem=f"q{slot}")
                    evu = p.emit("vector", lambda h, sl=slot, ti=thr_inv:
                                 h.tensor_scalar(
                        out=Ub[:, :],
                        in0=RAW[:, sl * 512:(sl + 1) * 512], scalar1=ti,
                        scalar2=0.5, op0=ALU.mult, op1=ALU.is_ge), inc="V",
                        waits=(evd, ev_thr))
                    evv = p.emit("vector", lambda h, sl=slot, ti=thr_inv:
                                 h.tensor_scalar(
                        out=Vb[:, :],
                        in0=RAW[:, sl * 512:(sl + 1) * 512], scalar1=ti,
                        scalar2=-0.5, op0=ALU.mult, op1=ALU.is_le), inc="V",
                        waits=(evd,))
                    raw_guard[slot] = evv
                    evw = p.emit("vector", lambda h, kt=kt: h.tensor_sub(
                        out=Gb[:, kt * 512:(kt + 1) * 512],
                        in0=Ub[:, :], in1=Vb[:, :]), inc="V",
                        waits=(evu, evv, ev_gb_users if kt == 0 else None))
                    ev_q.append((evw, evv))
                for i in range(NT):
                    evm = None
                    for kt in range(KT):
                        evm = p.emit("tensor", lambda h, i=i, kt=kt: h.matmul(
                            PW2v[i][:, :], BbT(kt, i),
                            Gb[:, kt * 512:(kt + 1) * 512],
                            start=(kt == 0), stop=(kt == KT - 1)),
                            inc="T" if kt == KT - 1 else None,
                            waits=(ev_q[kt][0], ev_xqT,
                                   ev_free[i] if kt == 0 else None))
                    ev_mm[i] = evm
                ev_gb_users = ev_mm[7]
                for i in range(NT):
                    if wt == 0:
                        ev_gate[i] = p.emit("scalar", lambda h, i=i: h.activation(
                            out=GTS[:, i * 512:(i + 1) * 512], in_=PW2v[i][:, :],
                            func=AF.Silu, scale=st(i, SL_XWG), bias=CK[:, 2:3]),
                            inc="A", waits=(ev_mm[i], ev_xw, ev_ck))
                        ev_free[i] = ev_gate[i]
                    else:
                        tslot = i % 2
                        evt = p.emit("vector", lambda h, i=i, tslot=tslot:
                                     h.scalar_tensor_tensor(
                            out=Hs[:, tslot * 512:(tslot + 1) * 512],
                            in0=PW2v[i][:, :], scalar=st(i, SL_XWU),
                            in1=GTS[:, i * 512:(i + 1) * 512],
                            op0=ALU.mult, op1=ALU.mult), inc="V",
                            waits=(ev_mm[i], ev_gate[i], ev_xw,
                                   ev_tdma_prev[tslot]))
                        evr = p.emit("vector", lambda h, i=i, tslot=tslot:
                                     h.reduce_max(
                            out=st(i, SL_T1),
                            in_=Hs[:, tslot * 512:(tslot + 1) * 512], axis=AX,
                            apply_absolute_value=True), inc="V", waits=(evt,))
                        p.emit("vector", lambda h, i=i: h.tensor_max(
                            out=st(i, SL_MXT), in0=st(i, SL_MXT),
                            in1=st(i, SL_T1)), inc="V", waits=(evr,))
                        evdma = p.dma("gpsimd",
                                      d_t.ap()[i * 128:(i + 1) * 128,
                                               w * 512:(w + 1) * 512],
                                      Hs[:, tslot * 512:(tslot + 1) * 512],
                                      waits=(evt,),
                                      sem="dt" if tslot == 0 else "dt2")
                        ev_tdma_prev[tslot] = evdma
                        ev_free[i] = evt
    if "tq" in taps:
        out_evs.append(p.dma("gpsimd", taps["tq"], d_t.ap(),
                             waits=(("dg", p.cnt["dg"]),)))
        p.barrier()
    p.barrier()

    for i in range(NT):
        e1 = p.emit("vector", lambda h, i=i: h.tensor_scalar(
            out=st(i, SL_T0), in0=st(i, SL_MXT), scalar1=1.0 / 127.0,
            scalar2=1e-5, op0=ALU.mult, op1=ALU.add), inc="V")
        p.emit("vector", lambda h, i=i: h.reciprocal(
            out=st(i, SL_RCPT), in_=st(i, SL_T0)), inc="V", waits=(e1,))
        p.emit("vector", lambda h, i=i: h.tensor_mul(
            out=st(i, SL_TSWS), in0=st(i, SL_T0), in1=THR[:, 6:7]), inc="V",
            waits=(e1, ev_thr))
    ev_t3 = ("V", p.cnt["V"])
    if "thrdump" in taps:
        out_evs.append(p.dma("gpsimd", taps["thrdump"], THR[:, :],
                             waits=(ev_thr,)))
        p.barrier()
    if "statdump" in taps:
        out_evs.append(p.dma("gpsimd", taps["statdump"], STAT[:, :],
                             waits=(ev_t3,)))
        p.barrier()

    with ExitStack() as pctx:
        PTR3v = pctx.enter_context(nc.psum_tensor("PTR3", [128, 512], BF16))
        PW3v = [pctx.enter_context(nc.psum_tensor(f"PW3{i}", [128, 512], F32))
              for i in range(4)]
        ev_add = {}
        ev_tq_prev = None
        for q in range(4):
            ev_tq = []
            tload_user = [ev_tq_prev, ev_tq_prev]
            for i in range(NT):
                tbuf = Hs if i % 2 == 0 else X2
                evd = p.dma("gpsimd", tbuf[:, 0:D],
                            d_t.ap()[i * 128:(i + 1) * 128,
                                     q * 2048:(q + 1) * 2048],
                            waits=(tload_user[i % 2],),
                            sem="dt" if i % 2 == 0 else "dt2")
                evi = p.emit("vector", lambda h, i=i, tbuf=tbuf: h.tensor_scalar_mul(
                    out=I32S[:, :], in0=tbuf[:, 0:D],
                    scalar1=st(i, SL_RCPT)), inc="V",
                    waits=(evd, ev_t3, ev_tq[-1] if ev_tq else ev_tq_prev))
                tload_user[i % 2] = evi
                evb = p.emit("vector", lambda h: h.tensor_copy(
                    out=WS[:, 0:2048], in_=I32S[:, :]), inc="V", waits=(evi,))
                for g in range(4):
                    evtr = None
                    for j in range(4):
                        kt = g * 4 + j
                        evtr = p.emit("tensor", lambda h, kt=kt, j=j: h.transpose(
                            PTR3v[:, j * 128:(j + 1) * 128],
                            WS[:, kt * 128:(kt + 1) * 128], ident[:, :]),
                            inc="T", waits=(evb,
                                            ev_tq[-1] if ev_tq else ev_tq_prev))
                    dst = Cb[:].rearrange("q (kt t) -> q kt t", kt=KT)[
                        :, g * 4:(g + 1) * 4, i * 128:(i + 1) * 128]
                    eve = p.emit("scalar", lambda h, dst=dst: h.copy(
                        out=dst, in_=PTR3v[:].rearrange("q (a b) -> q a b", a=4)),
                        inc="A", waits=(evtr,))
                    ev_tq.append(eve)
            ev_tqT = ev_tq[-1]
            ev_tq_prev = ev_tqT
            ev_gb3 = None
            for sl in range(4):
                ev_q = []
                raw_guard3 = getattr(p, "_rg3", {})
                for kt in range(KT):
                    slot = kt % 4
                    row0 = q * 2048 + kt * 128
                    evd = p.dma("sync", RAW[:, slot * 512:(slot + 1) * 512],
                                dwdq[row0:row0 + 128, sl * 512:(sl + 1) * 512],
                                waits=(raw_guard3.get(slot),), sem=f"q{slot}")
                    evu = p.emit("vector", lambda h, sl_=slot: h.tensor_scalar(
                        out=Ub[:, :],
                        in0=RAW[:, sl_ * 512:(sl_ + 1) * 512],
                        scalar1=THR[:, 2:3], scalar2=0.5,
                        op0=ALU.mult, op1=ALU.is_ge), inc="V", waits=(evd,))
                    evv = p.emit("vector", lambda h, sl_=slot: h.tensor_scalar(
                        out=Vb[:, :],
                        in0=RAW[:, sl_ * 512:(sl_ + 1) * 512],
                        scalar1=THR[:, 2:3], scalar2=-0.5,
                        op0=ALU.mult, op1=ALU.is_le), inc="V", waits=(evd,))
                    raw_guard3[slot] = evv
                    p._rg3 = raw_guard3
                    evw = p.emit("vector", lambda h, kt=kt: h.tensor_sub(
                        out=Gb[:, kt * 512:(kt + 1) * 512],
                        in0=Ub[:, :], in1=Vb[:, :]), inc="V",
                        waits=(evu, evv, ev_gb3 if kt == 0 else None))
                    ev_q.append((evw, evv))
                for i in range(NT):
                    bank = i % 4
                    evm = None
                    for kt in range(KT):
                        evm = p.emit("tensor", lambda h, bank=bank, kt=kt, i=i:
                                     h.matmul(
                            PW3v[bank][:, :], CbT(kt, i),
                            Gb[:, kt * 512:(kt + 1) * 512],
                            start=(kt == 0), stop=(kt == KT - 1)),
                            inc="T" if kt == KT - 1 else None,
                            waits=(ev_q[kt][0], ev_tqT,
                                   ev_add.get(bank) if kt == 0 else None))
                    dst = A8[:, i * D + sl * 512: i * D + (sl + 1) * 512]
                    if q == 0:
                        eva = p.emit("vector", lambda h, dst=dst, bank=bank:
                                     h.tensor_copy(out=dst, in_=PW3v[bank][:, :]),
                                     inc="V", waits=(evm,))
                    else:
                        eva = p.emit("vector", lambda h, dst=dst, bank=bank:
                                     h.tensor_add(out=dst, in0=dst,
                                                  in1=PW3v[bank][:, :]),
                                     inc="V", waits=(evm,))
                    ev_add[bank] = eva
                    if i == NT - 1:
                        ev_gb3 = evm
    ev_y = ("V", p.cnt["V"])
    for i in range(NT):
        p.dma("gpsimd", d_y.ap()[i * 128:(i + 1) * 128, :],
              A8[:, i * D:(i + 1) * D], waits=(ev_y,))
    if "qffn" in taps:
        def mk_qf(i):
            return p.emit("vector", lambda h, i=i: h.tensor_scalar_mul(
                out=Hs[:, :], in0=A8[:, i * D:(i + 1) * D],
                scalar1=st(i, SL_TSWS)), inc="V", waits=(ev_y,))
        dump_tiles("qffn", mk_qf)
    p.barrier()

    ev_cb = p.dma("gpsimd", CB[:, :], dn1.broadcast_to([128, D]))
    ev_gw1 = p.dma("gpsimd",
                   OSC[:, 0:KT * LOW].rearrange("q (kt l) -> q kt l", kt=KT),
                   dgw1.rearrange("(kt q) l -> q kt l", q=128), sem="dw", gate=True)
    ev_gw2 = p.dma("gpsimd", Z1W[0:16, 1024:1024 + KEY], dgw2, sem="dw", gate=True)

    ev_hT = []
    ev_xr = {}
    with nc.psum_tensor("PTR4", [128, 512], BF16) as PTR4v:
        for i in range(NT):
            xa = X2[:, :]
            evx = p.dma("gpsimd", xa, dx[i * 128:(i + 1) * 128, :],
                        waits=(ev_xr.get(i - 1),))
            evt1 = p.emit("vector", lambda h, xa=xa, i=i: h.tensor_scalar(
                out=Hs[:, 0:D], in0=xa, scalar1=st(i, SL_RR2),
                scalar2=st(i, SL_MRR2), op0=ALU.mult, op1=ALU.subtract), inc="V",
                waits=(evx,))
            ev_xr[i] = evt1
            evt2 = p.emit("vector", lambda h: h.tensor_mul(
                out=XQB[:, :], in0=Hs[:, 0:D], in1=CB[:, :]), inc="V",
                waits=(evt1, ev_cb, ev_hT[-1] if ev_hT else None))
            for g in range(4):
                evtr = None
                for j in range(4):
                    kt = g * 4 + j
                    evtr = p.emit("tensor", lambda h, kt=kt, j=j: h.transpose(
                        PTR4v[:, j * 128:(j + 1) * 128],
                        XQB[:, kt * 128:(kt + 1) * 128], ident[:, :]),
                        inc="T", waits=(evt2, ev_hT[-1] if ev_hT else None))
                dst = Bb[:].rearrange("q (kt t) -> q kt t", kt=KT)[
                    :, g * 4:(g + 1) * 4, i * 128:(i + 1) * 128]
                eve = p.emit("scalar", lambda h, dst=dst: h.copy(
                    out=dst, in_=PTR4v[:].rearrange("q (a b) -> q a b", a=4)),
                    inc="A", waits=(evtr,))
                ev_hT.append(eve)
    ev_hTd = ev_hT[-1]
    p.barrier()

    ev_S0 = p.emit("vector", lambda h: h.memset(S[:, :], 0.0), inc="V")
    ev_Sbf0 = p.emit("vector", lambda h: h.memset(S_bf[:, :], 0.0), inc="V")
    ev_qe = {}
    ev_ke = {}
    ev_psh = {}
    with ExitStack() as pctx:
        PZ1 = pctx.enter_context(nc.psum_tensor("PZ1", [16, 1024], F32))
        PZ2 = pctx.enter_context(nc.psum_tensor("PZ2", [128, 1024], F32))
        PQ = pctx.enter_context(nc.psum_tensor("PQ", [128, 1024], F32))
        PK = pctx.enter_context(nc.psum_tensor("PK", [128, 1024], F32))
        ev_z1 = None
        for half in range(2):
            for kt in range(KT):
                ev_z1 = p.emit("tensor", lambda h, kt=kt, half=half: h.matmul(
                    PZ1[:, half * 512:(half + 1) * 512],
                    OSC[:, kt * LOW:(kt + 1) * LOW],
                    Bb[:, kt * TPC + half * 512: kt * TPC + (half + 1) * 512],
                    start=(kt == 0), stop=(kt == KT - 1)),
                    inc="T" if kt == KT - 1 else None,
                    waits=(ev_gw1, ev_hTd))
        ev_z1sb = p.emit("vector", lambda h: h.tensor_copy(
            out=Z1W[0:16, 0:1024], in_=PZ1[:, :]), inc="V", waits=(ev_z1,))

        ev_wl = {}
        for h_ in range(H):
            sl = h_ % 2
            wq_dst = WS[:, sl * 2048:(sl + 1) * 2048]
            wk_dst = WS[:, 4096 + sl * 2048: 4096 + (sl + 1) * 2048]
            ev_wl[("q", h_)] = p.dma(
                "gpsimd", wq_dst.rearrange("q (kt m) -> q kt m", kt=KT),
                dwq.rearrange("(kt q) m -> q kt m", q=128)[
                    :, :, h_ * 128:(h_ + 1) * 128],
                waits=(ev_qe.get(h_ - 2),), sem="dw")
            ev_wl[("k", h_)] = p.dma(
                "gpsimd", wk_dst.rearrange("q (kt m) -> q kt m", kt=KT),
                dwk.rearrange("(kt q) m -> q kt m", q=128)[
                    :, :, h_ * 128:(h_ + 1) * 128],
                waits=(ev_ke.get(h_ - 2),), sem="dw")

            ev_z2 = None
            for half in range(2):
                ev_z2 = p.emit("tensor", lambda h, h_=h_, half=half: h.matmul(
                    PZ2[:, half * 512:(half + 1) * 512],
                    Z1W[0:16, 1024 + h_ * 128:1024 + (h_ + 1) * 128],
                    Z1W[0:16, half * 512:(half + 1) * 512],
                    start=True, stop=True),
                    inc="T" if half == 1 else None,
                    waits=(ev_gw2, ev_z1sb,
                           ev_psh.get(h_ - 1)))
            e1 = p.emit("scalar", lambda h, h_=h_: h.activation(
                out=Hs[:, 0:1024], in_=PZ2[:, :], func=AF.Abs,
                scale=CK[:, 0:1], bias=GKB[:, h_:h_ + 1]), inc="A",
                waits=(ev_z2, ev_gkb, ev_ck))
            e2 = p.emit("scalar", lambda h: h.activation(
                out=Hs[:, 1024:2048], in_=Hs[:, 0:1024], func=AF.Exp,
                scale=-1.0), inc="A", waits=(e1,))
            e3 = p.emit("scalar", lambda h: h.activation(
                out=Hs[:, 0:1024], in_=Hs[:, 1024:2048], func=AF.Ln,
                scale=CK[:, 0:1], bias=CK[:, 0:1]), inc="A", waits=(e2,))
            e4 = p.emit("scalar", lambda h, h_=h_: h.activation(
                out=Hs[:, 1024:2048], in_=PZ2[:, :], func=AF.Relu,
                scale=CK[:, 1:2], bias=NGKB[:, h_:h_ + 1]), inc="A",
                waits=(e3, ev_ngkb, ev_ckm))
            e5 = p.emit("vector", lambda h: h.tensor_add(
                out=Hs[:, 0:1024], in0=Hs[:, 0:1024], in1=Hs[:, 1024:2048]),
                inc="V", waits=(e4,))
            evsp_dump = None
            if "sp" in taps:
                evsp_dump = p.dma("gpsimd", taps["sp"][h_ * 128:(h_ + 1) * 128, :],
                                  Hs[:, 0:1024], waits=(e5,))
                out_evs.append(evsp_dump)
            e6 = p.emit("vector", lambda h: h.tensor_tensor_scan(
                out=Hs[:, 1024:2048], data0=SEGM[:, :], data1=Hs[:, 0:1024],
                initial=0.0, op0=ALU.mult, op1=ALU.add), inc="V",
                waits=(e5, ev_seg))
            e7 = p.emit("scalar", lambda h: h.activation(
                out=Hs[:, 0:1024], in_=Hs[:, 1024:2048], func=AF.Exp,
                scale=-IGN), inc="A", waits=(e6, evsp_dump))
            e8 = p.emit("scalar", lambda h: h.activation(
                out=EK[:, :], in_=Hs[:, 1024:2048], func=AF.Exp,
                scale=IGN), inc="A", waits=(e6,))
            e9 = p.emit("vector", lambda h, h_=h_: h.tensor_copy(
                out=DEC[:, h_ * NCH:(h_ + 1) * NCH],
                in_=Hs[:, 0:1024].rearrange("q (c j) -> q c j", c=NCH)
                [:, :, 63:64].squeeze(2)), inc="V", waits=(e7,))
            e10 = p.emit("vector", lambda h, h_=h_: h.tensor_tensor_scan(
                out=PIN[:, h_ * NCH:(h_ + 1) * NCH],
                data0=DEC[:, h_ * NCH:(h_ + 1) * NCH], data1=ONES16[:, :],
                initial=1.0, op0=ALU.mult, op1=ALU.mult), inc="V",
                waits=(e9, ev_o16))
            p.emit("vector", lambda h, h_=h_: h.memset(
                PSH[:, h_ * NCH:h_ * NCH + 1], 1.0), inc="V")
            ev_psh[h_] = p.emit("vector", lambda h, h_=h_: h.tensor_copy(
                out=PSH[:, h_ * NCH + 1:(h_ + 1) * NCH],
                in_=PIN[:, h_ * NCH:(h_ + 1) * NCH - 1]), inc="V", waits=(e10,))
            ev_q = None
            ev_k = None
            for half in range(2):
                for kt in range(KT):
                    ev_q = p.emit("tensor", lambda h, sl=sl, kt=kt, half=half:
                                  h.matmul(
                        PQ[:, half * 512:(half + 1) * 512],
                        WS[:, sl * 2048 + kt * 128: sl * 2048 + (kt + 1) * 128],
                        Bb[:, kt * TPC + half * 512: kt * TPC + (half + 1) * 512],
                        start=(kt == 0), stop=(kt == KT - 1)),
                        inc="T" if kt == KT - 1 else None,
                        waits=(ev_wl[("q", h_)], ev_hTd,
                               ev_qe.get(h_ - 1) if kt == 0 else None))
            for half in range(2):
                for kt in range(KT):
                    ev_k = p.emit("tensor", lambda h, sl=sl, kt=kt, half=half:
                                  h.matmul(
                        PK[:, half * 512:(half + 1) * 512],
                        WS[:, 4096 + sl * 2048 + kt * 128:
                           4096 + sl * 2048 + (kt + 1) * 128],
                        Bb[:, kt * TPC + half * 512: kt * TPC + (half + 1) * 512],
                        start=(kt == 0), stop=(kt == KT - 1)),
                        inc="T" if kt == KT - 1 else None,
                        waits=(ev_wl[("k", h_)],
                               ev_ke.get(h_ - 1) if kt == 0 else None))
            ev_qe[h_] = p.emit("vector", lambda h, h_=h_: h.scalar_tensor_tensor(
                out=Eb[:, h_ * TPC:(h_ + 1) * TPC], in0=PQ[:, :], scalar=C1,
                in1=Hs[:, 0:1024], op0=ALU.mult, op1=ALU.mult), inc="V",
                waits=(ev_q, e7, e9))
            ev_ke[h_] = p.emit("vector", lambda h, h_=h_: h.tensor_mul(
                out=Fb[:, h_ * TPC:(h_ + 1) * TPC], in0=PK[:, :],
                in1=EK[:, :]), inc="V", waits=(ev_k, e8))
            for c in range(NCH):
                p.emit("vector", lambda h, h_=h_, c=c: h.tensor_scalar_mul(
                    out=Gb[:, h_ * TPC + c * 64: h_ * TPC + (c + 1) * 64],
                    in0=Eb[:, h_ * TPC + c * 64: h_ * TPC + (c + 1) * 64],
                    scalar1=PSH[:, h_ * NCH + c: h_ * NCH + c + 1]), inc="V",
                    waits=(ev_qe[h_], ev_psh[h_]))
    ev_qt = ("V", p.cnt["V"])
    if "qe" in taps:
        for h_ in range(H):
            eve = p.emit("vector", lambda h, h_=h_: h.tensor_copy(
                out=Hs[:, 0:1024], in_=Eb[:, h_ * TPC:(h_ + 1) * TPC]),
                inc="V", waits=(ev_qe[h_],))
            out_evs.append(p.dma("gpsimd", taps["qe"][h_ * 128:(h_ + 1) * 128, :],
                                 Hs[:, 0:1024], waits=(eve,)))
            p.barrier()
    if "ke" in taps:
        for h_ in range(H):
            eve = p.emit("vector", lambda h, h_=h_: h.tensor_copy(
                out=Hs[:, 0:1024], in_=Fb[:, h_ * TPC:(h_ + 1) * TPC]),
                inc="V", waits=(ev_ke[h_],))
            out_evs.append(p.dma("gpsimd", taps["ke"][h_ * 128:(h_ + 1) * 128, :],
                                 Hs[:, 0:1024], waits=(eve,)))
            p.barrier()
    p.barrier()

    with ExitStack() as pctx:
        PV = [pctx.enter_context(nc.psum_tensor(f"PV{i}", [128, 512], F32))
              for i in range(2)]
        ev_vfree = [None, None]
        ev_wv = None
        for sl in range(4):
            ev_wv_prev = ev_wv
            ev_wv = None
            for kt in range(KT):
                ev_wv = p.dma("gpsimd", WS[:, kt * 512:(kt + 1) * 512],
                              dwv[kt * 128:(kt + 1) * 128,
                                  sl * 512:(sl + 1) * 512],
                              waits=(ev_vfree[0], ev_vfree[1]) if kt == 0 else (),
                              sem="dw", gate=(kt == 0))
            for i in range(NT):
                b = i % 2
                evm = None
                for kt in range(KT):
                    evm = p.emit("tensor", lambda h, b=b, kt=kt, i=i: h.matmul(
                        PV[b][:, :], BbT(kt, i),
                        WS[:, kt * 512:(kt + 1) * 512],
                        start=(kt == 0), stop=(kt == KT - 1)),
                        inc="T" if kt == KT - 1 else None,
                        waits=(ev_wv, ev_vfree[b] if kt == 0 else None))
                eve = p.emit("scalar", lambda h, b=b, i=i, sl=sl: h.copy(
                    out=Cb[:, i * D + sl * 512: i * D + (sl + 1) * 512],
                    in_=PV[b][:, :]), inc="A", waits=(evm,))
                ev_vfree[b] = eve
    ev_v = ("A", p.cnt["A"])
    p.barrier()

    with ExitStack() as pctx:
        pO = pctx.enter_context(nc.psum_tensor("pO", [64, 2048], F32))
        pA = pctx.enter_context(nc.psum_tensor("pA", [64, 64], F32))
        pT = pctx.enter_context(nc.psum_tensor("pT", [64, 128], BF16))
        pS = pctx.enter_context(nc.psum_tensor("pS", [128, 256], F32))
        ev_S_upd = {h_: ev_S0 for h_ in range(H)}
        ev_Sbf = {h_: ev_Sbf0 for h_ in range(H)}
        ev_oe = {}
        ev_Am = None
        ev_kd = None
        ev_pS_user = None
        ev_o2_prev = None
        ev_s1_prev = None
        for c in range(NCH):
            r = (c % 2) * 64
            i = c // 2
            for h_ in range(H):
                qs = Eb[:, h_ * TPC + c * 64: h_ * TPC + (c + 1) * 64]
                ks = Fb[:, h_ * TPC + c * 64: h_ * TPC + (c + 1) * 64]
                vsl = Cb[r:r + 64, i * D + h_ * HV: i * D + (h_ + 1) * HV]
                evA = p.emit("tensor", lambda h, qs=qs, ks=ks: h.matmul(
                    pA[:, :], ks, qs, start=True, stop=True), inc="T",
                    waits=(ev_qe[h_], ev_ke[h_], ev_Am))
                ev_Am = p.emit("vector", lambda h, r=r: h.tensor_mul(
                    out=SAM[r:r + 64, :], in0=pA[:, :], in1=maskt[:, :]),
                    inc="V", waits=(evA, ev_o2_prev))
                evk1 = p.emit("vector", lambda h, ks=ks, h_=h_, c=c: h.tensor_scalar_mul(
                    out=SKDT[:, :], in0=ks,
                    scalar1=DEC[:, h_ * NCH + c: h_ * NCH + c + 1]), inc="V",
                    waits=(ev_ke[h_], ev_kd))
                evk2 = p.emit("tensor", lambda h: h.transpose(
                    pT[:, :], SKDT[:, :], ident[:, :]), inc="T",
                    waits=(evk1, ev_kd))
                evk3 = p.emit("scalar", lambda h, r=r: h.copy(
                    out=SKD[r:r + 64, :], in_=pT[:, :]), inc="A",
                    waits=(evk2, ev_s1_prev))
                ev_kd = evk3
                evo1 = p.emit("tensor", lambda h, qs=qs, h_=h_: h.matmul(
                    pO[:, h_ * HV:(h_ + 1) * HV], qs,
                    S_bf[:, h_ * HV:(h_ + 1) * HV], start=True, stop=False),
                    inc="T",
                    waits=(ev_Sbf[h_], ev_oe.get((c - 1, h_)),
                           ev_oe.get((c - 1, h_ ^ 1)),
                           ev_oe.get((c, h_ ^ 1))))
                evo2 = p.emit("tensor", lambda h, r=r, h_=h_, vsl=vsl: h.matmul(
                    pO[:, h_ * HV:(h_ + 1) * HV], SAM[r:r + 64, :], vsl,
                    start=False, stop=True), inc="T", waits=(ev_Am, ev_v))
                ev_o2_prev = evo2
                evoe = p.emit("scalar", lambda h, r=r, i=i, h_=h_: h.copy(
                    out=Db[r:r + 64, i * D + h_ * HV: i * D + (h_ + 1) * HV],
                    in_=pO[:, h_ * HV:(h_ + 1) * HV]), inc="A", waits=(evo2,))
                ev_oe[(c, h_)] = evoe
                evs1 = p.emit("tensor", lambda h, r=r, vsl=vsl: h.matmul(
                    pS[:, :], SKD[r:r + 64, :], vsl, start=True, stop=True),
                    inc="T", waits=(evk3, ev_pS_user))
                ev_s1_prev = evs1
                ev_S_upd[h_] = p.emit("vector", lambda h, h_=h_, c=c:
                                      h.scalar_tensor_tensor(
                    out=S[:, h_ * HV:(h_ + 1) * HV],
                    in0=S[:, h_ * HV:(h_ + 1) * HV],
                    scalar=DEC[:, h_ * NCH + c: h_ * NCH + c + 1],
                    in1=pS[:, :], op0=ALU.mult, op1=ALU.add), inc="V",
                    waits=(evs1, evo1))
                ev_pS_user = ev_S_upd[h_]
                ev_Sbf[h_] = p.emit("vector", lambda h, h_=h_: h.tensor_copy(
                    out=S_bf[:, h_ * HV:(h_ + 1) * HV],
                    in_=S[:, h_ * HV:(h_ + 1) * HV]), inc="V",
                    waits=(ev_S_upd[h_],))
    ev_scan = ("V", p.cnt["V"])
    if "o_loc" in taps:
        def mk_ol(i):
            return p.emit("vector", lambda h, i=i: h.tensor_copy(
                out=Hs[:, :], in_=Db[:, i * D:(i + 1) * D]), inc="V",
                waits=(ev_scan,))
        dump_tiles("o_loc", mk_ol)
    if "S" in taps:
        eve = p.emit("vector", lambda h: h.tensor_copy(
            out=Hs[:, :], in_=S[:, :]), inc="V", waits=(ev_scan,))
        out_evs.append(p.dma("gpsimd", taps["S"], Hs[:, :], waits=(eve,)))
        p.barrier()
    p.barrier()

    ev_pe = p.emit("vector", lambda h: h.tensor_copy(
        out=PE8[:, :],
        in_=PIN[:].rearrange("q (h c) -> q h c", h=H)[:, :, NCH - 1:NCH]
        .squeeze(2)), inc="V")
    evd1 = p.dma("gpsimd", ag_in.ap()[:, 0:2048], S[:, :], waits=(ev_scan,))
    evd2 = p.dma("gpsimd", ag_in.ap()[:, 2048:2056], PE8[:, :], waits=(ev_pe,))
    ev_ag = p.emit("gpsimd", lambda h: h.collective_compute(
        "AllGather", ALU.bypass, replica_groups=[list(range(NCORE))],
        ins=[ag_in.ap().opt()], outs=[ag_out.ap().opt()]),
        inc="cc", waits=(evd1, evd2))
    ev_Sz = p.emit("vector", lambda h: h.memset(S[:, :], 0.0), inc="V",
                   waits=(evd1,))
    ev_pref = ev_Sz
    sj_user = [None]
    for j in range(NCORE):
        evl1 = p.dma("gpsimd", Hs[:, 0:2048],
                     ag_out.ap()[j * 128:(j + 1) * 128, 0:2048],
                     waits=(ev_ag, sj_user[0]))
        evl2 = p.dma("gpsimd", PJ8[:, :],
                     ag_out.ap()[j * 128:(j + 1) * 128, 2048:2056],
                     waits=(ev_ag, ev_pref))
        evm = p.emit("vector", lambda h, j=j: h.tensor_scalar(
            out=PJM[:, :], in0=PJ8[:, :], scalar1=GSEL[:, j:j + 1],
            scalar2=GSEL[:, 8 + j:9 + j], op0=ALU.mult, op1=ALU.add), inc="V",
            waits=(evl2, ev_gsel))
        evsm = p.emit("vector", lambda h, j=j: h.tensor_scalar_mul(
            out=Hs[:, 0:2048], in0=Hs[:, 0:2048],
            scalar1=GSEL[:, j:j + 1]), inc="V", waits=(evl1,))
        for h_ in range(H):
            ev_pref = p.emit("vector", lambda h, h_=h_: h.scalar_tensor_tensor(
                out=S[:, h_ * HV:(h_ + 1) * HV], in0=S[:, h_ * HV:(h_ + 1) * HV],
                scalar=PJM[:, h_:h_ + 1],
                in1=Hs[:, h_ * HV:(h_ + 1) * HV],
                op0=ALU.mult, op1=ALU.add), inc="V", waits=(evm, evsm, ev_pref))
        sj_user[0] = ev_pref
    ev_sin = ev_pref
    ev_sin = p.emit("vector", lambda h: h.tensor_copy(
        out=S_bf[:, :], in_=S[:, :]), inc="V", waits=(ev_pref,))
    p.barrier()

    with ExitStack() as pctx:
        PC = [pctx.enter_context(nc.psum_tensor(f"PC{i}", [128, 256], F32))
              for i in range(2)]
        ev_cfree = [None, None]
        for h_ in range(H):
            for i in range(NT):
                b = (h_ * NT + i) % 2
                evc = p.emit("tensor", lambda h, h_=h_, i=i, b=b: h.matmul(
                    PC[b][:, :], Gb[:, h_ * TPC + i * 128: h_ * TPC + (i + 1) * 128],
                    S_bf[:, h_ * HV:(h_ + 1) * HV], start=True, stop=True),
                    inc="T", waits=(ev_sin, ev_qt, ev_cfree[b]))
                eva = p.emit("vector", lambda h, h_=h_, i=i, b=b: h.tensor_add(
                    out=Db[:, i * D + h_ * HV: i * D + (h_ + 1) * HV],
                    in0=Db[:, i * D + h_ * HV: i * D + (h_ + 1) * HV],
                    in1=PC[b][:, :]), inc="V", waits=(evc, ev_scan))
                ev_cfree[b] = eva
    ev_corr = ("V", p.cnt["V"])
    if "ocor" in taps:
        def mk_oc(i):
            return p.emit("vector", lambda h, i=i: h.tensor_copy(
                out=Hs[:, :], in_=Db[:, i * D:(i + 1) * D]), inc="V",
                waits=(ev_corr,))
        dump_tiles("ocor", mk_oc)
    for i in range(NT):
        evsq = p.emit("vector", lambda h, i=i: h.tensor_mul(
            out=X2[:, 0:D], in0=Db[:, i * D:(i + 1) * D],
            in1=Db[:, i * D:(i + 1) * D]), inc="V", waits=(ev_corr,))
        ev_red = None
        for h_ in range(H):
            ev_red = p.emit("vector", lambda h, i=i, h_=h_: h.reduce_sum(
                out=RSQ[:, i * 8 + h_: i * 8 + h_ + 1],
                in_=X2[:, h_ * HV:(h_ + 1) * HV], axis=AX), inc="V",
                waits=(evsq,))
        if "sqdump" in taps and i == 7:
            out_evs.append(p.dma("gpsimd", taps["sqdump"], X2[:, :],
                                 waits=(("V", p.cnt["V"]),)))
            p.barrier()
        e1 = p.emit("vector", lambda h, i=i: h.tensor_scalar(
            out=RSQ[:, i * 8:(i + 1) * 8], in0=RSQ[:, i * 8:(i + 1) * 8],
            scalar1=1.0 / HV, scalar2=1e-6, op0=ALU.mult, op1=ALU.add),
            inc="V", waits=(ev_red,))
        e2 = p.emit("vector", lambda h, i=i: h.reciprocal(
            out=RSQ[:, i * 8:(i + 1) * 8], in_=RSQ[:, i * 8:(i + 1) * 8]),
            inc="V", waits=(e1,))
    ev_e2all = ("V", p.cnt["V"])
    p.emit("scalar", lambda h: h.activation(
        out=RSQ[:, :], in_=RSQ[:, :], func=AF.Sqrt), inc="A",
        waits=(ev_e2all,))
    ev_rsq = ("A", p.cnt["A"])
    ev_rsqV = ("V", p.cnt["V"])
    ev_cb2 = p.dma("gpsimd", CB[:, :], dgnw.broadcast_to([128, D]),
                   waits=(ev_xr.get(NT - 1),))
    if "cbdump" in taps:
        out_evs.append(p.dma("gpsimd", taps["cbdump"], CB[:, :],
                             waits=(ev_cb2,)))
        p.barrier()
    if "rsqdump" in taps:
        out_evs.append(p.dma("gpsimd", taps["rsqdump"], RSQ[:, :],
                             waits=(ev_rsq,)))
        p.barrier()
    with ExitStack() as pctx:
        PG = [pctx.enter_context(nc.psum_tensor(f"PG{i}", [128, 512], F32))
              for i in range(2)]
        PTR7v = pctx.enter_context(nc.psum_tensor("PTR7", [128, 512], BF16))
        ev_gfree = [None, None]
        ev_tr_ev = None
        ev_em2_prev = None
        ev_tr4_prev = None
        wbufs = [WS, Fb]
        wsems = ["dw", "dw2"]
        ev_buf_user = [None, None]
        for sl in range(4):
            wb = sl % 2
            evl = None
            for kt in range(KT):
                evl = p.dma("gpsimd", wbufs[wb][:, kt * 512:(kt + 1) * 512],
                            dwg[kt * 128:(kt + 1) * 128,
                                sl * 512:(sl + 1) * 512],
                            waits=(ev_buf_user[wb],) if kt == 0 else (),
                            sem=wsems[wb], gate=(kt == 0))
            for i in range(NT):
                b = i % 2
                evm = None
                for kt in range(KT):
                    evm = p.emit("tensor", lambda h, b=b, kt=kt, i=i, wb=wb: h.matmul(
                        PG[b][:, :], BbT(kt, i),
                        wbufs[wb][:, kt * 512:(kt + 1) * 512],
                        start=(kt == 0), stop=(kt == KT - 1)),
                        inc="T" if kt == KT - 1 else None,
                        waits=(evl, ev_corr,
                               ev_gfree[b] if kt == 0 else None))
                ev_buf_user[wb] = evm
                evsil = p.emit("scalar", lambda h, b=b: h.activation(
                    out=Gb[:, 0:512], in_=PG[b][:, :], func=AF.Silu), inc="A",
                    waits=(evm, ev_rsq, ev_em2_prev, ev_corr))
                ev_gfree[b] = evsil
                h0, h1 = sl * 2, sl * 2 + 1
                en1 = p.emit("vector", lambda h, i=i, h0=h0: h.tensor_scalar_mul(
                    out=Gb[:, 512:768],
                    in0=Db[:, i * D + h0 * HV: i * D + (h0 + 1) * HV],
                    scalar1=RSQ[:, i * 8 + h0: i * 8 + h0 + 1]), inc="V",
                    waits=(ev_rsq, ev_corr, ev_em2_prev))
                en2 = p.emit("vector", lambda h, i=i, h1=h1: h.tensor_scalar_mul(
                    out=Gb[:, 768:1024],
                    in0=Db[:, i * D + h1 * HV: i * D + (h1 + 1) * HV],
                    scalar1=RSQ[:, i * 8 + h1: i * 8 + h1 + 1]), inc="V",
                    waits=(ev_rsq,))
                em1 = p.emit("vector", lambda h, sl=sl: h.tensor_mul(
                    out=Gb[:, 512:1024], in0=Gb[:, 512:1024],
                    in1=CB[:, sl * 512:(sl + 1) * 512]), inc="V",
                    waits=(en1, en2, ev_cb2))
                em2 = p.emit("vector", lambda h: h.tensor_mul(
                    out=Ub[:, :], in0=Gb[:, 512:1024],
                    in1=Gb[:, 0:512]), inc="V",
                    waits=(em1, evsil, ev_tr4_prev))
                ev_em2_prev = em2
                evtr = None
                for j in range(4):
                    evtr = p.emit("tensor", lambda h, j=j: h.transpose(
                        PTR7v[:, j * 128:(j + 1) * 128],
                        Ub[:, j * 128:(j + 1) * 128], ident[:, :]),
                        inc="T", waits=(em2, ev_tr_ev))
                ev_tr4_prev = evtr
                dst = Cb[:].rearrange("q (kt t) -> q kt t", kt=KT)[
                    :, sl * 4:(sl + 1) * 4, i * 128:(i + 1) * 128]
                ev_tr_ev = p.emit("scalar", lambda h, dst=dst: h.copy(
                    out=dst, in_=PTR7v[:].rearrange("q (a b) -> q a b", a=4)),
                    inc="A", waits=(evtr,))
    ev_ofT = ("A", p.cnt["A"])
    if "cbof" in taps:
        out_evs.append(p.dma("gpsimd", taps["cbof"], Cb[:, :],
                             waits=(ev_ofT,)))
        p.barrier()
    if "ofin" in taps:
        def mk_of(i):
            return p.emit("vector", lambda h, i=i: h.tensor_copy(
                out=Hs[:, :], in_=Db[:, i * D:(i + 1) * D]), inc="V",
                waits=(ev_ofT,))
        dump_tiles("ofin", mk_of)
    p.barrier()

    ev_acc = {}
    for i in range(NT):
        xa = X2[:, :]
        evx = p.dma("gpsimd", xa, dx[i * 128:(i + 1) * 128, :],
                    waits=(ev_acc.get(i - 1), ev_rsqV))
        ev_acc[i] = p.emit("vector", lambda h, xa=xa, i=i: h.tensor_scalar(
            out=A8[:, i * D:(i + 1) * D], in0=xa, scalar1=st(i, SL_R),
            scalar2=st(i, SL_MR), op0=ALU.mult, op1=ALU.subtract), inc="V",
            waits=(evx,))
    with ExitStack() as pctx:
        PO2 = [pctx.enter_context(nc.psum_tensor(f"PO2{i}", [128, 512], F32))
               for i in range(2)]
        ev_ofree = [None, None]
        wbufs = [WS, Fb]
        wsems = ["dw", "dw2"]
        ev_buf_user2 = [None, None]
        for sl in range(4):
            wb = sl % 2
            evl = None
            for kt in range(KT):
                evl = p.dma("gpsimd", wbufs[wb][:, kt * 512:(kt + 1) * 512],
                            dwo[kt * 128:(kt + 1) * 128,
                                sl * 512:(sl + 1) * 512],
                            waits=(ev_buf_user2[wb],) if kt == 0 else (),
                            sem=wsems[wb], gate=(kt == 0))
            for i in range(NT):
                b = i % 2
                evm = None
                for kt in range(KT):
                    evm = p.emit("tensor", lambda h, b=b, kt=kt, i=i, wb=wb: h.matmul(
                        PO2[b][:, :], CbT(kt, i),
                        wbufs[wb][:, kt * 512:(kt + 1) * 512],
                        start=(kt == 0), stop=(kt == KT - 1)),
                        inc="T" if kt == KT - 1 else None,
                        waits=(evl, ev_ofT,
                               ev_ofree[b] if kt == 0 else None))
                ev_buf_user2[wb] = evm
                eva = p.emit("vector", lambda h, b=b, i=i, sl=sl: h.tensor_add(
                    out=A8[:, i * D + sl * 512: i * D + (sl + 1) * 512],
                    in0=A8[:, i * D + sl * 512: i * D + (sl + 1) * 512],
                    in1=PO2[b][:, :]), inc="V", waits=(evm, ev_acc[i]))
                ev_ofree[b] = eva
    ev_x1 = ("V", p.cnt["V"])
    if "x1" in taps:
        def mk_x1(i):
            return p.emit("vector", lambda h, i=i: h.tensor_copy(
                out=Hs[:, :], in_=A8[:, i * D:(i + 1) * D]), inc="V",
                waits=(ev_x1,))
        dump_tiles("x1", mk_x1)
    p.barrier()

    ev_cb3 = p.dma("gpsimd", CB[:, :], dffw.broadcast_to([128, D]))
    for i in range(NT):
        evsq = p.emit("scalar", lambda h, i=i: h.activation(
            out=Hs[:, 0:D], in_=A8[:, i * D:(i + 1) * D], func=AF.Square,
            accum_out=st(i, 12)), inc="A", waits=(ev_x1,))
        e1 = p.emit("vector", lambda h, i=i: h.tensor_scalar(
            out=st(i, 13), in0=st(i, 12), scalar1=1.0 / D, scalar2=1e-6,
            op0=ALU.mult, op1=ALU.add), inc="V", waits=(evsq,))
        e2 = p.emit("vector", lambda h, i=i: h.reciprocal(
            out=st(i, 14), in_=st(i, 13)), inc="V", waits=(e1,))
        p.emit("scalar", lambda h, i=i: h.activation(
            out=st(i, SL_R2B), in_=st(i, 14), func=AF.Sqrt), inc="A",
            waits=(e2,))
    ev_r2b = ("A", p.cnt["A"])
    ev_h2 = []
    with nc.psum_tensor("PTR8", [128, 512], BF16) as PTR8v:
        for i in range(NT):
            evh = p.emit("vector", lambda h, i=i: h.scalar_tensor_tensor(
                out=Gb[:, 1024:3072], in0=A8[:, i * D:(i + 1) * D],
                scalar=st(i, SL_R2B), in1=CB[:, :], op0=ALU.mult,
                op1=ALU.mult), inc="V", waits=(ev_r2b, ev_cb3, ev_x1,
                                               ev_h2[-1] if ev_h2 else None))
            for g in range(4):
                evtr = None
                for j in range(4):
                    kt = g * 4 + j
                    evtr = p.emit("tensor", lambda h, kt=kt, j=j: h.transpose(
                        PTR8v[:, j * 128:(j + 1) * 128],
                        Gb[:, 1024 + kt * 128:1024 + (kt + 1) * 128],
                        ident[:, :]),
                        inc="T", waits=(evh, ev_h2[-1] if ev_h2 else None))
                dst = Cb[:].rearrange("q (kt t) -> q kt t", kt=KT)[
                    :, g * 4:(g + 1) * 4, i * 128:(i + 1) * 128]
                eve = p.emit("scalar", lambda h, dst=dst: h.copy(
                    out=dst, in_=PTR8v[:].rearrange("q (a b) -> q a b", a=4)),
                    inc="A", waits=(evtr,))
                ev_h2.append(eve)
    ev_h2T = ev_h2[-1]
    p.barrier()

    chunks = [(0, 16), (16, 32), (32, MA)]
    with ExitStack() as pctx:
        PG2 = pctx.enter_context(nc.psum_tensor("PG8", [128, 1024], F32))
        PU2 = pctx.enter_context(nc.psum_tensor("PU8", [128, 1024], F32))
        PD = [pctx.enter_context(nc.psum_tensor(f"PD8{i}", [128, 512], F32))
              for i in range(2)]
        ev_ffb = None
        ev_dfree = [None, None]
        wbufs = [WS, Fb]
        wsems = ["dw", "dw2"]
        ev_buf_user3 = [None, None]
        grot = 0
        for (m0, m1) in chunks:
            nm = m1 - m0
            ev_ff = []
            for mg in range((nm + 1) // 2):
                gw = min(2, nm - mg * 2)
                c0 = (m0 + mg * 2) * 128
                wb = grot % 2
                grot += 1
                wbuf = wbufs[wb]
                ev_wg2 = None
                ev_wu2 = None
                for kt in range(KT):
                    ev_wg2 = p.dma("gpsimd", wbuf[:, kt * 256: kt * 256 + gw * 128],
                                   dwga[kt * 128:(kt + 1) * 128, c0:c0 + gw * 128],
                                   waits=(ev_buf_user3[wb],) if kt == 0 else (),
                                   sem=wsems[wb], gate=(kt == 0))
                    ev_wu2 = p.dma("gpsimd",
                                   wbuf[:, 4096 + kt * 256: 4096 + kt * 256 + gw * 128],
                                   dwua[kt * 128:(kt + 1) * 128, c0:c0 + gw * 128],
                                   sem=wsems[wb], gate=False)
                for ml in range(gw):
                    m = m0 + mg * 2 + ml
                    evg = None
                    evu = None
                    for half in range(2):
                        for kt in range(KT):
                            evg = p.emit("tensor", lambda h, kt=kt, ml=ml, half=half,
                                         wbuf=wbuf: h.matmul(
                                PG2[:, half * 512:(half + 1) * 512],
                                wbuf[:, kt * 256 + ml * 128: kt * 256 + (ml + 1) * 128],
                                Cb[:, kt * TPC + half * 512:
                                   kt * TPC + (half + 1) * 512],
                                start=(kt == 0), stop=(kt == KT - 1)),
                                inc="T" if kt == KT - 1 else None,
                                waits=(ev_wg2, ev_wu2, ev_h2T,
                                       ev_ff[-1] if (ev_ff and kt == 0) else None))
                    for half in range(2):
                        for kt in range(KT):
                            evu = p.emit("tensor", lambda h, kt=kt, ml=ml, half=half,
                                         wbuf=wbuf: h.matmul(
                                PU2[:, half * 512:(half + 1) * 512],
                                wbuf[:, 4096 + kt * 256 + ml * 128:
                                   4096 + kt * 256 + (ml + 1) * 128],
                                Cb[:, kt * TPC + half * 512:
                                   kt * TPC + (half + 1) * 512],
                                start=(kt == 0), stop=(kt == KT - 1)),
                                inc="T" if kt == KT - 1 else None,
                                waits=(ev_ff[-1] if (ev_ff and kt == 0) else None,))
                    ev_buf_user3[wb] = evu
                    evs1 = p.emit("scalar", lambda h: h.activation(
                        out=Gb[:, 0:512], in_=PG2[:, 0:512], func=AF.Silu),
                        inc="A", waits=(evg,))
                    evs2 = p.emit("scalar", lambda h: h.activation(
                        out=Gb[:, 512:1024], in_=PG2[:, 512:1024],
                        func=AF.Silu), inc="A", waits=(evg,))
                    mloc = m - m0
                    ef1 = p.emit("vector", lambda h, mloc=mloc: h.tensor_mul(
                        out=Bb[:, mloc * TPC: mloc * TPC + 512],
                        in0=Gb[:, 0:512], in1=PU2[:, 0:512]), inc="V",
                        waits=(evs1, evu))
                    ef2 = p.emit("vector", lambda h, mloc=mloc: h.tensor_mul(
                        out=Bb[:, mloc * TPC + 512: mloc * TPC + 1024],
                        in0=Gb[:, 512:1024], in1=PU2[:, 512:1024]), inc="V",
                        waits=(evs2, evu))
                    ev_ff.append(ef2)
                    ev_ff.append(ef1)
            ev_ffd = ev_ff[-1]
            ev_ffv = ("V", p.cnt["V"])
            wda_sub = (nm + 15) // 16
            for sl in range(4):
                wb = grot % 2
                grot += 1
                wbuf = wbufs[wb]
                ev_wd2 = None
                for jm in range(nm):
                    ev_wd2 = p.dma("gpsimd", wbuf[:, jm * 512:(jm + 1) * 512],
                                   dwda[(m0 + jm) * 128:(m0 + jm + 1) * 128,
                                        sl * 512:(sl + 1) * 512],
                                   waits=(ev_ffv, ev_buf_user3[wb])
                                   if jm == 0 else (),
                                   sem=wsems[wb], gate=(jm == 0))
                for i in range(NT):
                    b = i % 2
                    evm = None
                    for jm in range(nm):
                        evm = p.emit("tensor", lambda h, b=b, jm=jm, i=i, nm=nm,
                                     wbuf=wbuf: h.matmul(
                            PD[b][:, :],
                            Bb[:, jm * TPC + i * 128: jm * TPC + (i + 1) * 128],
                            wbuf[:, jm * 512:(jm + 1) * 512],
                            start=(jm == 0), stop=(jm == nm - 1)),
                            inc="T" if jm == nm - 1 else None,
                            waits=(ev_wd2, ev_ffv,
                                   ev_dfree[b] if jm == 0 else None))
                    eva = p.emit("vector", lambda h, b=b, i=i, sl=sl: h.tensor_add(
                        out=A8[:, i * D + sl * 512: i * D + (sl + 1) * 512],
                        in0=A8[:, i * D + sl * 512: i * D + (sl + 1) * 512],
                        in1=PD[b][:, :]), inc="V", waits=(evm,))
                    ev_dfree[b] = eva
                    ev_buf_user3[wb] = evm
            ev_ffb = ("V", p.cnt["V"])
    p.barrier()

    ev_fin = {}
    ost = [None, None]
    for i in range(NT):
        obuf = OUTS if i % 2 == 0 else Hs
        evy = p.dma("gpsimd", X2[:, :], d_y.ap()[i * 128:(i + 1) * 128, :],
                    waits=(ev_fin.get(i - 1),))
        e1 = p.emit("vector", lambda h, i=i, obuf=obuf: h.scalar_tensor_tensor(
            out=obuf[:, 0:D], in0=X2[:, :], scalar=st(i, SL_TSWS),
            in1=A8[:, i * D:(i + 1) * D], op0=ALU.mult, op1=ALU.add), inc="V",
            waits=(evy, ost[i % 2]))
        evx = p.dma("gpsimd", X2[:, :], dx[i * 128:(i + 1) * 128, :],
                    waits=(e1,))
        e2 = p.emit("vector", lambda h, obuf=obuf: h.tensor_add(
            out=obuf[:, 0:D], in0=obuf[:, 0:D], in1=X2[:, :]), inc="V",
            waits=(evx,))
        ev_fin[i] = e2
        ev_o = p.dma("gpsimd", dout[i * 128:(i + 1) * 128, :],
                     obuf[:, 0:D], waits=(e2,),
                     sem="dt" if i % 2 == 0 else "dt2")
        ost[i % 2] = ev_o
        out_evs.append(ev_o)

    p.barrier()
    with nc.Block() as block:
        p.finalize(block)
    ctx.close()
    return nc


_NC_CACHE = {}


def _get_nc(debug_taps=()):
    key = tuple(sorted(debug_taps))
    if key not in _NC_CACHE:
        _NC_CACHE[key] = build_nc(debug_taps=key)
    return _NC_CACHE[key]


def stage_inputs(inputs, debug_taps=()):
    g = lambda k: np.ascontiguousarray(np.asarray(inputs[k], dtype=np.float32))
    x = g("L0_attn_x").reshape(T, D)
    wgT = np.ascontiguousarray(g("L0_ffn_w_gate").T)
    wuT = np.ascontiguousarray(g("L0_ffn_w_up").T)
    wdT = np.ascontiguousarray(g("L0_ffn_w_down").T)
    shared = dict(
        wq=g("L0_attn_wq"), wk=g("L0_attn_wk"), wv=g("L0_attn_wv"),
        gk_w1=g("L0_attn_gk_w1"), gk_w2=g("L0_attn_gk_w2"),
        gk_b=g("L0_attn_gk_b"), wg=g("L0_attn_wg"), wo=g("L0_attn_wo"),
        w_gate_a=g("L0_attn_w_gate"), w_up_a=g("L0_attn_w_up"),
        w_down_a=g("L0_attn_w_down"),
        wgT=wgT, wuT=wuT, wdT=wdT,
        n1w_row=g("L0_attn_norm1_w").reshape(1, D),
        ffnw_row=g("L0_attn_ffn_norm_w").reshape(1, D),
        gnw_row=np.ascontiguousarray(
            np.tile(g("L0_attn_gnorm_w"), H)).reshape(1, D),
        ident_in=np.eye(128, dtype=np.float32),
        maskT_in=np.triu(np.ones((64, 64), dtype=np.float32)),
        seg_row=np.tile(
            np.concatenate([[0.0], np.ones(63)]).astype(np.float32),
            NCH).reshape(1, TPC),
    )
    in_maps = []
    for c in range(NCORE):
        im = dict(shared)
        im["x"] = np.ascontiguousarray(x[c * TPC:(c + 1) * TPC])
        gs = np.zeros((1, 16), np.float32)
        gs[0, :c] = 1.0
        gs[0, 8:16] = 1.0 - gs[0, 0:8]
        im["gsel_row"] = gs
        sl = np.empty((12, 128, 4096), np.float32)
        sl[0:4] = wgT[c * 256:(c + 1) * 256].reshape(4, 128, 4096)
        sl[4:8] = wuT[c * 256:(c + 1) * 256].reshape(4, 128, 4096)
        sl[8:12] = wdT[c * 1024:(c + 1) * 1024].reshape(4, 128, 4096)
        im["w_scale_slice"] = sl
        in_maps.append(im)
    return in_maps


def kernel(**inputs):
    nc = _get_nc()
    in_maps = stage_inputs(inputs)
    res = run_bass_kernel_spmd(nc, in_maps, core_ids=list(range(NCORE)))
    out = np.concatenate([r["out"] for r in res.results], axis=0)
    return out.reshape(1, T, D).astype(np.float32)


# version 07 (final)
# speedup vs baseline: 1.2072x; 1.1194x over previous
import numpy as np
from contextlib import ExitStack

import concourse.bass as bass
import concourse.mybir as mybir
from concourse.bass_utils import run_bass_kernel_spmd

F32 = mybir.dt.float32
BF16 = mybir.dt.bfloat16
I32 = mybir.dt.int32
AF = mybir.ActivationFunctionType
ALU = mybir.AluOpType
AX = mybir.AxisListType.X

NCORE = 8
T = 8192
TPC = T // NCORE
NT = TPC // 128
D = 2048
KT = D // 128
H = 8
HK = 128
HV = 256
KEY = 1024
CH = 64
NCH = TPC // CH
HIDA = 5504
MA = HIDA // 128
HIDQ = 8192
LOW = 16
C1 = float(HK) ** -0.5
IGN = 1.0 / 16.0
ENGS = ("tensor", "vector", "scalar", "sync", "gpsimd")

SL_R, SL_MR, SL_RR2, SL_MRR2, SL_SQ, SL_RCPQ, SL_XWG, SL_XWU = range(8)
SL_TSWS, SL_RCPT, SL_MXT, SL_R2B, SL_T0, SL_T1, SL_T2, SL_T3 = range(8, 16)


class Prog:
    def __init__(self, nc, sems):
        self.nc = nc
        self.sem = sems
        self.q = {e: [] for e in ENGS}
        self.cnt = {s: 0 for s in sems}
        self.lastw = {e: {} for e in ENGS}

    def emit(self, eng, fn, inc=None, waits=(), inc_n=1):
        wl = []
        for w in waits:
            if w is None:
                continue
            s, v = w
            if v <= 0 or self.lastw[eng].get(s, 0) >= v:
                continue
            self.lastw[eng][s] = v
            wl.append((s, v))
        if inc is not None:
            self.cnt[inc] += inc_n

        def run(h, _fn=fn, _wl=tuple(wl), _inc=inc, _n=inc_n):
            for s, v in _wl:
                h.wait_ge(self.sem[s], v)
            ins = _fn(h)
            if _inc is not None:
                ins.then_inc(self.sem[_inc], _n)

        self.q[eng].append(run)
        return (inc, self.cnt[inc]) if inc is not None else None

    def dma(self, engq, out, in_, waits=(), slow=False, sem=None, gate=None):
        s = sem or {"sync": "ds", "gpsimd": "dg"}[engq]
        if gate is None:
            gate = (engq == "gpsimd")
        w = list(waits)
        if gate:
            w.append((s, self.cnt[s]))
        return self.emit(engq, lambda h, o=out, i=in_, sl=slow: h.dma_start(
            out=o, in_=i, allow_slow_non_contiguous=sl),
            inc=s, waits=w, inc_n=16)

    def barrier(self):
        evs = [(s, self.cnt[s]) for s in self.cnt if self.cnt[s] > 0]
        for e in ENGS:
            for s, v in evs:
                if self.lastw[e].get(s, 0) < v:
                    self.lastw[e][s] = v
                    self.q[e].append(lambda h, _s=s, _v=v: h.wait_ge(self.sem[_s], _v))

    def finalize(self, block):
        for e in ENGS:
            def mk(ops_):
                def body(h):
                    for f in ops_:
                        f(h)
                return body
            getattr(block, e)(mk(list(self.q[e])))


def build_nc(debug_taps=()):
    nc = bass.Bass("TRN2", target_bir_lowering=False)

    dx = nc.dram_tensor("x", [TPC, D], F32, kind="ExternalInput").ap()
    dwq = nc.dram_tensor("wq", [D, KEY], F32, kind="ExternalInput").ap()
    dwk = nc.dram_tensor("wk", [D, KEY], F32, kind="ExternalInput").ap()
    dwv = nc.dram_tensor("wv", [D, D], F32, kind="ExternalInput").ap()
    dgw1 = nc.dram_tensor("gk_w1", [D, LOW], F32, kind="ExternalInput").ap()
    dgw2 = nc.dram_tensor("gk_w2", [LOW, KEY], F32, kind="ExternalInput").ap()
    dgkb = nc.dram_tensor("gk_b", [KEY], F32, kind="ExternalInput").ap()
    dwg = nc.dram_tensor("wg", [D, D], F32, kind="ExternalInput").ap()
    dwo = nc.dram_tensor("wo", [D, D], F32, kind="ExternalInput").ap()
    dwga = nc.dram_tensor("w_gate_a", [D, HIDA], F32, kind="ExternalInput").ap()
    dwua = nc.dram_tensor("w_up_a", [D, HIDA], F32, kind="ExternalInput").ap()
    dwda = nc.dram_tensor("w_down_a", [HIDA, D], F32, kind="ExternalInput").ap()
    dwgq = nc.dram_tensor("wgT", [D, HIDQ], F32, kind="ExternalInput").ap()
    dwuq = nc.dram_tensor("wuT", [D, HIDQ], F32, kind="ExternalInput").ap()
    dwdq = nc.dram_tensor("wdT", [HIDQ, D], F32, kind="ExternalInput").ap()
    dwsl = nc.dram_tensor("w_scale_slice", [12, 128, 4096], F32, kind="ExternalInput").ap()
    dn1 = nc.dram_tensor("n1w_row", [1, D], F32, kind="ExternalInput").ap()
    dffw = nc.dram_tensor("ffnw_row", [1, D], F32, kind="ExternalInput").ap()
    dgnw = nc.dram_tensor("gnw_row", [1, D], F32, kind="ExternalInput").ap()
    dident = nc.dram_tensor("ident_in", [128, 128], F32, kind="ExternalInput").ap()
    dmaskt = nc.dram_tensor("maskT_in", [64, 64], F32, kind="ExternalInput").ap()
    dgsel = nc.dram_tensor("gsel_row", [1, 16], F32, kind="ExternalInput").ap()
    dseg = nc.dram_tensor("seg_row", [1, TPC], F32, kind="ExternalInput").ap()

    dout = nc.dram_tensor("out", [TPC, D], F32, kind="ExternalOutput").ap()

    d_t = nc.dram_tensor("t_buf", [TPC, HIDQ], F32)
    d_y = nc.dram_tensor("y_buf", [TPC, D], F32)
    d_sc = nc.dram_tensor("sc_bounce", [1, 8], F32)
    ag_sin = nc.dram_tensor("ag_sin", [1, 4], F32)
    ag_sout = nc.dram_tensor("ag_sout", [1, 4], F32)
    ag_in = nc.dram_tensor("ag_in", [128, 2056], F32)
    ag_out = nc.dram_tensor("ag_out", [128 * NCORE, 2056], F32)

    taps = {}
    tap_shapes = {"xn": [TPC, D], "qffn": [TPC, D], "h": [TPC, D],
                  "sp": [KEY, TPC], "qe": [KEY, TPC], "ke": [KEY, TPC],
                  "o_loc": [TPC, D], "S": [HK, H * HV], "x1": [TPC, D],
                  "ofin": [TPC, D], "tq": [TPC, HIDQ], "ocor": [TPC, D],
                  "cbof": [128, KT * TPC], "cbdump": [128, D],
                  "rsqdump": [128, 64], "sqdump": [128, D],
                  "statdump": [128, 144], "thrdump": [128, 8],
                  "xqdump": [128, KT * TPC]}
    for name in debug_taps:
        taps[name] = nc.dram_tensor("tap_" + name, tap_shapes[name], F32,
                                    kind="ExternalOutput").ap()

    ctx = ExitStack()
    E = ctx.enter_context
    sems = {s: E(nc.semaphore(s)) for s in ("T", "V", "A", "G", "ds", "dg", "dt", "dt2", "dw", "dw2", "cc", "q0", "q1", "q2", "q3")}
    p = Prog(nc, sems)

    ident = E(nc.sbuf_tensor("ident", [128, 128], BF16))
    maskt = E(nc.sbuf_tensor("maskt", [64, 64], BF16))
    STAT = E(nc.sbuf_tensor("STAT", [128, 144], F32))
    THR = E(nc.sbuf_tensor("THR", [128, 8], F32))
    CK = E(nc.sbuf_tensor("CK", [128, 8], F32))
    DEC = E(nc.sbuf_tensor("DEC", [128, H * NCH], F32))
    PSH = E(nc.sbuf_tensor("PSH", [128, H * NCH], F32))
    PIN = E(nc.sbuf_tensor("PIN", [128, H * NCH], F32))
    GKB = E(nc.sbuf_tensor("GKB", [128, H], F32))
    NGKB = E(nc.sbuf_tensor("NGKB", [128, H], F32))
    GSEL = E(nc.sbuf_tensor("GSEL", [128, 16], F32))
    SEGM = E(nc.sbuf_tensor("SEGM", [128, TPC], BF16))
    ONES16 = E(nc.sbuf_tensor("ONES16", [128, 16], F32))
    PE8 = E(nc.sbuf_tensor("PE8", [128, 8], F32))
    PJ8 = E(nc.sbuf_tensor("PJ8", [128, 8], F32))
    PJM = E(nc.sbuf_tensor("PJM", [128, 8], F32))
    RSQ = E(nc.sbuf_tensor("RSQ", [128, 64], F32))
    SC1 = E(nc.sbuf_tensor("SC1", [1, 8], F32))
    SAM = E(nc.sbuf_tensor("SAM", [128, 64], BF16))
    SKDT = E(nc.sbuf_tensor("SKDT", [128, 64], BF16))
    SKD = E(nc.sbuf_tensor("SKD", [128, 128], BF16))
    Ub = E(nc.sbuf_tensor("Ub", [128, 512], BF16))
    Vb = E(nc.sbuf_tensor("Vb", [128, 512], BF16))

    M0 = 24544
    A = nc.alloc_sbuf_tensor_at
    Bb = A("Bb", [128, KT * TPC], BF16, offset=M0)
    Cb = A("Cb", [128, KT * TPC], BF16, offset=M0 + 32768)
    A8 = A("A8", [128, NT * D], F32, offset=M0 + 65536)
    Db = A("Db", [128, NT * D], BF16, offset=M0 + 65536)
    Z1W = A("Z1W", [16, 2048], BF16, offset=M0 + 65536 + 4096)
    XQB = A("XQB", [128, D], BF16, offset=M0 + 65536 + 8192)
    OSC = A("OSC", [128, 256], BF16, offset=M0 + 65536 + 12288)
    Eb = A("Eb", [128, H * TPC], BF16, offset=M0 + 98304)
    GTS = A("GTS", [128, NT * 512], F32, offset=M0 + 98304)
    EK = A("EK", [128, 1024], F32, offset=M0 + 114688)
    S_bf = A("S_bf", [128, H * HV], BF16, offset=M0 + 118784)
    S = A("S", [128, H * HV], F32, offset=M0 + 122880)
    RAW = A("RAW", [128, 2048], F32, offset=M0 + 131072)
    Fb = A("Fb", [128, H * TPC], BF16, offset=M0 + 131072)
    Gb = A("Gb", [128, H * TPC], BF16, offset=M0 + 147456)
    WS = A("WS", [128, KT * 512], BF16, offset=M0 + 163840)
    X2 = A("X2", [128, D], F32, offset=M0 + 180224)
    I32S = A("I32S", [128, D], I32, offset=M0 + 188416)
    CB = A("CB", [128, D], F32, offset=M0 + 188416)
    OUTS = A("OUTS", [128, D], F32, offset=M0 + 188416)
    Hs = A("Hs", [128, 2048], F32, offset=M0 + 196608)
    XB2 = A("XB2", [128, D], F32, offset=M0 + 32768)

    def st(i, sl):
        return STAT[:, i * 16 + sl: i * 16 + sl + 1]

    def BbT(kt, i):
        return Bb[:, kt * TPC + i * 128: kt * TPC + (i + 1) * 128]

    def CbT(kt, i):
        return Cb[:, kt * TPC + i * 128: kt * TPC + (i + 1) * 128]

    out_evs = []

    def dump_tiles(tap, srcfn, n=NT, evs=()):
        for i in range(n):
            eve = srcfn(i)
            out_evs.append(p.dma("gpsimd", taps[tap][i * 128:(i + 1) * 128, :],
                                 Hs[:, :], waits=(eve,) + tuple(evs)))
            p.barrier()

    p.dma("gpsimd", ident[:, :], dident)
    p.dma("gpsimd", maskt[:, :], dmaskt)
    ev_gsel = p.dma("gpsimd", GSEL[:, :], dgsel.broadcast_to([128, 16]))
    ev_seg = p.dma("gpsimd", SEGM[:, :], dseg.broadcast_to([128, TPC]))
    ev_gkb = p.dma("gpsimd", GKB[:, :], dgkb.rearrange("(h q) -> q h", h=H),
                   slow=True)
    ev_ck = p.emit("vector", lambda h: h.memset(CK[:, 0:3], 0.0), inc="V")
    ev_ck = p.emit("vector", lambda h: h.memset(CK[:, 0:1], 1.0), inc="V")
    ev_ckm = p.emit("vector", lambda h: h.memset(CK[:, 1:2], -1.0), inc="V")
    ev_o16 = p.emit("vector", lambda h: h.memset(ONES16[:, :], 1.0), inc="V")
    ev_ngkb = p.emit("vector", lambda h: h.tensor_scalar_mul(
        out=NGKB[:, :], in0=GKB[:, :], scalar1=-1.0), inc="V", waits=(ev_gkb,))

    ev_pacc0 = p.emit("vector", lambda h: h.memset(STAT[:, 128:133], 0.0), inc="V")
    last_red = ev_pacc0
    last_raw_user = [None, None]
    for sidx in range(12):
        wcol = sidx // 4
        for qc in range(4):
            slot = (sidx * 4 + qc) % 2
            evd = p.dma("sync", RAW[:, slot * 1024: slot * 1024 + 1024],
                        dwsl[sidx, :, qc * 1024:(qc + 1) * 1024],
                        waits=(last_raw_user[slot],), sem=f"q{slot}")
            evr = p.emit("vector", lambda h, sl=slot: h.reduce_sum(
                out=STAT[:, 132:133], in_=RAW[:, sl * 1024: sl * 1024 + 1024],
                axis=AX, apply_absolute_value=True), inc="V",
                waits=(evd, last_red))
            last_raw_user[slot] = evr
            last_red = p.emit("vector", lambda h, w=wcol: h.tensor_add(
                out=STAT[:, 128 + w:129 + w], in0=STAT[:, 128 + w:129 + w],
                in1=STAT[:, 132:133]), inc="V", waits=(evr,))
    with nc.psum_tensor("PS1", [1, 4], F32) as PS1:
        ev_mm1 = p.emit("tensor", lambda h: h.matmul(
            PS1[:, :], CK[:, 0:1], STAT[:, 128:132], start=True, stop=True),
            inc="T", waits=(last_red, ev_ck))
        ev_sc1 = p.emit("vector", lambda h: h.tensor_copy(
            out=SC1[0:1, 0:4], in_=PS1[:, :]), inc="V", waits=(ev_mm1,))
    ev_scd = p.dma("gpsimd", ag_sin.ap(), SC1[0:1, 0:4], waits=(ev_sc1,))
    ev_ar = p.emit("gpsimd", lambda h: h.collective_compute(
        "AllReduce", ALU.add, replica_groups=[list(range(NCORE))],
        ins=[ag_sin.ap().opt()], outs=[ag_sout.ap().opt()]),
        inc="cc", waits=(ev_scd,))
    ev_scl = p.dma("gpsimd", SC1[0:1, 0:4], ag_sout.ap(), waits=(ev_ar,))
    ev_ws = p.emit("vector", lambda h: h.tensor_scalar(
        out=SC1[0:1, 4:7], in0=SC1[0:1, 0:3], scalar1=1.0 / (D * HIDQ),
        scalar2=1e-5, op0=ALU.mult, op1=ALU.add), inc="V", waits=(ev_scl,))
    ev_inv = p.emit("vector", lambda h: h.reciprocal(
        out=SC1[0:1, 0:3], in_=SC1[0:1, 4:7]), inc="V", waits=(ev_ws,))
    ev_scb = p.dma("gpsimd", d_sc.ap(), SC1[0:1, 0:8], waits=(ev_inv,))
    ev_thr = p.dma("gpsimd", THR[:, :], d_sc.ap().broadcast_to([128, 8]),
                   waits=(ev_scb,))

    ev_xq_done = []
    ev_xread = {}
    with nc.psum_tensor("PTR1", [128, 512], BF16) as PTR1v:
        for i in range(NT):
            xa = X2[:, :] if i % 2 == 0 else XB2[:, :]
            evx = p.dma("gpsimd", xa, dx[i * 128:(i + 1) * 128, :],
                        waits=(ev_xread.get(i - 2),),
                        sem="dt" if i % 2 == 0 else "dt2")
            ev_s1 = p.emit("vector", lambda h, xa=xa, i=i: h.reduce_sum(
                out=st(i, 8), in_=xa, axis=AX), inc="V", waits=(evx,))
            ev_sq = p.emit("scalar", lambda h, xa=xa, i=i: h.activation(
                out=A8[:, 0:D], in_=xa, func=AF.Square,
                accum_out=st(i, 10)), inc="A", waits=(evx,))
            e1 = p.emit("vector", lambda h, i=i: h.tensor_scalar_mul(
                out=st(i, 9), in0=st(i, 8), scalar1=1.0 / D), inc="V",
                waits=(ev_s1,))
            e2 = p.emit("vector", lambda h, i=i: h.tensor_scalar_mul(
                out=st(i, 10), in0=st(i, 10), scalar1=1.0 / D), inc="V",
                waits=(ev_sq,))
            e3 = p.emit("vector", lambda h, i=i: h.tensor_mul(
                out=st(i, 11), in0=st(i, 9), in1=st(i, 9)), inc="V", waits=(e1,))
            e4 = p.emit("vector", lambda h, i=i: h.tensor_sub(
                out=st(i, 11), in0=st(i, 10), in1=st(i, 11)), inc="V",
                waits=(e2, e3))
            e5 = p.emit("vector", lambda h, i=i: h.tensor_scalar_add(
                out=st(i, 11), in0=st(i, 11), scalar1=1e-5), inc="V", waits=(e4,))
            e6 = p.emit("vector", lambda h, i=i: h.reciprocal(
                out=st(i, 12), in_=st(i, 11)), inc="V", waits=(e5,))
            ev_r = p.emit("scalar", lambda h, i=i: h.activation(
                out=st(i, SL_R), in_=st(i, 12), func=AF.Sqrt), inc="A",
                waits=(e6,))
            ev_mr = p.emit("vector", lambda h, i=i: h.tensor_mul(
                out=st(i, SL_MR), in0=st(i, 9), in1=st(i, SL_R)), inc="V",
                waits=(ev_r,))
            e7 = p.emit("vector", lambda h, i=i: h.tensor_mul(
                out=st(i, 13), in0=st(i, SL_R), in1=st(i, SL_R)), inc="V",
                waits=(ev_r,))
            e8 = p.emit("vector", lambda h, i=i: h.tensor_scalar(
                out=st(i, 13), in0=st(i, 13), scalar1=-1e-5,
                scalar2=1.0 + 1e-6, op0=ALU.mult, op1=ALU.add), inc="V",
                waits=(e7,))
            e9 = p.emit("vector", lambda h, i=i: h.reciprocal(
                out=st(i, 14), in_=st(i, 13)), inc="V", waits=(e8,))
            ev_r2 = p.emit("scalar", lambda h, i=i: h.activation(
                out=st(i, 15), in_=st(i, 14), func=AF.Sqrt), inc="A", waits=(e9,))
            p.emit("vector", lambda h, i=i: h.tensor_mul(
                out=st(i, SL_RR2), in0=st(i, SL_R), in1=st(i, 15)), inc="V",
                waits=(ev_r2,))
            p.emit("vector", lambda h, i=i: h.tensor_mul(
                out=st(i, SL_MRR2), in0=st(i, SL_MR), in1=st(i, 15)), inc="V",
                waits=(ev_r2, ev_mr))
            ev_xn = p.emit("vector", lambda h, xa=xa, i=i: h.tensor_scalar(
                out=Hs[:, 0:D], in0=xa, scalar1=st(i, SL_R),
                scalar2=st(i, SL_MR), op0=ALU.mult, op1=ALU.subtract), inc="V",
                waits=(ev_mr,))
            ev_xread[i] = ev_xn
            ev_mx = p.emit("vector", lambda h, i=i: h.reduce_max(
                out=st(i, 12), in_=Hs[:, 0:D], axis=AX,
                apply_absolute_value=True), inc="V", waits=(ev_xn,))
            ev_sqv = p.emit("vector", lambda h, i=i: h.tensor_scalar(
                out=st(i, SL_SQ), in0=st(i, 12), scalar1=1.0 / 127.0,
                scalar2=1e-5, op0=ALU.mult, op1=ALU.add), inc="V", waits=(ev_mx,))
            ev_rq = p.emit("vector", lambda h, i=i: h.reciprocal(
                out=st(i, SL_RCPQ), in_=st(i, SL_SQ)), inc="V", waits=(ev_sqv,))
            ev_i32 = p.emit("vector", lambda h, i=i: h.tensor_scalar_mul(
                out=I32S[:, :], in0=Hs[:, 0:D],
                scalar1=st(i, SL_RCPQ)), inc="V", waits=(ev_rq,))
            ev_xqb = p.emit("vector", lambda h: h.tensor_copy(
                out=XQB[:, :], in_=I32S[:, :]), inc="V",
                waits=(ev_i32, ev_xq_done[-1] if ev_xq_done else None))
            for g in range(4):
                evtr = None
                for j in range(4):
                    kt = g * 4 + j
                    evtr = p.emit("tensor", lambda h, kt=kt, j=j: h.transpose(
                        PTR1v[:, j * 128:(j + 1) * 128],
                        XQB[:, kt * 128:(kt + 1) * 128], ident[:, :]),
                        inc="T", waits=(ev_xqb, ev_sc1,
                                        ev_xq_done[-1] if ev_xq_done else None))
                dst = Bb[:].rearrange("q (kt t) -> q kt t", kt=KT)[
                    :, g * 4:(g + 1) * 4, i * 128:(i + 1) * 128]
                eve = p.emit("scalar", lambda h, dst=dst: h.copy(
                    out=dst, in_=PTR1v[:].rearrange("q (a b) -> q a b", a=4)),
                    inc="A", waits=(evtr,))
                ev_xq_done.append(eve)
    ev_xqT = ev_xq_done[-1]

    if "xqdump" in taps:
        out_evs.append(p.dma("gpsimd", taps["xqdump"], Bb[:, :],
                             waits=(ev_xqT,)))
        p.barrier()
    if "xn" in taps:
        def mk_xn(i):
            xa = X2[:, :]
            evx = p.dma("gpsimd", xa, dx[i * 128:(i + 1) * 128, :],
                        waits=(ev_xqT,))
            return p.emit("vector", lambda h, xa=xa, i=i: h.tensor_scalar(
                out=Hs[:, :], in0=xa, scalar1=st(i, SL_R), scalar2=st(i, SL_MR),
                op0=ALU.mult, op1=ALU.subtract), inc="V", waits=(evx,))
        dump_tiles("xn", mk_xn)

    p.barrier()

    for i in range(NT):
        p.emit("vector", lambda h, i=i: h.tensor_mul(
            out=st(i, SL_XWG), in0=st(i, SL_SQ), in1=THR[:, 4:5]), inc="V",
            waits=(ev_thr,))
        p.emit("vector", lambda h, i=i: h.tensor_mul(
            out=st(i, SL_XWU), in0=st(i, SL_SQ), in1=THR[:, 5:6]), inc="V",
            waits=(ev_thr,))
        p.emit("vector", lambda h, i=i: h.memset(st(i, SL_MXT), 0.0), inc="V")
    ev_xw = ("V", p.cnt["V"])

    with ExitStack() as pctx:
        PW2v = [pctx.enter_context(nc.psum_tensor(f"PW2{i}", [128, 512], F32))
              for i in range(8)]
        ev_mm = [None] * 8
        ev_free = [None] * 8
        ev_gate = [None] * 8
        ev_gb_users = [None, None]
        ev_tdma_prev = [None, None]
        raw_guard = {}
        qbufs = [Gb, Db]
        for w in range(16):
            for wt in range(2):
                g2 = (w * 2 + wt) % 2
                qb = qbufs[g2]
                dsrc = dwgq if wt == 0 else dwuq
                thr_inv = THR[:, 0:1] if wt == 0 else THR[:, 1:2]
                ev_q = []
                for kt in range(KT):
                    slot = kt % 4
                    evd = p.dma("sync", RAW[:, slot * 512:(slot + 1) * 512],
                                dsrc[kt * 128:(kt + 1) * 128,
                                     w * 512:(w + 1) * 512],
                                waits=(raw_guard.get(slot),), sem=f"q{slot}")
                    evu = p.emit("vector", lambda h, sl=slot, ti=thr_inv:
                                 h.tensor_scalar(
                        out=Ub[:, :],
                        in0=RAW[:, sl * 512:(sl + 1) * 512], scalar1=ti,
                        scalar2=0.5, op0=ALU.mult, op1=ALU.is_ge), inc="V",
                        waits=(evd, ev_thr))
                    evv = p.emit("vector", lambda h, sl=slot, ti=thr_inv:
                                 h.tensor_scalar(
                        out=Vb[:, :],
                        in0=RAW[:, sl * 512:(sl + 1) * 512], scalar1=ti,
                        scalar2=-0.5, op0=ALU.mult, op1=ALU.is_le), inc="V",
                        waits=(evd,))
                    raw_guard[slot] = evv
                    evw = p.emit("vector", lambda h, kt=kt, qb=qb: h.tensor_sub(
                        out=qb[:, kt * 512:(kt + 1) * 512],
                        in0=Ub[:, :], in1=Vb[:, :]), inc="V",
                        waits=(evu, evv,
                               ev_gb_users[g2] if kt == 0 else None))
                    ev_q.append((evw, evv))
                for i in range(NT):
                    evm = None
                    for kt in range(KT):
                        evm = p.emit("tensor", lambda h, i=i, kt=kt, qb=qb: h.matmul(
                            PW2v[i][:, :], BbT(kt, i),
                            qb[:, kt * 512:(kt + 1) * 512],
                            start=(kt == 0), stop=(kt == KT - 1)),
                            inc="T" if kt == KT - 1 else None,
                            waits=(ev_q[kt][0], ev_xqT,
                                   ev_free[i] if kt == 0 else None))
                    ev_mm[i] = evm
                ev_gb_users[g2] = ev_mm[7]
                for i in range(NT):
                    if wt == 0:
                        ev_gate[i] = p.emit("scalar", lambda h, i=i: h.activation(
                            out=GTS[:, i * 512:(i + 1) * 512], in_=PW2v[i][:, :],
                            func=AF.Silu, scale=st(i, SL_XWG), bias=CK[:, 2:3]),
                            inc="A", waits=(ev_mm[i], ev_xw, ev_ck))
                        ev_free[i] = ev_gate[i]
                    else:
                        tslot = i % 2
                        evt = p.emit("vector", lambda h, i=i, tslot=tslot:
                                     h.scalar_tensor_tensor(
                            out=Hs[:, tslot * 512:(tslot + 1) * 512],
                            in0=PW2v[i][:, :], scalar=st(i, SL_XWU),
                            in1=GTS[:, i * 512:(i + 1) * 512],
                            op0=ALU.mult, op1=ALU.mult), inc="V",
                            waits=(ev_mm[i], ev_gate[i], ev_xw,
                                   ev_tdma_prev[tslot]))
                        evr = p.emit("vector", lambda h, i=i, tslot=tslot:
                                     h.reduce_max(
                            out=st(i, SL_T1),
                            in_=Hs[:, tslot * 512:(tslot + 1) * 512], axis=AX,
                            apply_absolute_value=True), inc="V", waits=(evt,))
                        p.emit("vector", lambda h, i=i: h.tensor_max(
                            out=st(i, SL_MXT), in0=st(i, SL_MXT),
                            in1=st(i, SL_T1)), inc="V", waits=(evr,))
                        evdma = p.dma("gpsimd",
                                      d_t.ap()[i * 128:(i + 1) * 128,
                                               w * 512:(w + 1) * 512],
                                      Hs[:, tslot * 512:(tslot + 1) * 512],
                                      waits=(evt,),
                                      sem="dt" if tslot == 0 else "dt2")
                        ev_tdma_prev[tslot] = evdma
                        ev_free[i] = evt
    if "tq" in taps:
        out_evs.append(p.dma("gpsimd", taps["tq"], d_t.ap(),
                             waits=(("dg", p.cnt["dg"]),)))
        p.barrier()
    p.barrier()

    for i in range(NT):
        e1 = p.emit("vector", lambda h, i=i: h.tensor_scalar(
            out=st(i, SL_T0), in0=st(i, SL_MXT), scalar1=1.0 / 127.0,
            scalar2=1e-5, op0=ALU.mult, op1=ALU.add), inc="V")
        p.emit("vector", lambda h, i=i: h.reciprocal(
            out=st(i, SL_RCPT), in_=st(i, SL_T0)), inc="V", waits=(e1,))
        p.emit("vector", lambda h, i=i: h.tensor_mul(
            out=st(i, SL_TSWS), in0=st(i, SL_T0), in1=THR[:, 6:7]), inc="V",
            waits=(e1, ev_thr))
    ev_t3 = ("V", p.cnt["V"])
    if "thrdump" in taps:
        out_evs.append(p.dma("gpsimd", taps["thrdump"], THR[:, :],
                             waits=(ev_thr,)))
        p.barrier()
    if "statdump" in taps:
        out_evs.append(p.dma("gpsimd", taps["statdump"], STAT[:, :],
                             waits=(ev_t3,)))
        p.barrier()

    with ExitStack() as pctx:
        PTR3v = pctx.enter_context(nc.psum_tensor("PTR3", [128, 512], BF16))
        PW3v = [pctx.enter_context(nc.psum_tensor(f"PW3{i}", [128, 512], F32))
              for i in range(4)]
        ev_add = {}
        ev_tq_prev = None
        qbufs3 = [Gb, Bb]
        ev_gb3v = [None, None]
        for q in range(4):
            ev_tq = []
            tload_user = [ev_tq_prev, ev_tq_prev]
            for i in range(NT):
                tbuf = Hs if i % 2 == 0 else X2
                evd = p.dma("gpsimd", tbuf[:, 0:D],
                            d_t.ap()[i * 128:(i + 1) * 128,
                                     q * 2048:(q + 1) * 2048],
                            waits=(tload_user[i % 2],),
                            sem="dt" if i % 2 == 0 else "dt2")
                evi = p.emit("vector", lambda h, i=i, tbuf=tbuf: h.tensor_scalar_mul(
                    out=I32S[:, :], in0=tbuf[:, 0:D],
                    scalar1=st(i, SL_RCPT)), inc="V",
                    waits=(evd, ev_t3, ev_tq[-1] if ev_tq else ev_tq_prev))
                tload_user[i % 2] = evi
                evb = p.emit("vector", lambda h: h.tensor_copy(
                    out=WS[:, 0:2048], in_=I32S[:, :]), inc="V", waits=(evi,))
                for g in range(4):
                    evtr = None
                    for j in range(4):
                        kt = g * 4 + j
                        evtr = p.emit("tensor", lambda h, kt=kt, j=j: h.transpose(
                            PTR3v[:, j * 128:(j + 1) * 128],
                            WS[:, kt * 128:(kt + 1) * 128], ident[:, :]),
                            inc="T", waits=(evb,
                                            ev_tq[-1] if ev_tq else ev_tq_prev))
                    dst = Cb[:].rearrange("q (kt t) -> q kt t", kt=KT)[
                        :, g * 4:(g + 1) * 4, i * 128:(i + 1) * 128]
                    eve = p.emit("scalar", lambda h, dst=dst: h.copy(
                        out=dst, in_=PTR3v[:].rearrange("q (a b) -> q a b", a=4)),
                        inc="A", waits=(evtr,))
                    ev_tq.append(eve)
            ev_tqT = ev_tq[-1]
            ev_tq_prev = ev_tqT
            for sl in range(4):
                g3 = (q * 4 + sl) % 2
                qb = qbufs3[g3]
                ev_q = []
                raw_guard3 = getattr(p, "_rg3", {})
                for kt in range(KT):
                    slot = kt % 4
                    row0 = q * 2048 + kt * 128
                    evd = p.dma("sync", RAW[:, slot * 512:(slot + 1) * 512],
                                dwdq[row0:row0 + 128, sl * 512:(sl + 1) * 512],
                                waits=(raw_guard3.get(slot),), sem=f"q{slot}")
                    evu = p.emit("vector", lambda h, sl_=slot: h.tensor_scalar(
                        out=Ub[:, :],
                        in0=RAW[:, sl_ * 512:(sl_ + 1) * 512],
                        scalar1=THR[:, 2:3], scalar2=0.5,
                        op0=ALU.mult, op1=ALU.is_ge), inc="V", waits=(evd,))
                    evv = p.emit("vector", lambda h, sl_=slot: h.tensor_scalar(
                        out=Vb[:, :],
                        in0=RAW[:, sl_ * 512:(sl_ + 1) * 512],
                        scalar1=THR[:, 2:3], scalar2=-0.5,
                        op0=ALU.mult, op1=ALU.is_le), inc="V", waits=(evd,))
                    raw_guard3[slot] = evv
                    p._rg3 = raw_guard3
                    evw = p.emit("vector", lambda h, kt=kt, qb=qb: h.tensor_sub(
                        out=qb[:, kt * 512:(kt + 1) * 512],
                        in0=Ub[:, :], in1=Vb[:, :]), inc="V",
                        waits=(evu, evv,
                               ev_gb3v[g3] if kt == 0 else None))
                    ev_q.append((evw, evv))
                for i in range(NT):
                    bank = i % 4
                    evm = None
                    for kt in range(KT):
                        evm = p.emit("tensor", lambda h, bank=bank, kt=kt, i=i,
                                     qb=qb: h.matmul(
                            PW3v[bank][:, :], CbT(kt, i),
                            qb[:, kt * 512:(kt + 1) * 512],
                            start=(kt == 0), stop=(kt == KT - 1)),
                            inc="T" if kt == KT - 1 else None,
                            waits=(ev_q[kt][0], ev_tqT,
                                   ev_add.get(bank) if kt == 0 else None))
                    dst = A8[:, i * D + sl * 512: i * D + (sl + 1) * 512]
                    if q == 0:
                        eva = p.emit("vector", lambda h, dst=dst, bank=bank:
                                     h.tensor_copy(out=dst, in_=PW3v[bank][:, :]),
                                     inc="V", waits=(evm,))
                    else:
                        eva = p.emit("vector", lambda h, dst=dst, bank=bank:
                                     h.tensor_add(out=dst, in0=dst,
                                                  in1=PW3v[bank][:, :]),
                                     inc="V", waits=(evm,))
                    ev_add[bank] = eva
                    if i == NT - 1:
                        ev_gb3v[g3] = evm
    ev_y = ("V", p.cnt["V"])
    for i in range(NT):
        p.dma("gpsimd", d_y.ap()[i * 128:(i + 1) * 128, :],
              A8[:, i * D:(i + 1) * D], waits=(ev_y,))
    if "qffn" in taps:
        def mk_qf(i):
            return p.emit("vector", lambda h, i=i: h.tensor_scalar_mul(
                out=Hs[:, :], in0=A8[:, i * D:(i + 1) * D],
                scalar1=st(i, SL_TSWS)), inc="V", waits=(ev_y,))
        dump_tiles("qffn", mk_qf)
    p.barrier()

    ev_cb = p.dma("gpsimd", CB[:, :], dn1.broadcast_to([128, D]))
    ev_gw1 = p.dma("gpsimd",
                   OSC[:, 0:KT * LOW].rearrange("q (kt l) -> q kt l", kt=KT),
                   dgw1.rearrange("(kt q) l -> q kt l", q=128), sem="dw", gate=True)
    ev_gw2 = p.dma("gpsimd", Z1W[0:16, 1024:1024 + KEY], dgw2, sem="dw", gate=True)

    ev_hT = []
    ev_xr = {}
    with nc.psum_tensor("PTR4", [128, 512], BF16) as PTR4v:
        for i in range(NT):
            xa = X2[:, :]
            evx = p.dma("gpsimd", xa, dx[i * 128:(i + 1) * 128, :],
                        waits=(ev_xr.get(i - 1),))
            evt1 = p.emit("vector", lambda h, xa=xa, i=i: h.tensor_scalar(
                out=Hs[:, 0:D], in0=xa, scalar1=st(i, SL_RR2),
                scalar2=st(i, SL_MRR2), op0=ALU.mult, op1=ALU.subtract), inc="V",
                waits=(evx,))
            ev_xr[i] = evt1
            evt2 = p.emit("vector", lambda h: h.tensor_mul(
                out=XQB[:, :], in0=Hs[:, 0:D], in1=CB[:, :]), inc="V",
                waits=(evt1, ev_cb, ev_hT[-1] if ev_hT else None))
            for g in range(4):
                evtr = None
                for j in range(4):
                    kt = g * 4 + j
                    evtr = p.emit("tensor", lambda h, kt=kt, j=j: h.transpose(
                        PTR4v[:, j * 128:(j + 1) * 128],
                        XQB[:, kt * 128:(kt + 1) * 128], ident[:, :]),
                        inc="T", waits=(evt2, ev_hT[-1] if ev_hT else None))
                dst = Bb[:].rearrange("q (kt t) -> q kt t", kt=KT)[
                    :, g * 4:(g + 1) * 4, i * 128:(i + 1) * 128]
                eve = p.emit("scalar", lambda h, dst=dst: h.copy(
                    out=dst, in_=PTR4v[:].rearrange("q (a b) -> q a b", a=4)),
                    inc="A", waits=(evtr,))
                ev_hT.append(eve)
    ev_hTd = ev_hT[-1]
    p.barrier()

    ev_S0 = p.emit("vector", lambda h: h.memset(S[:, :], 0.0), inc="V")
    ev_Sbf0 = p.emit("vector", lambda h: h.memset(S_bf[:, :], 0.0), inc="V")
    ev_qe = {}
    ev_ke = {}
    ev_psh = {}
    with ExitStack() as pctx:
        PZ1 = pctx.enter_context(nc.psum_tensor("PZ1", [16, 1024], F32))
        PZ2 = pctx.enter_context(nc.psum_tensor("PZ2", [128, 1024], F32))
        PQ = pctx.enter_context(nc.psum_tensor("PQ", [128, 1024], F32))
        PK = pctx.enter_context(nc.psum_tensor("PK", [128, 1024], F32))
        ev_z1 = None
        for half in range(2):
            for kt in range(KT):
                ev_z1 = p.emit("tensor", lambda h, kt=kt, half=half: h.matmul(
                    PZ1[:, half * 512:(half + 1) * 512],
                    OSC[:, kt * LOW:(kt + 1) * LOW],
                    Bb[:, kt * TPC + half * 512: kt * TPC + (half + 1) * 512],
                    start=(kt == 0), stop=(kt == KT - 1)),
                    inc="T" if kt == KT - 1 else None,
                    waits=(ev_gw1, ev_hTd))
        ev_z1sb = p.emit("vector", lambda h: h.tensor_copy(
            out=Z1W[0:16, 0:1024], in_=PZ1[:, :]), inc="V", waits=(ev_z1,))

        ev_wl = {}
        for h_ in range(H):
            sl = h_ % 2
            wq_dst = WS[:, sl * 2048:(sl + 1) * 2048]
            wk_dst = WS[:, 4096 + sl * 2048: 4096 + (sl + 1) * 2048]
            ev_wl[("q", h_)] = p.dma(
                "gpsimd", wq_dst.rearrange("q (kt m) -> q kt m", kt=KT),
                dwq.rearrange("(kt q) m -> q kt m", q=128)[
                    :, :, h_ * 128:(h_ + 1) * 128],
                waits=(ev_qe.get(h_ - 2),), sem="dw")
            ev_wl[("k", h_)] = p.dma(
                "gpsimd", wk_dst.rearrange("q (kt m) -> q kt m", kt=KT),
                dwk.rearrange("(kt q) m -> q kt m", q=128)[
                    :, :, h_ * 128:(h_ + 1) * 128],
                waits=(ev_ke.get(h_ - 2),), sem="dw")

            ev_z2 = None
            for half in range(2):
                ev_z2 = p.emit("tensor", lambda h, h_=h_, half=half: h.matmul(
                    PZ2[:, half * 512:(half + 1) * 512],
                    Z1W[0:16, 1024 + h_ * 128:1024 + (h_ + 1) * 128],
                    Z1W[0:16, half * 512:(half + 1) * 512],
                    start=True, stop=True),
                    inc="T" if half == 1 else None,
                    waits=(ev_gw2, ev_z1sb,
                           ev_psh.get(h_ - 1)))
            e1 = p.emit("scalar", lambda h, h_=h_: h.activation(
                out=Hs[:, 0:1024], in_=PZ2[:, :], func=AF.Abs,
                scale=CK[:, 0:1], bias=GKB[:, h_:h_ + 1]), inc="A",
                waits=(ev_z2, ev_gkb, ev_ck))
            e2 = p.emit("scalar", lambda h: h.activation(
                out=Hs[:, 1024:2048], in_=Hs[:, 0:1024], func=AF.Exp,
                scale=-1.0), inc="A", waits=(e1,))
            e3 = p.emit("scalar", lambda h: h.activation(
                out=Hs[:, 0:1024], in_=Hs[:, 1024:2048], func=AF.Ln,
                scale=CK[:, 0:1], bias=CK[:, 0:1]), inc="A", waits=(e2,))
            e4 = p.emit("scalar", lambda h, h_=h_: h.activation(
                out=Hs[:, 1024:2048], in_=PZ2[:, :], func=AF.Relu,
                scale=CK[:, 1:2], bias=NGKB[:, h_:h_ + 1]), inc="A",
                waits=(e3, ev_ngkb, ev_ckm))
            e5 = p.emit("vector", lambda h: h.tensor_add(
                out=Hs[:, 0:1024], in0=Hs[:, 0:1024], in1=Hs[:, 1024:2048]),
                inc="V", waits=(e4,))
            evsp_dump = None
            if "sp" in taps:
                evsp_dump = p.dma("gpsimd", taps["sp"][h_ * 128:(h_ + 1) * 128, :],
                                  Hs[:, 0:1024], waits=(e5,))
                out_evs.append(evsp_dump)
            e6 = p.emit("vector", lambda h: h.tensor_tensor_scan(
                out=Hs[:, 1024:2048], data0=SEGM[:, :], data1=Hs[:, 0:1024],
                initial=0.0, op0=ALU.mult, op1=ALU.add), inc="V",
                waits=(e5, ev_seg))
            e7 = p.emit("scalar", lambda h: h.activation(
                out=Hs[:, 0:1024], in_=Hs[:, 1024:2048], func=AF.Exp,
                scale=-IGN), inc="A", waits=(e6, evsp_dump))
            e8 = p.emit("scalar", lambda h: h.activation(
                out=EK[:, :], in_=Hs[:, 1024:2048], func=AF.Exp,
                scale=IGN), inc="A", waits=(e6,))
            e9 = p.emit("vector", lambda h, h_=h_: h.tensor_copy(
                out=DEC[:, h_ * NCH:(h_ + 1) * NCH],
                in_=Hs[:, 0:1024].rearrange("q (c j) -> q c j", c=NCH)
                [:, :, 63:64].squeeze(2)), inc="V", waits=(e7,))
            e10 = p.emit("vector", lambda h, h_=h_: h.tensor_tensor_scan(
                out=PIN[:, h_ * NCH:(h_ + 1) * NCH],
                data0=DEC[:, h_ * NCH:(h_ + 1) * NCH], data1=ONES16[:, :],
                initial=1.0, op0=ALU.mult, op1=ALU.mult), inc="V",
                waits=(e9, ev_o16))
            p.emit("vector", lambda h, h_=h_: h.memset(
                PSH[:, h_ * NCH:h_ * NCH + 1], 1.0), inc="V")
            ev_psh[h_] = p.emit("vector", lambda h, h_=h_: h.tensor_copy(
                out=PSH[:, h_ * NCH + 1:(h_ + 1) * NCH],
                in_=PIN[:, h_ * NCH:(h_ + 1) * NCH - 1]), inc="V", waits=(e10,))
            ev_q = None
            ev_k = None
            for half in range(2):
                for kt in range(KT):
                    ev_q = p.emit("tensor", lambda h, sl=sl, kt=kt, half=half:
                                  h.matmul(
                        PQ[:, half * 512:(half + 1) * 512],
                        WS[:, sl * 2048 + kt * 128: sl * 2048 + (kt + 1) * 128],
                        Bb[:, kt * TPC + half * 512: kt * TPC + (half + 1) * 512],
                        start=(kt == 0), stop=(kt == KT - 1)),
                        inc="T" if kt == KT - 1 else None,
                        waits=(ev_wl[("q", h_)], ev_hTd,
                               ev_qe.get(h_ - 1) if kt == 0 else None))
            for half in range(2):
                for kt in range(KT):
                    ev_k = p.emit("tensor", lambda h, sl=sl, kt=kt, half=half:
                                  h.matmul(
                        PK[:, half * 512:(half + 1) * 512],
                        WS[:, 4096 + sl * 2048 + kt * 128:
                           4096 + sl * 2048 + (kt + 1) * 128],
                        Bb[:, kt * TPC + half * 512: kt * TPC + (half + 1) * 512],
                        start=(kt == 0), stop=(kt == KT - 1)),
                        inc="T" if kt == KT - 1 else None,
                        waits=(ev_wl[("k", h_)],
                               ev_ke.get(h_ - 1) if kt == 0 else None))
            ev_qe[h_] = p.emit("vector", lambda h, h_=h_: h.scalar_tensor_tensor(
                out=Eb[:, h_ * TPC:(h_ + 1) * TPC], in0=PQ[:, :], scalar=C1,
                in1=Hs[:, 0:1024], op0=ALU.mult, op1=ALU.mult), inc="V",
                waits=(ev_q, e7, e9))
            ev_ke[h_] = p.emit("vector", lambda h, h_=h_: h.tensor_mul(
                out=Fb[:, h_ * TPC:(h_ + 1) * TPC], in0=PK[:, :],
                in1=EK[:, :]), inc="V", waits=(ev_k, e8))
            for c in range(NCH):
                p.emit("vector", lambda h, h_=h_, c=c: h.tensor_scalar_mul(
                    out=Gb[:, h_ * TPC + c * 64: h_ * TPC + (c + 1) * 64],
                    in0=Eb[:, h_ * TPC + c * 64: h_ * TPC + (c + 1) * 64],
                    scalar1=PSH[:, h_ * NCH + c: h_ * NCH + c + 1]), inc="V",
                    waits=(ev_qe[h_], ev_psh[h_]))
    ev_qt = ("V", p.cnt["V"])
    if "qe" in taps:
        for h_ in range(H):
            eve = p.emit("vector", lambda h, h_=h_: h.tensor_copy(
                out=Hs[:, 0:1024], in_=Eb[:, h_ * TPC:(h_ + 1) * TPC]),
                inc="V", waits=(ev_qe[h_],))
            out_evs.append(p.dma("gpsimd", taps["qe"][h_ * 128:(h_ + 1) * 128, :],
                                 Hs[:, 0:1024], waits=(eve,)))
            p.barrier()
    if "ke" in taps:
        for h_ in range(H):
            eve = p.emit("vector", lambda h, h_=h_: h.tensor_copy(
                out=Hs[:, 0:1024], in_=Fb[:, h_ * TPC:(h_ + 1) * TPC]),
                inc="V", waits=(ev_ke[h_],))
            out_evs.append(p.dma("gpsimd", taps["ke"][h_ * 128:(h_ + 1) * 128, :],
                                 Hs[:, 0:1024], waits=(eve,)))
            p.barrier()
    p.barrier()

    with ExitStack() as pctx:
        PV = [pctx.enter_context(nc.psum_tensor(f"PV{i}", [128, 512], F32))
              for i in range(2)]
        ev_vfree = [None, None]
        ev_wv = None
        for sl in range(4):
            ev_wv_prev = ev_wv
            ev_wv = None
            for kt in range(KT):
                ev_wv = p.dma("gpsimd", WS[:, kt * 512:(kt + 1) * 512],
                              dwv[kt * 128:(kt + 1) * 128,
                                  sl * 512:(sl + 1) * 512],
                              waits=(ev_vfree[0], ev_vfree[1]) if kt == 0 else (),
                              sem="dw", gate=(kt == 0))
            for i in range(NT):
                b = i % 2
                evm = None
                for kt in range(KT):
                    evm = p.emit("tensor", lambda h, b=b, kt=kt, i=i: h.matmul(
                        PV[b][:, :], BbT(kt, i),
                        WS[:, kt * 512:(kt + 1) * 512],
                        start=(kt == 0), stop=(kt == KT - 1)),
                        inc="T" if kt == KT - 1 else None,
                        waits=(ev_wv, ev_vfree[b] if kt == 0 else None))
                eve = p.emit("scalar", lambda h, b=b, i=i, sl=sl: h.copy(
                    out=Cb[:, i * D + sl * 512: i * D + (sl + 1) * 512],
                    in_=PV[b][:, :]), inc="A", waits=(evm,))
                ev_vfree[b] = eve
    ev_v = ("A", p.cnt["A"])
    p.barrier()

    with ExitStack() as pctx:
        pO = pctx.enter_context(nc.psum_tensor("pO", [64, 2048], F32))
        pA = pctx.enter_context(nc.psum_tensor("pA", [64, 64], F32))
        pT = pctx.enter_context(nc.psum_tensor("pT", [64, 128], BF16))
        pS = pctx.enter_context(nc.psum_tensor("pS", [128, 256], F32))
        ev_S_upd = {h_: ev_S0 for h_ in range(H)}
        ev_Sbf = {h_: ev_Sbf0 for h_ in range(H)}
        ev_oe = {}
        ev_Am = None
        ev_kd = None
        ev_pS_user = None
        ev_o2_prev = None
        ev_s1_prev = None
        for c in range(NCH):
            r = (c % 2) * 64
            i = c // 2
            for h_ in range(H):
                qs = Eb[:, h_ * TPC + c * 64: h_ * TPC + (c + 1) * 64]
                ks = Fb[:, h_ * TPC + c * 64: h_ * TPC + (c + 1) * 64]
                vsl = Cb[r:r + 64, i * D + h_ * HV: i * D + (h_ + 1) * HV]
                evA = p.emit("tensor", lambda h, qs=qs, ks=ks: h.matmul(
                    pA[:, :], ks, qs, start=True, stop=True), inc="T",
                    waits=(ev_qe[h_], ev_ke[h_], ev_Am))
                ev_Am = p.emit("vector", lambda h, r=r: h.tensor_mul(
                    out=SAM[r:r + 64, :], in0=pA[:, :], in1=maskt[:, :]),
                    inc="V", waits=(evA, ev_o2_prev))
                evk1 = p.emit("vector", lambda h, ks=ks, h_=h_, c=c: h.tensor_scalar_mul(
                    out=SKDT[:, :], in0=ks,
                    scalar1=DEC[:, h_ * NCH + c: h_ * NCH + c + 1]), inc="V",
                    waits=(ev_ke[h_], ev_kd))
                evk2 = p.emit("tensor", lambda h: h.transpose(
                    pT[:, :], SKDT[:, :], ident[:, :]), inc="T",
                    waits=(evk1, ev_kd))
                evk3 = p.emit("scalar", lambda h, r=r: h.copy(
                    out=SKD[r:r + 64, :], in_=pT[:, :]), inc="A",
                    waits=(evk2, ev_s1_prev))
                ev_kd = evk3
                evo1 = p.emit("tensor", lambda h, qs=qs, h_=h_: h.matmul(
                    pO[:, h_ * HV:(h_ + 1) * HV], qs,
                    S_bf[:, h_ * HV:(h_ + 1) * HV], start=True, stop=False),
                    inc="T",
                    waits=(ev_Sbf[h_], ev_oe.get((c - 1, h_)),
                           ev_oe.get((c - 1, h_ ^ 1)),
                           ev_oe.get((c, h_ ^ 1))))
                evo2 = p.emit("tensor", lambda h, r=r, h_=h_, vsl=vsl: h.matmul(
                    pO[:, h_ * HV:(h_ + 1) * HV], SAM[r:r + 64, :], vsl,
                    start=False, stop=True), inc="T", waits=(ev_Am, ev_v))
                ev_o2_prev = evo2
                evoe = p.emit("scalar", lambda h, r=r, i=i, h_=h_: h.copy(
                    out=Db[r:r + 64, i * D + h_ * HV: i * D + (h_ + 1) * HV],
                    in_=pO[:, h_ * HV:(h_ + 1) * HV]), inc="A", waits=(evo2,))
                ev_oe[(c, h_)] = evoe
                evs1 = p.emit("tensor", lambda h, r=r, vsl=vsl: h.matmul(
                    pS[:, :], SKD[r:r + 64, :], vsl, start=True, stop=True),
                    inc="T", waits=(evk3, ev_pS_user))
                ev_s1_prev = evs1
                ev_S_upd[h_] = p.emit("vector", lambda h, h_=h_, c=c:
                                      h.scalar_tensor_tensor(
                    out=S[:, h_ * HV:(h_ + 1) * HV],
                    in0=S[:, h_ * HV:(h_ + 1) * HV],
                    scalar=DEC[:, h_ * NCH + c: h_ * NCH + c + 1],
                    in1=pS[:, :], op0=ALU.mult, op1=ALU.add), inc="V",
                    waits=(evs1, evo1))
                ev_pS_user = ev_S_upd[h_]
                ev_Sbf[h_] = p.emit("vector", lambda h, h_=h_: h.tensor_copy(
                    out=S_bf[:, h_ * HV:(h_ + 1) * HV],
                    in_=S[:, h_ * HV:(h_ + 1) * HV]), inc="V",
                    waits=(ev_S_upd[h_],))
    ev_scan = ("V", p.cnt["V"])
    if "o_loc" in taps:
        def mk_ol(i):
            return p.emit("vector", lambda h, i=i: h.tensor_copy(
                out=Hs[:, :], in_=Db[:, i * D:(i + 1) * D]), inc="V",
                waits=(ev_scan,))
        dump_tiles("o_loc", mk_ol)
    if "S" in taps:
        eve = p.emit("vector", lambda h: h.tensor_copy(
            out=Hs[:, :], in_=S[:, :]), inc="V", waits=(ev_scan,))
        out_evs.append(p.dma("gpsimd", taps["S"], Hs[:, :], waits=(eve,)))
        p.barrier()
    p.barrier()

    ev_pe = p.emit("vector", lambda h: h.tensor_copy(
        out=PE8[:, :],
        in_=PIN[:].rearrange("q (h c) -> q h c", h=H)[:, :, NCH - 1:NCH]
        .squeeze(2)), inc="V")
    evd1 = p.dma("gpsimd", ag_in.ap()[:, 0:2048], S[:, :], waits=(ev_scan,))
    evd2 = p.dma("gpsimd", ag_in.ap()[:, 2048:2056], PE8[:, :], waits=(ev_pe,))
    ev_ag = p.emit("gpsimd", lambda h: h.collective_compute(
        "AllGather", ALU.bypass, replica_groups=[list(range(NCORE))],
        ins=[ag_in.ap().opt()], outs=[ag_out.ap().opt()]),
        inc="cc", waits=(evd1, evd2))
    ev_Sz = p.emit("vector", lambda h: h.memset(S[:, :], 0.0), inc="V",
                   waits=(evd1,))
    ev_pref = ev_Sz
    sj_user = [None]
    for j in range(NCORE):
        evl1 = p.dma("gpsimd", Hs[:, 0:2048],
                     ag_out.ap()[j * 128:(j + 1) * 128, 0:2048],
                     waits=(ev_ag, sj_user[0]))
        evl2 = p.dma("gpsimd", PJ8[:, :],
                     ag_out.ap()[j * 128:(j + 1) * 128, 2048:2056],
                     waits=(ev_ag, ev_pref))
        evm = p.emit("vector", lambda h, j=j: h.tensor_scalar(
            out=PJM[:, :], in0=PJ8[:, :], scalar1=GSEL[:, j:j + 1],
            scalar2=GSEL[:, 8 + j:9 + j], op0=ALU.mult, op1=ALU.add), inc="V",
            waits=(evl2, ev_gsel))
        evsm = p.emit("vector", lambda h, j=j: h.tensor_scalar_mul(
            out=Hs[:, 0:2048], in0=Hs[:, 0:2048],
            scalar1=GSEL[:, j:j + 1]), inc="V", waits=(evl1,))
        for h_ in range(H):
            ev_pref = p.emit("vector", lambda h, h_=h_: h.scalar_tensor_tensor(
                out=S[:, h_ * HV:(h_ + 1) * HV], in0=S[:, h_ * HV:(h_ + 1) * HV],
                scalar=PJM[:, h_:h_ + 1],
                in1=Hs[:, h_ * HV:(h_ + 1) * HV],
                op0=ALU.mult, op1=ALU.add), inc="V", waits=(evm, evsm, ev_pref))
        sj_user[0] = ev_pref
    ev_sin = ev_pref
    ev_sin = p.emit("vector", lambda h: h.tensor_copy(
        out=S_bf[:, :], in_=S[:, :]), inc="V", waits=(ev_pref,))
    p.barrier()

    with ExitStack() as pctx:
        PC = [pctx.enter_context(nc.psum_tensor(f"PC{i}", [128, 256], F32))
              for i in range(2)]
        ev_cfree = [None, None]
        for h_ in range(H):
            for i in range(NT):
                b = (h_ * NT + i) % 2
                evc = p.emit("tensor", lambda h, h_=h_, i=i, b=b: h.matmul(
                    PC[b][:, :], Gb[:, h_ * TPC + i * 128: h_ * TPC + (i + 1) * 128],
                    S_bf[:, h_ * HV:(h_ + 1) * HV], start=True, stop=True),
                    inc="T", waits=(ev_sin, ev_qt, ev_cfree[b]))
                eva = p.emit("vector", lambda h, h_=h_, i=i, b=b: h.tensor_add(
                    out=Db[:, i * D + h_ * HV: i * D + (h_ + 1) * HV],
                    in0=Db[:, i * D + h_ * HV: i * D + (h_ + 1) * HV],
                    in1=PC[b][:, :]), inc="V", waits=(evc, ev_scan))
                ev_cfree[b] = eva
    ev_corr = ("V", p.cnt["V"])
    if "ocor" in taps:
        def mk_oc(i):
            return p.emit("vector", lambda h, i=i: h.tensor_copy(
                out=Hs[:, :], in_=Db[:, i * D:(i + 1) * D]), inc="V",
                waits=(ev_corr,))
        dump_tiles("ocor", mk_oc)
    for i in range(NT):
        evsq = p.emit("vector", lambda h, i=i: h.tensor_mul(
            out=X2[:, 0:D], in0=Db[:, i * D:(i + 1) * D],
            in1=Db[:, i * D:(i + 1) * D]), inc="V", waits=(ev_corr,))
        ev_red = None
        for h_ in range(H):
            ev_red = p.emit("vector", lambda h, i=i, h_=h_: h.reduce_sum(
                out=RSQ[:, i * 8 + h_: i * 8 + h_ + 1],
                in_=X2[:, h_ * HV:(h_ + 1) * HV], axis=AX), inc="V",
                waits=(evsq,))
        if "sqdump" in taps and i == 7:
            out_evs.append(p.dma("gpsimd", taps["sqdump"], X2[:, :],
                                 waits=(("V", p.cnt["V"]),)))
            p.barrier()
        e1 = p.emit("vector", lambda h, i=i: h.tensor_scalar(
            out=RSQ[:, i * 8:(i + 1) * 8], in0=RSQ[:, i * 8:(i + 1) * 8],
            scalar1=1.0 / HV, scalar2=1e-6, op0=ALU.mult, op1=ALU.add),
            inc="V", waits=(ev_red,))
        e2 = p.emit("vector", lambda h, i=i: h.reciprocal(
            out=RSQ[:, i * 8:(i + 1) * 8], in_=RSQ[:, i * 8:(i + 1) * 8]),
            inc="V", waits=(e1,))
    ev_e2all = ("V", p.cnt["V"])
    p.emit("scalar", lambda h: h.activation(
        out=RSQ[:, :], in_=RSQ[:, :], func=AF.Sqrt), inc="A",
        waits=(ev_e2all,))
    ev_rsq = ("A", p.cnt["A"])
    ev_rsqV = ("V", p.cnt["V"])
    ev_cb2 = p.dma("gpsimd", CB[:, :], dgnw.broadcast_to([128, D]),
                   waits=(ev_xr.get(NT - 1),))
    if "cbdump" in taps:
        out_evs.append(p.dma("gpsimd", taps["cbdump"], CB[:, :],
                             waits=(ev_cb2,)))
        p.barrier()
    if "rsqdump" in taps:
        out_evs.append(p.dma("gpsimd", taps["rsqdump"], RSQ[:, :],
                             waits=(ev_rsq,)))
        p.barrier()
    with ExitStack() as pctx:
        PG = [pctx.enter_context(nc.psum_tensor(f"PG{i}", [128, 512], F32))
              for i in range(2)]
        PTR7v = pctx.enter_context(nc.psum_tensor("PTR7", [128, 512], BF16))
        ev_gfree = [None, None]
        ev_tr_ev = None
        ev_em2_prev = None
        ev_tr4_prev = None
        wbufs = [WS, Fb]
        wsems = ["dw", "dw2"]
        ev_buf_user = [None, None]
        for sl in range(4):
            wb = sl % 2
            evl = None
            for kt in range(KT):
                evl = p.dma("gpsimd", wbufs[wb][:, kt * 512:(kt + 1) * 512],
                            dwg[kt * 128:(kt + 1) * 128,
                                sl * 512:(sl + 1) * 512],
                            waits=(ev_buf_user[wb],) if kt == 0 else (),
                            sem=wsems[wb], gate=(kt == 0))
            for i in range(NT):
                b = i % 2
                evm = None
                for kt in range(KT):
                    evm = p.emit("tensor", lambda h, b=b, kt=kt, i=i, wb=wb: h.matmul(
                        PG[b][:, :], BbT(kt, i),
                        wbufs[wb][:, kt * 512:(kt + 1) * 512],
                        start=(kt == 0), stop=(kt == KT - 1)),
                        inc="T" if kt == KT - 1 else None,
                        waits=(evl, ev_corr,
                               ev_gfree[b] if kt == 0 else None))
                ev_buf_user[wb] = evm
                evsil = p.emit("scalar", lambda h, b=b: h.activation(
                    out=Gb[:, 0:512], in_=PG[b][:, :], func=AF.Silu), inc="A",
                    waits=(evm, ev_rsq, ev_em2_prev, ev_corr))
                ev_gfree[b] = evsil
                h0, h1 = sl * 2, sl * 2 + 1
                en1 = p.emit("vector", lambda h, i=i, h0=h0: h.tensor_scalar_mul(
                    out=Gb[:, 512:768],
                    in0=Db[:, i * D + h0 * HV: i * D + (h0 + 1) * HV],
                    scalar1=RSQ[:, i * 8 + h0: i * 8 + h0 + 1]), inc="V",
                    waits=(ev_rsq, ev_corr, ev_em2_prev))
                en2 = p.emit("vector", lambda h, i=i, h1=h1: h.tensor_scalar_mul(
                    out=Gb[:, 768:1024],
                    in0=Db[:, i * D + h1 * HV: i * D + (h1 + 1) * HV],
                    scalar1=RSQ[:, i * 8 + h1: i * 8 + h1 + 1]), inc="V",
                    waits=(ev_rsq,))
                em1 = p.emit("vector", lambda h, sl=sl: h.tensor_mul(
                    out=Gb[:, 512:1024], in0=Gb[:, 512:1024],
                    in1=CB[:, sl * 512:(sl + 1) * 512]), inc="V",
                    waits=(en1, en2, ev_cb2))
                em2 = p.emit("vector", lambda h: h.tensor_mul(
                    out=Ub[:, :], in0=Gb[:, 512:1024],
                    in1=Gb[:, 0:512]), inc="V",
                    waits=(em1, evsil, ev_tr4_prev))
                ev_em2_prev = em2
                evtr = None
                for j in range(4):
                    evtr = p.emit("tensor", lambda h, j=j: h.transpose(
                        PTR7v[:, j * 128:(j + 1) * 128],
                        Ub[:, j * 128:(j + 1) * 128], ident[:, :]),
                        inc="T", waits=(em2, ev_tr_ev))
                ev_tr4_prev = evtr
                dst = Cb[:].rearrange("q (kt t) -> q kt t", kt=KT)[
                    :, sl * 4:(sl + 1) * 4, i * 128:(i + 1) * 128]
                ev_tr_ev = p.emit("scalar", lambda h, dst=dst: h.copy(
                    out=dst, in_=PTR7v[:].rearrange("q (a b) -> q a b", a=4)),
                    inc="A", waits=(evtr,))
    ev_ofT = ("A", p.cnt["A"])
    if "cbof" in taps:
        out_evs.append(p.dma("gpsimd", taps["cbof"], Cb[:, :],
                             waits=(ev_ofT,)))
        p.barrier()
    if "ofin" in taps:
        def mk_of(i):
            return p.emit("vector", lambda h, i=i: h.tensor_copy(
                out=Hs[:, :], in_=Db[:, i * D:(i + 1) * D]), inc="V",
                waits=(ev_ofT,))
        dump_tiles("ofin", mk_of)
    p.barrier()

    ev_acc = {}
    for i in range(NT):
        xa = X2[:, :]
        evx = p.dma("gpsimd", xa, dx[i * 128:(i + 1) * 128, :],
                    waits=(ev_acc.get(i - 1), ev_rsqV))
        ev_acc[i] = p.emit("vector", lambda h, xa=xa, i=i: h.tensor_scalar(
            out=A8[:, i * D:(i + 1) * D], in0=xa, scalar1=st(i, SL_R),
            scalar2=st(i, SL_MR), op0=ALU.mult, op1=ALU.subtract), inc="V",
            waits=(evx,))
    with ExitStack() as pctx:
        PO2 = [pctx.enter_context(nc.psum_tensor(f"PO2{i}", [128, 512], F32))
               for i in range(2)]
        ev_ofree = [None, None]
        wbufs = [WS, Fb]
        wsems = ["dw", "dw2"]
        ev_buf_user2 = [None, None]
        for sl in range(4):
            wb = sl % 2
            evl = None
            for kt in range(KT):
                evl = p.dma("gpsimd", wbufs[wb][:, kt * 512:(kt + 1) * 512],
                            dwo[kt * 128:(kt + 1) * 128,
                                sl * 512:(sl + 1) * 512],
                            waits=(ev_buf_user2[wb],) if kt == 0 else (),
                            sem=wsems[wb], gate=(kt == 0))
            for i in range(NT):
                b = i % 2
                evm = None
                for kt in range(KT):
                    evm = p.emit("tensor", lambda h, b=b, kt=kt, i=i, wb=wb: h.matmul(
                        PO2[b][:, :], CbT(kt, i),
                        wbufs[wb][:, kt * 512:(kt + 1) * 512],
                        start=(kt == 0), stop=(kt == KT - 1)),
                        inc="T" if kt == KT - 1 else None,
                        waits=(evl, ev_ofT,
                               ev_ofree[b] if kt == 0 else None))
                ev_buf_user2[wb] = evm
                eva = p.emit("vector", lambda h, b=b, i=i, sl=sl: h.tensor_add(
                    out=A8[:, i * D + sl * 512: i * D + (sl + 1) * 512],
                    in0=A8[:, i * D + sl * 512: i * D + (sl + 1) * 512],
                    in1=PO2[b][:, :]), inc="V", waits=(evm, ev_acc[i]))
                ev_ofree[b] = eva
    ev_x1 = ("V", p.cnt["V"])
    if "x1" in taps:
        def mk_x1(i):
            return p.emit("vector", lambda h, i=i: h.tensor_copy(
                out=Hs[:, :], in_=A8[:, i * D:(i + 1) * D]), inc="V",
                waits=(ev_x1,))
        dump_tiles("x1", mk_x1)
    p.barrier()

    ev_cb3 = p.dma("gpsimd", CB[:, :], dffw.broadcast_to([128, D]))
    for i in range(NT):
        evsq = p.emit("scalar", lambda h, i=i: h.activation(
            out=Hs[:, 0:D], in_=A8[:, i * D:(i + 1) * D], func=AF.Square,
            accum_out=st(i, 12)), inc="A", waits=(ev_x1,))
        e1 = p.emit("vector", lambda h, i=i: h.tensor_scalar(
            out=st(i, 13), in0=st(i, 12), scalar1=1.0 / D, scalar2=1e-6,
            op0=ALU.mult, op1=ALU.add), inc="V", waits=(evsq,))
        e2 = p.emit("vector", lambda h, i=i: h.reciprocal(
            out=st(i, 14), in_=st(i, 13)), inc="V", waits=(e1,))
        p.emit("scalar", lambda h, i=i: h.activation(
            out=st(i, SL_R2B), in_=st(i, 14), func=AF.Sqrt), inc="A",
            waits=(e2,))
    ev_r2b = ("A", p.cnt["A"])
    ev_h2 = []
    with nc.psum_tensor("PTR8", [128, 512], BF16) as PTR8v:
        for i in range(NT):
            evh = p.emit("vector", lambda h, i=i: h.scalar_tensor_tensor(
                out=Gb[:, 1024:3072], in0=A8[:, i * D:(i + 1) * D],
                scalar=st(i, SL_R2B), in1=CB[:, :], op0=ALU.mult,
                op1=ALU.mult), inc="V", waits=(ev_r2b, ev_cb3, ev_x1,
                                               ev_h2[-1] if ev_h2 else None))
            for g in range(4):
                evtr = None
                for j in range(4):
                    kt = g * 4 + j
                    evtr = p.emit("tensor", lambda h, kt=kt, j=j: h.transpose(
                        PTR8v[:, j * 128:(j + 1) * 128],
                        Gb[:, 1024 + kt * 128:1024 + (kt + 1) * 128],
                        ident[:, :]),
                        inc="T", waits=(evh, ev_h2[-1] if ev_h2 else None))
                dst = Cb[:].rearrange("q (kt t) -> q kt t", kt=KT)[
                    :, g * 4:(g + 1) * 4, i * 128:(i + 1) * 128]
                eve = p.emit("scalar", lambda h, dst=dst: h.copy(
                    out=dst, in_=PTR8v[:].rearrange("q (a b) -> q a b", a=4)),
                    inc="A", waits=(evtr,))
                ev_h2.append(eve)
    ev_h2T = ev_h2[-1]
    p.barrier()

    chunks = [(0, 16), (16, 32), (32, MA)]
    with ExitStack() as pctx:
        PG2 = pctx.enter_context(nc.psum_tensor("PG8", [128, 1024], F32))
        PU2 = pctx.enter_context(nc.psum_tensor("PU8", [128, 1024], F32))
        PD = [pctx.enter_context(nc.psum_tensor(f"PD8{i}", [128, 512], F32))
              for i in range(2)]
        ev_ffb = None
        ev_dfree = [None, None]
        wbufs = [WS, Fb]
        wsems = ["dw", "dw2"]
        ev_buf_user3 = [None, None]
        grot = 0
        for (m0, m1) in chunks:
            nm = m1 - m0
            ev_ff = []
            for mg in range((nm + 1) // 2):
                gw = min(2, nm - mg * 2)
                c0 = (m0 + mg * 2) * 128
                wb = grot % 2
                grot += 1
                wbuf = wbufs[wb]
                ev_wg2 = None
                ev_wu2 = None
                for kt in range(KT):
                    ev_wg2 = p.dma("gpsimd", wbuf[:, kt * 256: kt * 256 + gw * 128],
                                   dwga[kt * 128:(kt + 1) * 128, c0:c0 + gw * 128],
                                   waits=(ev_buf_user3[wb],) if kt == 0 else (),
                                   sem=wsems[wb], gate=(kt == 0))
                    ev_wu2 = p.dma("gpsimd",
                                   wbuf[:, 4096 + kt * 256: 4096 + kt * 256 + gw * 128],
                                   dwua[kt * 128:(kt + 1) * 128, c0:c0 + gw * 128],
                                   sem=wsems[wb], gate=False)
                for ml in range(gw):
                    m = m0 + mg * 2 + ml
                    evg = None
                    evu = None
                    for half in range(2):
                        for kt in range(KT):
                            evg = p.emit("tensor", lambda h, kt=kt, ml=ml, half=half,
                                         wbuf=wbuf: h.matmul(
                                PG2[:, half * 512:(half + 1) * 512],
                                wbuf[:, kt * 256 + ml * 128: kt * 256 + (ml + 1) * 128],
                                Cb[:, kt * TPC + half * 512:
                                   kt * TPC + (half + 1) * 512],
                                start=(kt == 0), stop=(kt == KT - 1)),
                                inc="T" if kt == KT - 1 else None,
                                waits=(ev_wg2, ev_wu2, ev_h2T,
                                       ev_ff[-1] if (ev_ff and kt == 0) else None))
                    for half in range(2):
                        for kt in range(KT):
                            evu = p.emit("tensor", lambda h, kt=kt, ml=ml, half=half,
                                         wbuf=wbuf: h.matmul(
                                PU2[:, half * 512:(half + 1) * 512],
                                wbuf[:, 4096 + kt * 256 + ml * 128:
                                   4096 + kt * 256 + (ml + 1) * 128],
                                Cb[:, kt * TPC + half * 512:
                                   kt * TPC + (half + 1) * 512],
                                start=(kt == 0), stop=(kt == KT - 1)),
                                inc="T" if kt == KT - 1 else None,
                                waits=(ev_ff[-1] if (ev_ff and kt == 0) else None,))
                    ev_buf_user3[wb] = evu
                    evs1 = p.emit("scalar", lambda h: h.activation(
                        out=Gb[:, 0:512], in_=PG2[:, 0:512], func=AF.Silu),
                        inc="A", waits=(evg,))
                    evs2 = p.emit("scalar", lambda h: h.activation(
                        out=Gb[:, 512:1024], in_=PG2[:, 512:1024],
                        func=AF.Silu), inc="A", waits=(evg,))
                    mloc = m - m0
                    ef1 = p.emit("vector", lambda h, mloc=mloc: h.tensor_mul(
                        out=Bb[:, mloc * TPC: mloc * TPC + 512],
                        in0=Gb[:, 0:512], in1=PU2[:, 0:512]), inc="V",
                        waits=(evs1, evu))
                    ef2 = p.emit("vector", lambda h, mloc=mloc: h.tensor_mul(
                        out=Bb[:, mloc * TPC + 512: mloc * TPC + 1024],
                        in0=Gb[:, 512:1024], in1=PU2[:, 512:1024]), inc="V",
                        waits=(evs2, evu))
                    ev_ff.append(ef2)
                    ev_ff.append(ef1)
            ev_ffd = ev_ff[-1]
            ev_ffv = ("V", p.cnt["V"])
            wda_sub = (nm + 15) // 16
            for sl in range(4):
                wb = grot % 2
                grot += 1
                wbuf = wbufs[wb]
                ev_wd2 = None
                for jm in range(nm):
                    ev_wd2 = p.dma("gpsimd", wbuf[:, jm * 512:(jm + 1) * 512],
                                   dwda[(m0 + jm) * 128:(m0 + jm + 1) * 128,
                                        sl * 512:(sl + 1) * 512],
                                   waits=(ev_ffv, ev_buf_user3[wb])
                                   if jm == 0 else (),
                                   sem=wsems[wb], gate=(jm == 0))
                for i in range(NT):
                    b = i % 2
                    evm = None
                    for jm in range(nm):
                        evm = p.emit("tensor", lambda h, b=b, jm=jm, i=i, nm=nm,
                                     wbuf=wbuf: h.matmul(
                            PD[b][:, :],
                            Bb[:, jm * TPC + i * 128: jm * TPC + (i + 1) * 128],
                            wbuf[:, jm * 512:(jm + 1) * 512],
                            start=(jm == 0), stop=(jm == nm - 1)),
                            inc="T" if jm == nm - 1 else None,
                            waits=(ev_wd2, ev_ffv,
                                   ev_dfree[b] if jm == 0 else None))
                    eva = p.emit("vector", lambda h, b=b, i=i, sl=sl: h.tensor_add(
                        out=A8[:, i * D + sl * 512: i * D + (sl + 1) * 512],
                        in0=A8[:, i * D + sl * 512: i * D + (sl + 1) * 512],
                        in1=PD[b][:, :]), inc="V", waits=(evm,))
                    ev_dfree[b] = eva
                    ev_buf_user3[wb] = evm
            ev_ffb = ("V", p.cnt["V"])
    p.barrier()

    ev_fin = {}
    ost = [None, None]
    for i in range(NT):
        obuf = OUTS if i % 2 == 0 else Hs
        evy = p.dma("gpsimd", X2[:, :], d_y.ap()[i * 128:(i + 1) * 128, :],
                    waits=(ev_fin.get(i - 1),))
        e1 = p.emit("vector", lambda h, i=i, obuf=obuf: h.scalar_tensor_tensor(
            out=obuf[:, 0:D], in0=X2[:, :], scalar=st(i, SL_TSWS),
            in1=A8[:, i * D:(i + 1) * D], op0=ALU.mult, op1=ALU.add), inc="V",
            waits=(evy, ost[i % 2]))
        evx = p.dma("gpsimd", X2[:, :], dx[i * 128:(i + 1) * 128, :],
                    waits=(e1,))
        e2 = p.emit("vector", lambda h, obuf=obuf: h.tensor_add(
            out=obuf[:, 0:D], in0=obuf[:, 0:D], in1=X2[:, :]), inc="V",
            waits=(evx,))
        ev_fin[i] = e2
        ev_o = p.dma("gpsimd", dout[i * 128:(i + 1) * 128, :],
                     obuf[:, 0:D], waits=(e2,),
                     sem="dt" if i % 2 == 0 else "dt2")
        ost[i % 2] = ev_o
        out_evs.append(ev_o)

    p.barrier()
    with nc.Block() as block:
        p.finalize(block)
    ctx.close()
    return nc


_NC_CACHE = {}


def _get_nc(debug_taps=()):
    key = tuple(sorted(debug_taps))
    if key not in _NC_CACHE:
        _NC_CACHE[key] = build_nc(debug_taps=key)
    return _NC_CACHE[key]


def stage_inputs(inputs, debug_taps=()):
    g = lambda k: np.ascontiguousarray(np.asarray(inputs[k], dtype=np.float32))
    x = g("L0_attn_x").reshape(T, D)
    wgT = np.ascontiguousarray(g("L0_ffn_w_gate").T)
    wuT = np.ascontiguousarray(g("L0_ffn_w_up").T)
    wdT = np.ascontiguousarray(g("L0_ffn_w_down").T)
    shared = dict(
        wq=g("L0_attn_wq"), wk=g("L0_attn_wk"), wv=g("L0_attn_wv"),
        gk_w1=g("L0_attn_gk_w1"), gk_w2=g("L0_attn_gk_w2"),
        gk_b=g("L0_attn_gk_b"), wg=g("L0_attn_wg"), wo=g("L0_attn_wo"),
        w_gate_a=g("L0_attn_w_gate"), w_up_a=g("L0_attn_w_up"),
        w_down_a=g("L0_attn_w_down"),
        wgT=wgT, wuT=wuT, wdT=wdT,
        n1w_row=g("L0_attn_norm1_w").reshape(1, D),
        ffnw_row=g("L0_attn_ffn_norm_w").reshape(1, D),
        gnw_row=np.ascontiguousarray(
            np.tile(g("L0_attn_gnorm_w"), H)).reshape(1, D),
        ident_in=np.eye(128, dtype=np.float32),
        maskT_in=np.triu(np.ones((64, 64), dtype=np.float32)),
        seg_row=np.tile(
            np.concatenate([[0.0], np.ones(63)]).astype(np.float32),
            NCH).reshape(1, TPC),
    )
    in_maps = []
    for c in range(NCORE):
        im = dict(shared)
        im["x"] = np.ascontiguousarray(x[c * TPC:(c + 1) * TPC])
        gs = np.zeros((1, 16), np.float32)
        gs[0, :c] = 1.0
        gs[0, 8:16] = 1.0 - gs[0, 0:8]
        im["gsel_row"] = gs
        sl = np.empty((12, 128, 4096), np.float32)
        sl[0:4] = wgT[c * 256:(c + 1) * 256].reshape(4, 128, 4096)
        sl[4:8] = wuT[c * 256:(c + 1) * 256].reshape(4, 128, 4096)
        sl[8:12] = wdT[c * 1024:(c + 1) * 1024].reshape(4, 128, 4096)
        im["w_scale_slice"] = sl
        in_maps.append(im)
    return in_maps


def kernel(**inputs):
    nc = _get_nc()
    in_maps = stage_inputs(inputs)
    res = run_bass_kernel_spmd(nc, in_maps, core_ids=list(range(NCORE)))
    out = np.concatenate([r["out"] for r in res.results], axis=0)
    return out.reshape(1, T, D).astype(np.float32)
